# version 19
# baseline (speedup 1.0000x reference)
import math
import contextlib
import numpy as np
import ml_dtypes
import concourse.bass as bass
import concourse.mybir as mybir
from concourse.bass_utils import run_bass_kernel_spmd

F32 = mybir.dt.float32
BF16 = mybir.dt.bfloat16
AX = mybir.AxisListType.X
ALU = mybir.AluOpType
ACT = mybir.ActivationFunctionType

NC = 8
BIGNEG = -1.0e9

FULL = dict(T=8192, D=2048, H=16, G=4, E=8, R=64, ag=True)


class Prog:

    def __init__(self):
        self.ops = []
        self.counts = {}

    def op(self, eng, fn, wait=(), inc=()):
        wait = [(s, int(v)) for (s, v) in wait if v > 0]
        inc = [(s, int(a)) for (s, a) in inc]
        self.ops.append((eng, fn, wait, inc))
        for s, a in inc:
            self.counts[s] = self.counts.get(s, 0) + a
        if len(inc) == 1:
            return self.counts[inc[0][0]]
        return None

    def val(self, sem):
        return self.counts.get(sem, 0)

    def emit(self, block, sems):
        for eng_name in ("sync", "tensor", "vector", "scalar", "gpsimd"):
            eng_ops = [o for o in self.ops if o[0] == eng_name]
            if not eng_ops:
                continue

            def make(ops_):
                def body(e):
                    for (_, fn, wait, inc) in ops_:
                        for s, v in wait:
                            e.wait_ge(sems[s], v)
                        instr = fn(e)
                        for s, a in inc:
                            instr = instr.then_inc(sems[s], a)
                return body
            getattr(block, eng_name)(make(eng_ops))


def apx(base_ap, offset_elems, dims):
    return bass.AP(base_ap.tensor, base_ap.offset + offset_elems,
                   [list(base_ap.ap[0])] + [list(d) for d in dims])


def dap(dram_ap, offset_elems, dims):
    return bass.AP(dram_ap.tensor, dram_ap.offset + offset_elems,
                   [list(d) for d in dims])


def rope_perm(dq=128):
    return np.concatenate([np.arange(0, dq, 2), np.arange(1, dq, 2)])


def make_cs(T, dq=128):
    inv = 1.0 / (10000.0 ** (np.arange(0, dq, 2, dtype=np.float64) / dq))
    f = np.arange(T, dtype=np.float64)[:, None] * inv[None, :]
    return np.concatenate([np.cos(f), np.sin(f)], axis=1).astype(np.float32)


def make_cb(core):
    W = 1008 + 512
    p = np.arange(128)[:, None]
    x = np.arange(W)[None, :]
    rel = x - 496
    ramp = -(128.0 * np.floor(rel / 16.0) + np.mod(rel, 16))
    cb = (ramp + p - 16.0 * core).astype(np.float32)
    cb[:, :496] = BIGNEG
    bnd = np.broadcast_to((x >= 496) & (x < 512), cb.shape)
    cb[np.asarray(bnd & (p > 16 * core + rel))] = BIGNEG
    return np.ascontiguousarray(cb)


def make_oh(E):
    oh = np.zeros((E, 4 * 128), dtype=np.float32)
    for pair in range(4):
        for m in range(128):
            e = 2 * pair + (1 if m >= 64 else 0)
            if e < E:
                oh[e, pair * 128 + m] = 1.0
    return oh


def build_nc(cfg):
    T, D, H, G, E, R = cfg["T"], cfg["D"], cfg["H"], cfg["G"], cfg["E"], cfg["R"]
    DQ = 128
    ND = D // 128
    NT = T // 128
    LQ = T // NC
    NOT = LQ // 128
    NQC = max(1, LQ // 512)
    NP = E // 2
    HD = H * DQ
    NH4 = HD // 512
    KVW = 2 * G * DQ
    GD = G * DQ
    NO4 = D // 512
    nbn = D // 512
    PPG = max(1, (H // G) // 2)
    NHP = H // 2
    slopes = [float(np.float32(2.0 ** (-(h + 1) * 8.0 / H))) for h in range(H)]
    c_tanh = float(np.float32(1.0 / (30.0 * math.sqrt(DQ))))
    inv_rank = float(np.float32(1.0 / R))

    nc = bass.Bass("TRN2", target_bir_lowering=False, debug=False)

    AG = bool(cfg.get("ag"))
    TB = T // NC if AG else T
    NTB = TB // 128
    SEC = 2 * G * DQ * TB
    VOFF = G * DQ * TB
    dxb = nc.dram_tensor("xb", [TB, D], F32, kind="ExternalInput").ap()
    dxq = nc.dram_tensor("xq", [LQ, D], F32, kind="ExternalInput").ap()
    dwq = nc.dram_tensor("wq", [D, HD], F32, kind="ExternalInput").ap()
    dwkv = nc.dram_tensor("wkv", [D, KVW], F32, kind="ExternalInput").ap()
    dwo = nc.dram_tensor("wo", [HD, D], F32, kind="ExternalInput").ap()
    dbo = nc.dram_tensor("bo1", [1, D], F32, kind="ExternalInput").ap()
    dat = nc.dram_tensor("at", [D, E * R], F32, kind="ExternalInput").ap()
    dbt = nc.dram_tensor("bt", [NP, 2 * R, D], F32, kind="ExternalInput").ap()
    dwrt = nc.dram_tensor("wrt", [D, E], F32, kind="ExternalInput").ap()
    dbr = nc.dram_tensor("br1", [1, E], F32, kind="ExternalInput").ap()
    dcb = nc.dram_tensor("cb", [128, 1520], F32, kind="ExternalInput").ap()
    dcsk = nc.dram_tensor("csk", [TB, DQ], F32, kind="ExternalInput").ap()
    dcsq = nc.dram_tensor("csq", [LQ, DQ], F32, kind="ExternalInput").ap()
    doh = nc.dram_tensor("oh", [E, 4 * 128], F32, kind="ExternalInput").ap()
    dib = nc.dram_tensor("identb", [128, 128], BF16, kind="ExternalInput").ap()
    from concourse.replica_groups import maybe_share_collective_output_space
    dkvo = nc.dram_tensor("kvown", [SEC], BF16, kind="Internal").ap()
    if AG:
        _sp = maybe_share_collective_output_space("AllGather",
                                                  [list(range(NC))])
        dkvag = nc.dram_tensor("kvag", [NC * SEC], BF16, kind="Internal",
                               addr_space=_sp).ap()
    else:
        dkvag = dkvo
    dout = nc.dram_tensor("out", [LQ, D], F32, kind="ExternalOutput").ap()
    if cfg.get("dbg"):
        dbg_qt = nc.dram_tensor("dbg_qt", [128, H * LQ], BF16, kind="ExternalOutput").ap()
        dbg_ct = nc.dram_tensor("dbg_ct", [128, H * LQ], BF16, kind="ExternalOutput").ap()
        dbg_x2 = nc.dram_tensor("dbg_x2", [128, NOT * D], BF16, kind="ExternalOutput").ap()
        dbg_ag = nc.dram_tensor("dbg_ag", [NC * SEC if AG else SEC], BF16, kind="ExternalOutput").ap()
        dbg_x1 = nc.dram_tensor("dbg_x1", [NTB * 128, D], BF16, kind="ExternalOutput").ap()
        dbg_y1 = nc.dram_tensor("dbg_y1", [128, NOT * D], BF16, kind="ExternalOutput").ap()

    est = contextlib.ExitStack()
    with est:
        sems = {}
        for s in ("ld1", "gld", "xk0", "xk1", "xk2", "cs0", "cs1", "cs2", "cs3", "cs4", "ksw", "vsw",
                  "qx0", "qx1", "wl0", "wl1", "kld0", "kld1", "vld0", "vld1",
                  "xo0", "xo1", "st", "v", "t", "s", "g", "agc"):
            sems[s] = est.enter_context(nc.semaphore(s))
        block = est.enter_context(nc.Block())

        sib = est.enter_context(nc.sbuf_tensor("sib", [128, 128], BF16))
        seps = est.enter_context(nc.sbuf_tensor("seps", [128, 1], F32))
        smv = [est.enter_context(nc.sbuf_tensor(f"smv{i}", [128, 2], F32)) for i in range(2)]
        srs = [est.enter_context(nc.sbuf_tensor(f"srs{i}", [128, 1], F32)) for i in range(2)]
        sst = [est.enter_context(nc.sbuf_tensor(f"sst{i}", [128, 24], F32)) for i in range(2)]
        x2 = est.enter_context(nc.sbuf_tensor("x2", [128, NOT * D], BF16))

        est_qb = contextlib.ExitStack()
        sQT = est_qb.enter_context(nc.sbuf_tensor("sQT", [128, H * LQ], BF16))
        sCB = est_qb.enter_context(nc.sbuf_tensor("sCB", [128, 1520], F32))

        P = Prog()
        P.op("sync", lambda e: e.dma_start(out=sCB[:], in_=dcb), inc=[("ld1", 16)])
        P.op("gpsimd", lambda e: e.dma_start(out=sib[:], in_=dib), inc=[("gld", 16)])
        v_eps = P.op("vector", lambda e: e.memset(seps[:], 1e-5), inc=[("v", 1)])

        ln_guard = [{"s": 0, "v": 0, "other": []} for _ in range(2)]
        ln_slot = [0]

        def emit_phase_barrier(extra=()):
            w = [("v", P.val("v")), ("s", P.val("s")), ("t", P.val("t")),
                 ("g", P.val("g")), ("st", P.val("st"))] + list(extra)
            for eng_ in ("sync", "tensor", "vector", "scalar", "gpsimd"):
                P.op(eng_, lambda e: None, wait=w)

        def emit_ln_stats(src_fn, first_waits=()):
            k = ln_slot[0]
            ln_slot[0] ^= 1
            g = ln_guard[k]
            for i in range(nbn):
                w0 = list(first_waits) + [("v", g["v"])] if i == 0 else []
                P.op("vector",
                     lambda e, i=i, k=k: e.bn_stats(sst[k][:, i * 6:(i + 1) * 6], src_fn(i)),
                     wait=w0, inc=[("v", 1)])
            w_ = [("v", P.val("v")), ("s", g["s"])] + list(g["other"])
            v_bn = P.op("vector", lambda e, k=k: e.bn_aggr(smv[k][:], sst[k][:, :nbn * 6]),
                        wait=w_, inc=[("v", 1)])
            g["v"] = v_bn
            w_ = [("v", v_bn), ("v", v_eps)] + list(g["other"])
            s_l = P.op("scalar", lambda e, k=k: e.activation(
                srs[k][:], smv[k][:, 1:2], ACT.Ln, bias=seps[:]), wait=w_, inc=[("s", 1)])
            s_r = P.op("scalar", lambda e, k=k: e.activation(
                srs[k][:], srs[k][:], ACT.Exp, scale=-0.5), wait=[("s", s_l)], inc=[("s", 1)])
            g["s"] = s_r
            return s_r, k

        def ln_done(k, waits):
            ln_guard[k]["other"] = list(waits)

        estA2 = contextlib.ExitStack()
        sWqb0 = estA2.enter_context(nc.sbuf_tensor("sWqb0", [128, ND * 512], BF16))
        with contextlib.ExitStack() as pa:
            sWkv = pa.enter_context(nc.sbuf_tensor("sWkv", [128, ND * KVW], BF16))
            xbuf = [pa.enter_context(nc.sbuf_tensor(f"xb{i}", [128, D], F32)) for i in range(3)]
            x1b = [pa.enter_context(nc.sbuf_tensor(f"x1b{i}", [128, D], BF16)) for i in range(2)]
            x1T = [pa.enter_context(nc.sbuf_tensor(f"x1T{i}", [128, ND * 128], BF16)) for i in range(2)]
            kvb = [pa.enter_context(nc.sbuf_tensor(f"kvb{i}", [128, KVW], BF16)) for i in range(2)]
            krp = [pa.enter_context(nc.sbuf_tensor(f"krp{i}", [128, GD], BF16)) for i in range(2)]
            kts = [pa.enter_context(nc.sbuf_tensor(f"kts{i}", [128, GD], BF16)) for i in range(2)]
            csb = [pa.enter_context(nc.sbuf_tensor(f"csb{i}", [128, DQ], F32)) for i in range(5)]
            tmp12 = pa.enter_context(nc.sbuf_tensor("tmp12", [128, GD], F32))
            pT = [pa.enter_context(nc.psum_tensor(f"pT{i}", [128, ND * 128], BF16)) for i in range(2)]
            pK = [pa.enter_context(nc.psum_tensor(f"pK{i}", [128, GD], F32)) for i in range(1)]
            pV = [pa.enter_context(nc.psum_tensor(f"pV{i}", [128, GD], F32)) for i in range(1)]
            pKT = [pa.enter_context(nc.psum_tensor(f"pKT{i}", [128, GD], BF16)) for i in range(2)]

            for d in range(ND):
                P.op("gpsimd", lambda e, d=d: e.dma_start(
                    out=sWkv[:, d * KVW:(d + 1) * KVW],
                    in_=dwkv[d * 128:(d + 1) * 128, :]), inc=[("gld", 16)])
            gld_w = P.val("gld")
            P.op("gpsimd", lambda e: e.dma_start(
                out=sWqb0[:],
                in_=dap(dwq, 0, [[HD, 128], [128 * HD, ND], [1, 512]])),
                inc=[("wl0", 16)])

            h_apply, h_rope, h_evT, h_kv, h_kvev, h_ktr, h_evKT, h_tr = \
                {}, {}, {}, {}, {}, {}, {}, {}
            xk_tot_d, cs_tot_d = {}, {}

            def a1_load(t):
                xs = f"xk{t % 3}"
                w_ = [("v", h_apply[t - 3])] if t >= 3 else []
                P.op("sync", lambda e, t=t: e.dma_start(
                    out=xbuf[t % 3][:], in_=dxb[t * 128:(t + 1) * 128, :]),
                    wait=w_, inc=[(xs, 16)])
                xk_tot_d[t] = P.val(xs)
                cs_s = f"cs{t % 5}"
                w_ = [("g", h_rope[t - 5])] if t >= 5 else []
                P.op("sync", lambda e, t=t: e.dma_start(
                    out=csb[t % 5][:], in_=dcsk[t * 128:(t + 1) * 128, :]),
                    wait=w_, inc=[(cs_s, 16)])
                cs_tot_d[t] = P.val(cs_s)

            def a1_ln(t):
                s_r, lk = emit_ln_stats(
                    lambda i, t=t: xbuf[t % 3][:, i * 512:(i + 1) * 512],
                    first_waits=[(f"xk{t % 3}", xk_tot_d[t])])
                w_ = [("s", s_r)]
                if t >= 2:
                    w_.append(("t", h_tr[t - 2]))
                if cfg.get("dbg"):
                    w_.append(("st", P.val("st")))
                h_apply[t] = P.op("vector", lambda e, t=t, lk=lk: e.tensor_scalar(
                    x1b[t % 2][:], xbuf[t % 3][:], smv[lk][:, 0:1], srs[lk][:],
                    ALU.subtract, ALU.mult), wait=w_, inc=[("v", 1)])
                ln_done(lk, [("v", h_apply[t])])

            def a1_dbg(t):
                P.op("sync", lambda e, t=t: e.dma_start(
                    out=dbg_x1[t * 128:(t + 1) * 128, :], in_=x1b[t % 2][:]),
                    wait=[("v", h_apply[t])], inc=[("st", 16)])

            def a1_trans(t):
                for d in range(ND):
                    w_ = [("v", h_apply[t])] if d == 0 else []
                    if d == 0 and t >= 2:
                        w_.append(("v", h_evT[t - 2]))
                    if d == 0 and t == 0:
                        w_.append(("gld", gld_w))
                    P.op("tensor", lambda e, d=d, t=t: e.transpose(
                        pT[t % 2][:, d * 128:(d + 1) * 128],
                        x1b[t % 2][:, d * 128:(d + 1) * 128], sib[:]),
                        wait=w_, inc=[("t", 1)])
                h_tr[t] = P.val("t")

            def a1_evT(t):
                w_ = [("t", h_tr[t])]
                if t >= 2:
                    w_.append(("t", h_kv[t - 2]))
                h_evT[t] = P.op("vector", lambda e, t=t: e.tensor_copy(
                    x1T[t % 2][:], pT[t % 2][:]), wait=w_, inc=[("v", 1)])

            def a1_kv(t):
                for d in range(ND):
                    for half in range(2):
                        w_ = []
                        if d == 0 and half == 0:
                            w_.append(("v", h_evT[t]))
                            if t >= 1:
                                w_.append(("s", h_kvev[t - 1]))
                            if t == 0:
                                w_.append(("gld", gld_w))
                        P.op("tensor", lambda e, d=d, half=half, t=t: e.matmul(
                            (pK if half == 0 else pV)[0][:],
                            x1T[t % 2][:, d * 128:(d + 1) * 128],
                            sWkv[:, d * KVW + half * GD:d * KVW + (half + 1) * GD],
                            start=(d == 0), stop=(d == ND - 1)),
                            wait=w_, inc=[("t", 1)])
                h_kv[t] = P.val("t")

            def a1_copies(t):
                w_ = [("t", h_kv[t])]
                if t >= 2:
                    w_.append(("g", h_rope[t - 2]))
                    w_.append(("vsw", 16 * t))
                P.op("scalar", lambda e, t=t: e.copy(
                    kvb[t % 2][:, 0:GD], pK[0][:]), wait=w_, inc=[("s", 1)])
                h_kvev[t] = P.op("scalar", lambda e, t=t: e.copy(
                    kvb[t % 2][:, GD:2 * GD], pV[0][:]), inc=[("s", 1)])

            def a1_rope(t):
                kp = t % 2
                ke = apx(kvb[kp][:], 0, [[DQ, G], [1, 64]])
                ko = apx(kvb[kp][:], 64, [[DQ, G], [1, 64]])
                re_ = apx(krp[kp][:], 0, [[DQ, G], [1, 64]])
                ro_ = apx(krp[kp][:], 64, [[DQ, G], [1, 64]])
                cse = apx(csb[t % 5][:], 0, [[0, G], [1, 64]])
                cso = apx(csb[t % 5][:], 64, [[0, G], [1, 64]])
                t1 = apx(tmp12[:], 0, [[64, G], [1, 64]])
                t2 = apx(tmp12[:], GD // 2, [[64, G], [1, 64]])
                w_ = [("s", h_kvev[t]), (f"cs{t % 5}", cs_tot_d[t])]
                if t >= 2:
                    w_.append(("t", h_ktr[t - 2]))
                if t >= 1:
                    w_.append(("g", h_rope[t - 1]))
                P.op("gpsimd", lambda e, a=t1, b=ke, c=cso: e.tensor_mul(a, b, c),
                     wait=w_, inc=[("g", 1)])
                P.op("gpsimd", lambda e, a=t2, b=ko, c=cso: e.tensor_mul(a, b, c),
                     inc=[("g", 1)])
                P.op("gpsimd", lambda e, a=re_, b=ke, c=cse: e.tensor_mul(a, b, c),
                     inc=[("g", 1)])
                g0 = P.op("gpsimd", lambda e, a=ro_, b=ko, c=cse: e.tensor_mul(a, b, c),
                          inc=[("g", 1)])
                g1 = P.op("gpsimd", lambda e, a=re_, b=t2: e.tensor_sub(a, a, b),
                          wait=[("g", g0)], inc=[("g", 1)])
                h_rope[t] = P.op("gpsimd", lambda e, a=ro_, b=t1: e.tensor_add(a, a, b),
                                 wait=[("g", g1)], inc=[("g", 1)])

            def a1_vstore(t):
                P.op("sync", lambda e, t=t: e.dma_start(
                    out=dap(dkvo, VOFF + t * 128 * GD, [[GD, 128], [1, GD]]),
                    in_=kvb[t % 2][:, GD:2 * GD]),
                    wait=[("s", h_kvev[t])], inc=[("vsw", 16)])

            def a1_ktrans(t):
                for g_ in range(G):
                    w_ = [("g", h_rope[t])] if g_ == 0 else []
                    if g_ == 0 and t >= 2:
                        w_.append(("v", h_evKT[t - 2]))
                    P.op("tensor", lambda e, g_=g_, t=t: e.transpose(
                        pKT[t % 2][:, g_ * 128:(g_ + 1) * 128],
                        krp[t % 2][:, g_ * 128:(g_ + 1) * 128], sib[:]),
                        wait=w_, inc=[("t", 1)])
                h_ktr[t] = P.val("t")

            def a1_evKT(t):
                w_ = [("t", h_ktr[t])]
                if t >= 2:
                    w_.append(("ksw", 16 * t))
                h_evKT[t] = P.op("vector", lambda e, t=t: e.tensor_copy(
                    kts[t % 2][:], pKT[t % 2][:]), wait=w_, inc=[("v", 1)])

            def a1_kstore(t):
                P.op("sync", lambda e, t=t: e.dma_start(
                    out=dap(dkvo, t * 128, [[TB, 128], [DQ * TB, G], [1, 128]]),
                    in_=apx(kts[t % 2][:], 0, [[128, G], [1, 128]])),
                    wait=[("v", h_evKT[t])], inc=[("ksw", 16)])

            a1_load(0)
            a1_load(1)
            for i in range(NTB + 3):
                if i + 2 < NTB:
                    a1_load(i + 2)
                if i < NTB:
                    a1_ln(i)
                    if cfg.get("dbg"):
                        a1_dbg(i)
                if 0 <= i - 1 < NTB:
                    a1_trans(i - 1)
                    a1_evT(i - 1)
                if 0 <= i - 2 < NTB:
                    a1_kv(i - 2)
                    a1_copies(i - 2)
                    a1_rope(i - 2)
                    a1_vstore(i - 2)
                if 0 <= i - 3 < NTB:
                    a1_ktrans(i - 3)
                    a1_evKT(i - 3)
                    a1_kstore(i - 3)

            ksw_tot = P.val("ksw")
            vsw_tot = P.val("vsw")
            v_a1_done = P.val("v")
            s_a1_done = P.val("s")
            if AG:
                g_ag = P.op("gpsimd", lambda e: e.collective_compute(
                    "AllGather", ALU.bypass,
                    replica_groups=[list(range(NC))],
                    ins=[dkvo.opt()], outs=[dkvag.opt()]),
                    wait=[("ksw", ksw_tot), ("vsw", vsw_tot)], inc=[("agc", 1)])
                if cfg.get("dbg"):
                    P.op("sync", lambda e: e.dma_start(
                        out=dap(dbg_ag, 0, [[1024, SEC // 1024], [1, 1024]]),
                        in_=dap(dkvo, 0, [[1024, SEC // 1024], [1, 1024]])),
                        wait=[("ksw", ksw_tot), ("vsw", vsw_tot)], inc=[("st", 16)])
            else:
                g_ag = 0

        emit_phase_barrier(extra=[("ksw", ksw_tot), ("vsw", vsw_tot)])
        with contextlib.ExitStack() as pa2:
            xqb = [pa2.enter_context(nc.sbuf_tensor(f"xqb{i}", [128, D], F32)) for i in range(2)]
            xq1b = [pa2.enter_context(nc.sbuf_tensor(f"xq1b{i}", [128, D], BF16)) for i in range(2)]
            oxT = pa2.enter_context(nc.sbuf_tensor("oxT", [128, ND * LQ], BF16))
            sWqb = [sWqb0,
                    pa2.enter_context(nc.sbuf_tensor("sWqb1", [128, ND * 512], BF16))]
            scq = pa2.enter_context(nc.sbuf_tensor("scq", [128, NOT * DQ], F32))
            qpb = [pa2.enter_context(nc.sbuf_tensor(f"qpb{i}", [128, 512], BF16)) for i in range(2)]
            qrp = [pa2.enter_context(nc.sbuf_tensor(f"qrp{i}", [128, 512], BF16)) for i in range(2)]
            qt1 = pa2.enter_context(nc.sbuf_tensor("qt1", [128, 256], F32))
            qt2 = pa2.enter_context(nc.sbuf_tensor("qt2", [128, 256], F32))
            pT2 = pa2.enter_context(nc.psum_tensor("pT2", [128, ND * 128], BF16))
            pQ = [pa2.enter_context(nc.psum_tensor(f"pQ{i}", [128, 512], F32)) for i in range(2)]
            pQT = pa2.enter_context(nc.psum_tensor("pQT", [128, 1024], BF16))

            P.op("sync", lambda e: e.dma_start(
                out=apx(scq[:], 0, [[128, NOT], [1, DQ]]),
                in_=dap(dcsq, 0, [[DQ, 128], [128 * DQ, NOT], [1, DQ]])),
                 inc=[("ld1", 16)])
            ld1_tot = P.val("ld1")

            ha_ap, ha_evT = {}, {}

            def emit_prep(ot):
                sx = ot % 2
                qs_ = f"qx{sx}"
                w_ = [("v", ha_ap[ot - 2])] if ot >= 2 else []
                P.op("sync", lambda e, ot=ot, sx=sx: e.dma_start(
                    out=xqb[sx][:], in_=dxq[ot * 128:(ot + 1) * 128, :]),
                    wait=w_, inc=[(qs_, 16)])
                qx_tot = P.val(qs_)
                s_r, lk = emit_ln_stats(
                    lambda i, sx=sx: xqb[sx][:, i * 512:(i + 1) * 512],
                    first_waits=[(qs_, qx_tot)])
                ha_ap[ot] = P.op("vector", lambda e, sx=sx, lk=lk: e.tensor_scalar(
                    xq1b[sx][:], xqb[sx][:], smv[lk][:, 0:1], srs[lk][:],
                    ALU.subtract, ALU.mult), wait=[("s", s_r)], inc=[("v", 1)])
                ln_done(lk, [("v", ha_ap[ot])])
                for d in range(ND):
                    w_ = [("v", ha_ap[ot])]
                    if d == 0 and ot > 0:
                        w_.append(("v", ha_evT[ot - 1]))
                    if d == 0 and ot == 0:
                        w_ += [("v", v_a1_done), ("s", s_a1_done)]
                    P.op("tensor", lambda e, d=d, sx=sx: e.transpose(
                        pT2[:, d * 128:(d + 1) * 128],
                        xq1b[sx][:, d * 128:(d + 1) * 128], sib[:]),
                        wait=w_, inc=[("t", 1)])
                ha_evT[ot] = P.op("vector", lambda e, ot=ot: e.tensor_copy(
                    apx(oxT[:], ot * 128, [[LQ, ND], [1, 128]]), pT2[:]),
                    wait=[("t", P.val("t"))], inc=[("v", 1)])

            emit_prep(0)
            if NOT > 1:
                emit_prep(1)

            h_wl, h_qmm, h_qrope, h_qtr, h_qev = {}, {}, {}, {}, {}

            def mk_qtail(idx, nc4, ot, qx):
                def emit():
                    for j in range(4):
                        w_ = [("v", h_qrope[idx])]
                        if j == 0 and idx >= 2:
                            w_.append(("v", h_qev[idx - 2]))
                        P.op("tensor", lambda e, j=j, qx=qx: e.transpose(
                            pQT[:, qx * 512 + j * 128:qx * 512 + (j + 1) * 128],
                            qrp[qx][:, j * 128:(j + 1) * 128], sib[:]),
                            wait=w_, inc=[("t", 1)])
                    h_qtr[idx] = P.val("t")
                    h_qev[idx] = P.op("vector", lambda e, nc4=nc4, ot=ot, qx=qx:
                                      e.tensor_copy(
                                          apx(sQT[:], (nc4 * 4) * LQ + ot * 128,
                                              [[LQ, 4], [1, 128]]),
                                          pQT[:, qx * 512:(qx + 1) * 512]),
                                      wait=[("t", h_qtr[idx])], inc=[("v", 1)])
                return emit

            pending_qtail = None
            idx = 0
            for nc4 in range(NH4):
                wp = nc4 % 2
                ws = f"wl{wp}"
                if nc4 > 0:
                    w_ = [("t", h_qmm[(nc4 - 2) * NOT + NOT - 1])] if nc4 >= 2 else []
                    P.op("gpsimd", lambda e, nc4=nc4, wp=wp: e.dma_start(
                        out=sWqb[wp][:],
                        in_=dap(dwq, nc4 * 512, [[HD, 128], [128 * HD, ND], [1, 512]])),
                        wait=w_, inc=[(ws, 16)])
                wl_tot = P.val(ws)
                for ot in range(NOT):
                    qp = idx % 2
                    for d in range(ND):
                        w_ = []
                        if d == 0:
                            w_ = [("v", ha_evT[ot]), (ws, wl_tot)]
                            if idx == 0:
                                w_.append(("s", s_a1_done))
                            if idx >= 2:
                                w_.append(("v", h_qev[idx - 2]))
                        P.op("tensor", lambda e, d=d, ot=ot, wp=wp, qp=qp: e.matmul(
                            pQ[qp][:],
                            oxT[:, d * LQ + ot * 128:d * LQ + (ot + 1) * 128],
                            sWqb[wp][:, d * 512:(d + 1) * 512],
                            start=(d == 0), stop=(d == ND - 1)),
                            wait=w_, inc=[("t", 1)])
                    h_qmm[idx] = P.val("t")
                    if nc4 == 0 and ot + 2 < NOT:
                        emit_prep(ot + 2)
                    if pending_qtail is not None:
                        pending_qtail()
                    qx = idx % 2
                    w_ = [("t", h_qmm[idx])]
                    if idx >= 2:
                        w_.append(("v", h_qrope[idx - 2]))
                    v_ev = P.op("vector", lambda e, qp=qp, qx=qx: e.tensor_copy(
                        qpb[qx][:], pQ[qp][:]), wait=w_, inc=[("v", 1)])
                    qe = apx(qpb[qx][:], 0, [[DQ, 4], [1, 64]])
                    qo = apx(qpb[qx][:], 64, [[DQ, 4], [1, 64]])
                    rqe = apx(qrp[qx][:], 0, [[DQ, 4], [1, 64]])
                    rqo = apx(qrp[qx][:], 64, [[DQ, 4], [1, 64]])
                    qce = apx(scq[:], ot * DQ, [[0, 4], [1, 64]])
                    qco = apx(scq[:], ot * DQ + 64, [[0, 4], [1, 64]])
                    qtt1 = apx(qt1[:], 0, [[64, 4], [1, 64]])
                    qtt2 = apx(qt2[:], 0, [[64, 4], [1, 64]])
                    w_ = [("v", v_ev), ("ld1", ld1_tot)]
                    if idx >= 2:
                        w_.append(("t", h_qtr[idx - 2]))
                    if idx >= 1:
                        w_.append(("v", h_qrope[idx - 1]))
                    P.op("vector", lambda e, a=qtt1, b=qe, c=qco: e.tensor_mul(a, b, c),
                         wait=w_, inc=[("v", 1)])
                    P.op("vector", lambda e, a=qtt2, b=qo, c=qco: e.tensor_mul(a, b, c),
                         inc=[("v", 1)])
                    P.op("vector", lambda e, a=rqe, b=qe, c=qce: e.tensor_mul(a, b, c),
                         inc=[("v", 1)])
                    gg = P.op("vector", lambda e, a=rqo, b=qo, c=qce: e.tensor_mul(a, b, c),
                              inc=[("v", 1)])
                    gg = P.op("vector", lambda e, a=rqe, b=qtt2: e.tensor_sub(a, a, b),
                              wait=[("v", gg)], inc=[("v", 1)])
                    h_qrope[idx] = P.op("vector", lambda e, a=rqo, b=qtt1: e.tensor_add(a, a, b),
                                        wait=[("v", gg)], inc=[("v", 1)])
                    pending_qtail = mk_qtail(idx, nc4, ot, qx)
                    idx += 1
            pending_qtail()
            v_qt_done = P.val("v")
            s_a2_done = P.val("s")
            if cfg.get("dbg"):
                P.op("sync", lambda e: e.dma_start(out=dbg_qt, in_=sQT[:]),
                     wait=[("v", v_qt_done)], inc=[("st", 16)])

        ld1_cb = P.val("ld1")
        estA2.close()
        estL2 = contextlib.ExitStack()
        est_qbL = contextlib.ExitStack()
        with contextlib.ExitStack() as pb:
            sWoB = estL2.enter_context(nc.sbuf_tensor("sWoB", [128, H * D], BF16))
            sCT = estL2.enter_context(nc.sbuf_tensor("sCT", [128, H * LQ], BF16))
            kcb = [est_qbL.enter_context(nc.sbuf_tensor(f"kcb{i}", [128, 8 * 128], BF16)) for i in range(2)]
            vcb = [est_qbL.enter_context(nc.sbuf_tensor(f"vcb{i}", [128, 8 * 129], BF16)) for i in range(2)]
            pbf = [est_qbL.enter_context(nc.sbuf_tensor(f"pbf{i}", [128, 1024], BF16)) for i in range(3)]
            tb = [est_qbL.enter_context(nc.sbuf_tensor(f"tb{i}", [128, 1024], BF16)) for i in range(2)]
            srec = est_qbL.enter_context(nc.sbuf_tensor("srec", [128, 1], F32))
            sctn = est_qbL.enter_context(nc.sbuf_tensor("sctn", [128, 128], BF16))
            psS = [pb.enter_context(nc.psum_tensor(f"psS{i}", [128, 1024], F32)) for i in range(2)]
            psC = pb.enter_context(nc.psum_tensor("psC", [128, 1536], F32))
            pTC = pb.enter_context(nc.psum_tensor("pTC", [128, 1024], BF16))

            for nc4 in range(NO4):
                P.op("gpsimd", lambda e, nc4=nc4: e.dma_start(
                    out=apx(sWoB[:], nc4 * 512, [[D, H], [1, 512]]),
                    in_=dap(dwo, nc4 * 512, [[D, 128], [128 * D, H], [1, 512]])),
                    wait=([("v", v_qt_done), ("s", s_a2_done)] if nc4 == 0 else []),
                    inc=[("gld", 16)])

            s_tb = P.op("scalar", lambda e: e.nop(),
                        wait=[("s", s_a2_done)], inc=[("s", 1)])
            P.op("vector", lambda e: e.memset(
                apx(vcb[0][:], 128, [[129, 8], [1, 1]]), 1.0), inc=[("v", 1)])
            v_ones = P.op("vector", lambda e: e.memset(
                apx(vcb[1][:], 128, [[129, 8], [1, 1]]), 1.0), inc=[("v", 1)])

            fin_cell = {"v": v_ones}
            rounds = [(qc, hp) for qc in range(NQC) for hp in range(NHP)]
            rpc = 1024 // TB if TB < 1024 else 1
            tpr = min(8, TB // 128)

            def round_setup(qc, hp):
                nkt = 4 * NC * (qc + 1)
                acc_last = [min(nkt - 1, 32 * qc + 8 * (a % 4) + 7)
                            for a in range(8)]
                return {"qc": qc, "hp": hp, "g_": hp // PPG, "nkt": nkt,
                        "nch": nkt // 8, "acc_last": acc_last,
                        "bank_stop_acc": [max(range(3 * b, min(3 * b + 3, 8)),
                                              key=lambda a: acc_last[a])
                                          for b in range(3)],
                        "h_kl": {}, "h_vl": {}, "h_sc": {}, "h_ctx": {},
                        "h_tanh": {}, "h_exp": {}, "fa_d": {},
                        "pend_ctx": {}, "pending_exp": None, "pbf_war": {}}

            def plan_loads(RS, ch, wk, wv):
                cp = ch % 2
                g_ = RS["g_"]
                if AG:
                    w0 = [("agc", g_ag)]
                    koff = (ch * 1024 // TB) * SEC + ((ch * 1024) % TB)
                    kdims = [[TB, 128], [SEC, rpc], [1, TB]] if rpc > 1 \
                        else [[TB, 128], [1, 1024]]
                    voff = (ch * 1024 // TB) * SEC + VOFF + g_ * DQ
                    vdims = [[GD, 128], [SEC, rpc], [128 * GD, tpr], [1, 128]] if rpc > 1 \
                        else [[GD, 128], [128 * GD, 8], [1, 128]]
                else:
                    w0 = [("ksw", ksw_tot)]
                    koff = ch * 1024
                    kdims = [[TB, 128], [1, 1024]]
                    voff = (ch * 1024) * GD + g_ * DQ
                    vdims = [[GD, 128], [128 * GD, 8], [1, 128]]
                P.op("sync", lambda e, cp=cp, g_=g_, koff=koff, kdims=kdims:
                     e.dma_start(
                         out=kcb[cp][:],
                         in_=dap(dkvag, (g_ * DQ) * TB + koff, kdims)),
                     wait=w0 + wk, inc=[(f"kld{cp}", 16)])
                RS["h_kl"][ch] = P.val(f"kld{cp}")
                if AG and rpc > 1:
                    for r2 in range(rpc):
                        P.op("sync", lambda e, cp=cp, voff=voff, r2=r2:
                             e.dma_start(
                                 out=apx(vcb[cp][:], r2 * tpr * 129,
                                         [[129, tpr], [1, 128]]),
                                 in_=dap(dkvag, voff + r2 * SEC,
                                         [[GD, 128], [128 * GD, tpr], [1, 128]])),
                             wait=(w0 + [("v", v_ones)] + wv) if r2 == 0 else [],
                             inc=[(f"vld{cp}", 16)])
                else:
                    P.op("sync", lambda e, cp=cp, voff=voff, vdims=vdims:
                         e.dma_start(
                             out=apx(vcb[cp][:], 0, [[129, 8], [1, 128]]),
                             in_=dap(dkvag, voff, vdims)),
                         wait=w0 + [("v", v_ones)] + wv,
                         inc=[(f"vld{cp}", 16)])
                RS["h_vl"][ch] = P.val(f"vld{cp}")

            def make_ctx(RS, kt, h_exp):
                sp, cp, ch = kt % 3, (kt // 8) % 2, kt // 8
                fa = RS["fa_d"][kt]
                acc_last, bank_stop_acc = RS["acc_last"], RS["bank_stop_acc"]

                def emit():
                    first_mm = [kt == 0] * 3
                    nmm = 0
                    for j in range(2):
                        qoff = fa
                        while qoff < 512:
                            sw = min(128 - (qoff % 128), 512 - qoff)
                            a = j * 4 + (qoff // 128)
                            bank = a // 3
                            st_flag = bool(first_mm[bank])
                            first_mm[bank] = False
                            sp_flag = ((kt == acc_last[a])
                                       and (a == bank_stop_acc[bank]))
                            w_ = []
                            if nmm == 0:
                                w_ = [("s", h_exp), (f"vld{cp}", RS["h_vl"][ch])]
                                if kt == 0:
                                    w_.append(("v", fin_cell["v"]))
                            nmm += 1
                            P.op("tensor", lambda e, j=j, qoff=qoff, sw=sw, a=a,
                                 sp=sp, cp=cp, kt=kt, st_flag=st_flag,
                                 sp_flag=sp_flag: e.matmul(
                                     psC[(qoff % 128):(qoff % 128) + sw,
                                         (a // 3) * 512 + (a % 3) * 129:
                                         (a // 3) * 512 + (a % 3) * 129 + 129],
                                     pbf[sp][:, j * 512 + qoff:j * 512 + qoff + sw],
                                     vcb[cp][:, (kt % 8) * 129:(kt % 8) * 129 + 129],
                                     start=st_flag, stop=sp_flag),
                                 wait=w_, inc=[("t", 1)])
                            qoff += sw
                    RS["h_ctx"][kt] = P.val("t")
                return emit

            def mk_exp(RS, kt):
                fa = RS["fa_d"][kt]
                w = 512 - fa

                def emit():
                    w_ = [("s", RS["h_tanh"][kt])]
                    if kt >= 3:
                        w_.append(("t", RS["h_ctx"][kt - 3]))
                    elif RS["pbf_war"].get(kt % 3):
                        w_.append(("t", RS["pbf_war"][kt % 3]))
                    return P.op("scalar", lambda e: e.activation(
                        apx(pbf[kt % 3][:], fa, [[512, 2], [1, w]]),
                        apx(tb[kt % 2][:], fa, [[512, 2], [1, w]]),
                        ACT.Exp, scale=30.0),
                        wait=w_, inc=[("s", 1)])
                return emit

            def emit_scores(RS, kt, w_extra0):
                qc, hp = RS["qc"], RS["hp"]
                sp, cp, ch = kt % 2, (kt // 8) % 2, kt // 8
                gk = kt - 32 * qc
                fa = 0 if gk < 0 else min(384, (gk // 8) * 128)
                RS["fa_d"][kt] = fa
                w = 512 - fa
                h_scj = {}
                for j in range(2):
                    h = 2 * hp + j
                    w_ = []
                    if j == 0:
                        w_ = [(f"kld{cp}", RS["h_kl"][ch])] + w_extra0
                    P.op("tensor", lambda e, j=j, h=h, kt=kt, sp=sp, cp=cp,
                         fa=fa, w=w, qc=qc: e.matmul(
                             psS[sp][:, j * 512 + fa:j * 512 + fa + w],
                             kcb[cp][:, (kt % 8) * 128:(kt % 8 + 1) * 128],
                             sQT[:, h * LQ + qc * 512 + fa:h * LQ + qc * 512 + fa + w],
                             start=True, stop=True),
                         wait=w_, inc=[("t", 1)])
                    h_scj[j] = P.val("t")
                RS["h_sc"][kt] = P.val("t")
                return h_scj

            def emit_stt(RS, kt, h_scj, extra0):
                qc, hp = RS["qc"], RS["hp"]
                sp = kt % 2
                fa = RS["fa_d"][kt]
                w = 512 - fa
                x0 = 496 - 16 * (kt - 32 * qc)
                for j in range(2):
                    h = 2 * hp + j
                    w_ = [("t", h_scj[j])] + (extra0 if j == 0 else [])
                    P.op("vector", lambda e, j=j, h=h, sp=sp, fa=fa, w=w, x0=x0:
                         e.scalar_tensor_tensor(
                             psS[sp][:, j * 512 + fa:j * 512 + fa + w],
                             sCB[:, x0 + fa:x0 + fa + w], slopes[h],
                             psS[sp][:, j * 512 + fa:j * 512 + fa + w],
                             ALU.mult, ALU.add),
                         wait=w_, inc=[("v", 1)])
                return P.val("v")

            def emit_tanh(RS, kt, h_stt, extra):
                sp = kt % 2
                fa = RS["fa_d"][kt]
                w = 512 - fa
                RS["h_tanh"][kt] = P.op("scalar", lambda e, sp=sp, fa=fa, w=w:
                                        e.activation(
                                            apx(tb[sp][:], fa, [[512, 2], [1, w]]),
                                            apx(psS[sp][:], fa, [[512, 2], [1, w]]),
                                            ACT.Tanh, scale=c_tanh),
                                        wait=[("v", h_stt)] + extra, inc=[("s", 1)])

            def make_fin_pieces(h_last, hp, qc):
                st = {"t_prev": 0}

                def acc_piece(a):
                    def emit():
                        col = (a // 3) * 512 + (a % 3) * 129
                        w_ = ([("t", h_last)] if a == 0
                              else [("v", P.val("v"))])
                        v_rc = P.op("vector", lambda e, col=col:
                                    e.reciprocal(
                                        srec[:],
                                        psC[:, col + 128:col + 129]),
                                    wait=w_, inc=[("v", 1)])
                        w_ = [("v", v_rc)]
                        if st["t_prev"]:
                            w_.append(("t", st["t_prev"]))
                        else:
                            w_.append(("v", fin_cell["v"]))
                        v_n = P.op("vector", lambda e, col=col:
                                   e.tensor_scalar(
                                       sctn[:], psC[:, col:col + 128],
                                       srec[:], None, ALU.mult),
                                   wait=w_, inc=[("v", 1)])
                        st["t_prev"] = P.op(
                            "tensor", lambda e, a=a: e.transpose(
                                pTC[:, a * 128:(a + 1) * 128],
                                sctn[:], sib[:]),
                            wait=[("v", v_n)], inc=[("t", 1)])
                    return emit

                def copy_piece():
                    fin_cell["v"] = P.op(
                        "vector", lambda e, hp=hp, qc=qc: e.tensor_copy(
                            apx(sCT[:], (2 * hp) * LQ + qc * 512,
                                [[LQ, 2], [128, 4], [1, 128]]),
                            pTC[:, :1024]),
                        wait=[("t", st["t_prev"])], inc=[("v", 1)])
                return [acc_piece(a) for a in range(8)] + [copy_piece]

            carry = None
            for ri, (qc, hp) in enumerate(rounds):
                if carry is None:
                    RS = round_setup(qc, hp)
                    t_round0 = P.val("t")
                    s_round0 = P.val("s")
                    for ch in range(min(2, RS["nch"])):
                        plan_loads(RS, ch, [("t", t_round0)], [("t", t_round0)])
                    start_kt = 0
                    fin_prev = None
                else:
                    RS, fin_prev = carry
                nkt, nch = RS["nkt"], RS["nch"]
                for kt in range(0 if carry is None else 2, nkt):
                    if carry is None and kt == 0:
                        sc0 = [("v", v_qt_done), ("s", s_round0)]
                    elif kt >= 2:
                        sc0 = [("s", RS["h_tanh"][kt - 2])]
                    else:
                        sc0 = []
                    h_scj = emit_scores(RS, kt, sc0)
                    if fin_prev is not None and kt in (2, 3):
                        for p_ in fin_prev[3 * (kt - 1):3 * (kt - 1) + 3]:
                            p_()
                    if kt - 3 in RS["pend_ctx"]:
                        RS["pend_ctx"].pop(kt - 3)()
                    if kt % 8 == 2 and kt > 2 and (kt - 2) // 8 + 1 < nch:
                        plan_loads(RS, (kt - 2) // 8 + 1,
                                   [("t", RS["h_sc"][kt - 1])],
                                   [("t", RS["h_ctx"][kt - 3])])
                    h_stt = emit_stt(RS, kt, h_scj,
                                     [("ld1", ld1_cb)] if (ri == 0 and kt == 0)
                                     else [])
                    extra = []
                    if ri == 0 and kt == 0:
                        extra.append(("s", s_tb))
                    if kt >= 2:
                        extra.append(("s", RS["h_exp"][kt - 2]))
                    emit_tanh(RS, kt, h_stt, extra)
                    if RS["pending_exp"] is not None:
                        RS["h_exp"][kt - 1] = RS["pending_exp"]()
                        RS["pend_ctx"][kt - 1] = make_ctx(RS, kt - 1,
                                                          RS["h_exp"][kt - 1])
                    RS["pending_exp"] = mk_exp(RS, kt)
                RS["h_exp"][nkt - 1] = RS["pending_exp"]()

                if ri + 1 < len(rounds):
                    RSn = round_setup(*rounds[ri + 1])
                    ch_e = nch - 1 if (nch - 1) % 2 == 0 else nch - 2
                    kt_e = 8 * ch_e + 7
                    plan_loads(RSn, 0, [("t", RS["h_sc"][kt_e])],
                               [("t", RS["h_ctx"][kt_e])])
                    for kt in (0, 1):
                        h_scj = emit_scores(
                            RSn, kt, [("s", RS["h_tanh"][nkt - 2 + kt])])
                        h_stt = emit_stt(RSn, kt, h_scj, [])
                        emit_tanh(RSn, kt, h_stt,
                                  [("s", RS["h_exp"][nkt - 2 + kt])])
                    RS["pend_ctx"][nkt - 1] = make_ctx(RS, nkt - 1,
                                                       RS["h_exp"][nkt - 1])
                    for k_ in sorted(RS["pend_ctx"]):
                        RS["pend_ctx"].pop(k_)()
                    ch_o = nch - 1 if (nch - 1) % 2 == 1 else nch - 2
                    kt_o = 8 * ch_o + 7
                    plan_loads(RSn, 1, [("t", RS["h_sc"][kt_o])],
                               [("t", RS["h_ctx"][kt_o])])
                    RSn["pbf_war"] = {
                        m: RS["h_ctx"][nkt - 1 - ((nkt - 1 - m) % 3)]
                        for m in range(3)}
                    RSn["h_exp"][0] = mk_exp(RSn, 0)()
                    RSn["pend_ctx"][0] = make_ctx(RSn, 0, RSn["h_exp"][0])
                    pieces = make_fin_pieces(RS["h_ctx"][nkt - 1], hp, qc)
                    for p_ in pieces[0:3]:
                        p_()
                    RSn["pending_exp"] = mk_exp(RSn, 1)
                    carry = (RSn, pieces)
                else:
                    RS["pend_ctx"][nkt - 1] = make_ctx(RS, nkt - 1,
                                                       RS["h_exp"][nkt - 1])
                    for k_ in sorted(RS["pend_ctx"]):
                        RS["pend_ctx"].pop(k_)()
                    for p_ in make_fin_pieces(RS["h_ctx"][nkt - 1], hp, qc):
                        p_()
                    carry = None
            v_b_done = P.val("v")
            t_b_done = P.val("t")
            s_b_done = P.val("s")
            if cfg.get("dbg"):
                P.op("sync", lambda e: e.dma_start(out=dbg_ct, in_=sCT[:]),
                     wait=[("v", v_b_done)], inc=[("st", 16)])

        est_qbL.close()

        emit_phase_barrier()
        with contextlib.ExitStack() as pc1:
            pc1a = pc1.enter_context(contextlib.ExitStack())
            xqb2 = [pc1a.enter_context(nc.sbuf_tensor(f"xqb2{i}", [128, D], F32)) for i in range(1)]
            sbo = pc1a.enter_context(nc.sbuf_tensor("sbo", [128, D], BF16))
            t3t = [pc1a.enter_context(nc.sbuf_tensor(f"t3t{i}", [128, D], F32)) for i in range(2)]
            psOf = [pc1a.enter_context(nc.psum_tensor(f"psOf{i}", [128, D], F32)) for i in range(2)]

            P.op("gpsimd", lambda e: e.dma_start(
                out=sbo[:], in_=dbo.partition_broadcast(128)), inc=[("gld", 16)])
            gld_c = P.val("gld")

            h_omm, h_t3, h_x2 = {}, {}, {}
            for ot in range(NOT):
                op_ = ot % 2
                sx = 0
                xs = "xo0"
                w_ = [("g", h_x2[ot - 1])] if ot >= 1 else []
                P.op("sync", lambda e, ot=ot, sx=sx: e.dma_start(
                    out=xqb2[sx][:], in_=dxq[ot * 128:(ot + 1) * 128, :]),
                    wait=w_, inc=[(xs, 16)])
                xo_tot = P.val(xs)
                for nc4 in range(NO4):
                    for hh in range(H):
                        w_ = []
                        if nc4 == 0 and hh == 0:
                            w_ = [("v", v_b_done), ("gld", gld_c)]
                            if ot >= 2:
                                w_.append(("v", h_t3[ot - 2]))
                        P.op("tensor", lambda e, hh=hh, ot=ot, nc4=nc4, op_=op_: e.matmul(
                            psOf[op_][:, nc4 * 512:(nc4 + 1) * 512],
                            sCT[:, hh * LQ + ot * 128:hh * LQ + (ot + 1) * 128],
                            sWoB[:, hh * D + nc4 * 512:hh * D + (nc4 + 1) * 512],
                            start=(hh == 0), stop=(hh == H - 1)),
                            wait=w_, inc=[("t", 1)])
                h_omm[ot] = P.val("t")
                v_bo = P.op("vector", lambda e, op_=op_: e.scalar_tensor_tensor(
                    psOf[op_][:], sbo[:], 1.0, psOf[op_][:], ALU.mult, ALU.add),
                    wait=[("t", h_omm[ot]), ("gld", gld_c)], inc=[("v", 1)])
                s_r, lk = emit_ln_stats(
                    lambda i, op_=op_: psOf[op_][:, i * 512:(i + 1) * 512],
                    first_waits=[("v", v_bo)])
                h_t3[ot] = P.op("vector", lambda e, op_=op_, lk=lk: e.tensor_scalar(
                    t3t[op_][:], psOf[op_][:], smv[lk][:, 0:1], srs[lk][:],
                    ALU.subtract, ALU.mult),
                    wait=[("s", s_r)] + ([("g", h_x2[ot - 2])] if ot >= 2 else []),
                    inc=[("v", 1)])
                ln_done(lk, [("v", h_t3[ot])])
                h_x2[ot] = P.op("gpsimd", lambda e, ot=ot, sx=sx, op_=op_: e.tensor_add(
                    x2[:, ot * D:(ot + 1) * D], t3t[op_][:], xqb2[sx][:]),
                    wait=[("v", h_t3[ot]), (xs, xo_tot)] +
                         ([("g", h_x2[ot - 1])] if ot >= 1 else []), inc=[("g", 1)])
            g_x2_done = P.val("g")
            if cfg.get("dbg"):
                P.op("sync", lambda e: e.dma_start(out=dbg_x2, in_=x2[:]),
                     wait=[("g", g_x2_done)], inc=[("st", 16)])

            pc1a.close()
            estL2.close()
            est_qb.close()

            emit_phase_barrier()
            with contextlib.ExitStack() as pc2:
                sAT = pc2.enter_context(nc.sbuf_tensor("sAT", [128, ND * 512], BF16))
                sBT = pc2.enter_context(nc.sbuf_tensor("sBT", [128, NP * D], BF16))
                swrt = pc2.enter_context(nc.sbuf_tensor("swrt", [128, ND * E], BF16))
                for dst, srcap in (
                        (sAT, dap(dat, 0, [[E * R, 128], [128 * E * R, ND], [1, 512]])),
                        (swrt, dap(dwrt, 0, [[E, 128], [128 * E, ND], [1, E]])),
                        (sBT, dap(dbt, 0, [[D, 128], [128 * D, NP], [1, D]]))):
                    P.op("gpsimd", lambda e, d=dst, s=srcap: e.dma_start(out=d[:], in_=s),
                         inc=[("gld", 16)])
                gld_c2 = P.val("gld")
                sbr = pc2.enter_context(nc.sbuf_tensor("sbr", [128, E], F32))
                sOH = pc2.enter_context(nc.sbuf_tensor("sOH", [8, 512], F32))
                x2ln = [pc2.enter_context(nc.sbuf_tensor(f"x2ln{i}", [128, D], BF16)) for i in range(2)]
                x2T = pc2.enter_context(nc.sbuf_tensor("x2T", [128, ND * 512], BF16))
                slg = [pc2.enter_context(nc.sbuf_tensor(f"slg{i}", [128, E], F32)) for i in range(2)]
                smx = [pc2.enter_context(nc.sbuf_tensor(f"smx{i}", [128, 1], F32)) for i in range(2)]
                snm = [pc2.enter_context(nc.sbuf_tensor(f"snm{i}", [128, 1], F32)) for i in range(2)]
                sex = [pc2.enter_context(nc.sbuf_tensor(f"sex{i}", [128, E], F32)) for i in range(2)]
                ssm = [pc2.enter_context(nc.sbuf_tensor(f"ssm{i}", [128, 1], F32)) for i in range(2)]
                src2 = [pc2.enter_context(nc.sbuf_tensor(f"src2{i}", [128, 1], F32)) for i in range(2)]
                smk = [pc2.enter_context(nc.sbuf_tensor(f"smk{i}", [128, E], F32)) for i in range(2)]
                sw_ = [pc2.enter_context(nc.sbuf_tensor(f"sw_{i}", [128, E], F32)) for i in range(4)]
                sif2 = pc2.enter_context(nc.sbuf_tensor("sif2", [128, 128], F32))
                swT = pc2.enter_context(nc.sbuf_tensor("swT", [8, 512], F32))
                wbb = pc2.enter_context(nc.sbuf_tensor("wbb", [128, 4 * 512], BF16))
                y2b = [pc2.enter_context(nc.sbuf_tensor(f"y2b{i}", [128, D], F32)) for i in range(2)]
                hpb = pc2.enter_context(nc.sbuf_tensor("hpb", [128, NP * 512], BF16))
                t4t = pc2.enter_context(nc.sbuf_tensor("t4t", [128, D], F32))
                outb = [pc2.enter_context(nc.sbuf_tensor(f"outb{i}", [128, D], F32)) for i in range(2)]

                P.op("sync", lambda e: e.dma_start(
                    out=sbr[:], in_=dbr.partition_broadcast(128)), inc=[("ld1", 16)])
                P.op("sync", lambda e: e.dma_start(out=sOH[:, :], in_=doh),
                     inc=[("ld1", 16)])
                ld1_tot = P.val("ld1")
                gi1 = P.op("gpsimd", lambda e: e.iota(
                    sif2[:], pattern=[[-1, 128]], base=0, channel_multiplier=1,
                    allow_small_or_imprecise_dtypes=True),
                    wait=[("g", g_x2_done)], inc=[("g", 1)])
                v_if = P.op("vector", lambda e: e.tensor_scalar(
                    sif2[:], sif2[:], 0.0, None, ALU.is_equal),
                    wait=[("g", gi1)], inc=[("v", 1)])

                h_oev2, st_prev = {}, {}
                h_t4, h_yev_d = {}, {}
                v_hp_qg_prev = 0
                yev_qg_prev = 0
                t_tr_prev, v_e3_prev = {}, {}
                for qg in range(max(1, LQ // 512)):
                    with contextlib.ExitStack() as pcq:
                        pca2 = pcq.enter_context(contextlib.ExitStack())
                        psR = [pca2.enter_context(nc.psum_tensor(
                            "psRq%d_%d" % (qg, i), [128, E], F32)) for i in range(2)]
                        pWT = pca2.enter_context(nc.psum_tensor("pWTq%d" % qg, [8, 128], F32))
                        pwb = pca2.enter_context(nc.psum_tensor("pwbq%d" % qg, [128, 512], F32))
                        pcaT = pcq.enter_context(contextlib.ExitStack())
                        pT3 = [pcaT.enter_context(nc.psum_tensor(
                            "pT3q%d_%d" % (qg, i), [128, ND * 128], BF16)) for i in range(2)]

                        v_e3_d, g_ap2_d = {}, {}
                        t_tr_d = {}
                        for tt in range(4):
                            ot = qg * 4 + tt
                            s_r, lk = emit_ln_stats(
                                lambda i, ot=ot: x2[:, ot * D + i * 512:ot * D + (i + 1) * 512],
                                first_waits=[("g", g_x2_done)])
                            w_ = [("s", s_r)]
                            if tt >= 2:
                                w_.append(("t", t_tr_d[tt - 2]))
                            elif t_tr_prev.get(tt + 2):
                                w_.append(("t", t_tr_prev[tt + 2]))
                            g_ap2 = P.op("vector", lambda e, ot=ot, lk=lk, tt=tt: e.tensor_scalar(
                                x2ln[tt % 2][:], x2[:, ot * D:(ot + 1) * D], smv[lk][:, 0:1],
                                srs[lk][:], ALU.subtract, ALU.mult),
                                wait=w_, inc=[("v", 1)])
                            ln_done(lk, [("v", g_ap2)])
                            g_ap2_d[tt] = g_ap2
                            for d in range(ND):
                                w_ = [("v", g_ap2)] if d == 0 else []
                                if d == 0 and tt >= 2:
                                    w_.append(("v", v_e3_d[tt - 2]))
                                if d == 0 and tt < 2:
                                    if v_e3_prev.get(tt + 2):
                                        w_.append(("v", v_e3_prev[tt + 2]))
                                    if v_hp_qg_prev:
                                        w_.append(("v", v_hp_qg_prev))
                                P.op("tensor", lambda e, d=d, tt=tt: e.transpose(
                                    pT3[tt % 2][:, d * 128:(d + 1) * 128],
                                    x2ln[tt % 2][:, d * 128:(d + 1) * 128], sib[:]),
                                    wait=w_, inc=[("t", 1)])
                            t_tr_d[tt] = P.val("t")
                            v_e3_d[tt] = P.op("vector", lambda e, tt=tt: e.tensor_copy(
                                apx(x2T[:], tt * 128, [[512, ND], [1, 128]]), pT3[tt % 2][:]),
                                wait=[("t", t_tr_d[tt])], inc=[("v", 1)])
                        t_tr_prev = t_tr_d
                        v_e3_prev = v_e3_d
                        pcaT.close()
                        pcaH = pcq.enter_context(contextlib.ExitStack())
                        psH = [pcaH.enter_context(nc.psum_tensor(
                            "psH%d_%d" % (qg, i), [128, 512], F32)) for i in range(4)]

                        v_lg_d, v_w1_d, t_h_d = {}, {}, {}
                        for tt in range(4):
                            for d in range(ND):
                                w_ = []
                                if d == 0:
                                    w_ = [("v", v_e3_d[tt])]
                                    if tt == 0:
                                        w_.append(("gld", gld_c2))
                                        if yev_qg_prev:
                                            w_.append(("v", yev_qg_prev))
                                    if tt >= 2:
                                        w_.append(("v", v_lg_d[tt - 2]))
                                P.op("tensor", lambda e, d=d, tt=tt: e.matmul(
                                    psR[tt % 2][:],
                                    x2T[:, d * 512 + tt * 128:d * 512 + (tt + 1) * 128],
                                    swrt[:, d * E:(d + 1) * E],
                                    start=(d == 0), stop=(d == ND - 1)),
                                    wait=w_, inc=[("t", 1)])
                            t_r = P.val("t")
                            v_lg = P.op("vector", lambda e, tt=tt: e.tensor_add(
                                slg[tt % 2][:], psR[tt % 2][:], sbr[:, :E]),
                                wait=[("t", t_r), ("ld1", ld1_tot)], inc=[("v", 1)])
                            v_lg_d[tt] = v_lg
                            v_mx = P.op("vector", lambda e, tt=tt: e.reduce_max(
                                smx[tt % 2][:], slg[tt % 2][:], AX), wait=[("v", v_lg)], inc=[("v", 1)])
                            v_nm = P.op("vector", lambda e, tt=tt: e.tensor_scalar_mul(
                                snm[tt % 2][:], smx[tt % 2][:], -1.0), wait=[("v", v_mx)], inc=[("v", 1)])
                            s_ex = P.op("scalar", lambda e, tt=tt: e.activation(
                                sex[tt % 2][:], slg[tt % 2][:], ACT.Exp, bias=snm[tt % 2][:], accum_out=ssm[tt % 2][:]),
                                wait=[("v", v_nm), ("s", P.val("s"))], inc=[("s", 1)])
                            v_rc = P.op("vector", lambda e, tt=tt: e.reciprocal(
                                src2[tt % 2][:], ssm[tt % 2][:]), wait=[("s", s_ex)], inc=[("v", 1)])
                            v_mk = P.op("vector", lambda e, tt=tt: e.tensor_scalar(
                                smk[tt % 2][:], slg[tt % 2][:], smx[tt % 2][:], None, ALU.is_equal),
                                wait=[("v", v_rc)], inc=[("v", 1)])
                            v_w0 = P.op("vector", lambda e, tt=tt: e.tensor_mul(
                                sw_[tt][:], smk[tt % 2][:], sex[tt % 2][:]), wait=[("v", v_mk)], inc=[("v", 1)])
                            v_w1_d[tt] = P.op("vector", lambda e, tt=tt: e.tensor_scalar(
                                sw_[tt][:], sw_[tt][:], src2[tt % 2][:], inv_rank, ALU.mult, ALU.mult),
                                wait=[("v", v_w0)], inc=[("v", 1)])
                            if tt == 1 or tt == 3:
                                for pair in (tt - 1, tt):
                                    for d in range(ND):
                                        w_ = []
                                        if d == 0 and pair == 0:
                                            w_ = [("v", v_e3_d[3])]
                                        P.op("tensor", lambda e, d=d, pair=pair: e.matmul(
                                            psH[pair][:],
                                            sAT[:, d * 512 + pair * 128:d * 512 + (pair + 1) * 128],
                                            x2T[:, d * 512:(d + 1) * 512],
                                            start=(d == 0), stop=(d == ND - 1)),
                                            wait=w_, inc=[("t", 1)])
                                    t_h_d[pair] = P.val("t")

                        v_wT_prev = 0
                        for tt in range(4):
                            w_ = [("v", v_w1_d[tt]), ("v", v_if)]
                            if v_wT_prev:
                                w_.append(("v", v_wT_prev))
                            t_wt = P.op("tensor", lambda e, tt=tt: e.transpose(
                                pWT[:], sw_[tt][:, :E], sif2[:]),
                                wait=w_, inc=[("t", 1)])
                            v_wT_prev = P.op("vector", lambda e, tt=tt: e.tensor_copy(
                                swT[:, tt * 128:(tt + 1) * 128], pWT[:]),
                                wait=[("t", t_wt)], inc=[("v", 1)])
                        v_swt = P.val("v")
                        v_wb_prev = 0
                        v_wb_d = {}
                        for pair in range(NP):
                            w_ = [("v", v_swt)]
                            if v_wb_prev:
                                w_.append(("v", v_wb_prev))
                            t_wb = P.op("tensor", lambda e, pair=pair: e.matmul(
                                pwb[:], sOH[:, pair * 128:(pair + 1) * 128],
                                swT[:, :512], start=True, stop=True),
                                wait=w_, inc=[("t", 1)])
                            v_wb_prev = P.op("vector", lambda e, pair=pair: e.tensor_copy(
                                wbb[:, pair * 512:(pair + 1) * 512], pwb[:]),
                                wait=[("t", t_wb)], inc=[("v", 1)])
                            v_wb_d[pair] = v_wb_prev
                        v_wbb = P.val("v")

                        for pair in range(NP):
                            P.op("vector", lambda e, pair=pair: e.tensor_mul(
                                hpb[:, pair * 512:(pair + 1) * 512], psH[pair][:],
                                wbb[:, pair * 512:(pair + 1) * 512]),
                                wait=[("t", t_h_d[pair]), ("v", v_wb_d[pair])], inc=[("v", 1)])
                        v_hp = P.val("v")
                        v_hp_qg_prev = v_hp
                        pcaH.close()
                        pca2.close()
                        psY = pcq.enter_context(nc.psum_tensor("psY%d" % qg, [128, D], F32))

                        for ts in range(4):
                            gi = qg * 4 + ts
                            ot = qg * 4 + ts
                            ob = gi % 2
                            for oc in range(NO4):
                                for pair in range(NP):
                                    w_ = []
                                    if oc == 0 and pair == 0:
                                        w_ = [("v", v_hp)]
                                        if ts > 0:
                                            w_.append(("v", h_yev_d[gi - 1]))
                                    P.op("tensor", lambda e, oc=oc, pair=pair, ts=ts: e.matmul(
                                        psY[:, oc * 512:(oc + 1) * 512],
                                        hpb[:, pair * 512 + ts * 128:pair * 512 + ts * 128 + 128],
                                        sBT[:, pair * D + oc * 512:pair * D + (oc + 1) * 512],
                                        start=(pair == 0), stop=(pair == NP - 1)),
                                        wait=w_, inc=[("t", 1)])
                            t_y = P.val("t")
                            yb = y2b[gi % 2]
                            h_yev = P.op("vector", lambda e, yb=yb: e.tensor_copy(
                                yb[:], psY[:]), wait=[("t", t_y)], inc=[("v", 1)])
                            h_yev_d[gi] = h_yev
                            s_r, lk = emit_ln_stats(
                                lambda i, yb=yb: yb[:, i * 512:(i + 1) * 512],
                                first_waits=[("v", h_yev)])
                            h_t4[gi] = P.op("vector", lambda e, lk=lk, yb=yb: e.tensor_scalar(
                                t4t[:], yb[:], smv[lk][:, 0:1], srs[lk][:],
                                ALU.subtract, ALU.mult), wait=[("s", s_r)], inc=[("v", 1)])
                            ln_done(lk, [("v", h_t4[gi])])
                            w_ = [("v", h_t4[gi]), ("g", g_x2_done)]
                            if gi >= 2:
                                w_.append(("st", P.val("st")))
                            v_ob = P.op("vector", lambda e, ot=ot, ob=ob: e.tensor_add(
                                outb[ob][:], t4t[:], x2[:, ot * D:(ot + 1) * D]),
                                wait=w_, inc=[("v", 1)])
                            st_prev[gi] = P.op("sync", lambda e, ot=ot, ob=ob: e.dma_start(
                                out=dout[ot * 128:(ot + 1) * 128, :], in_=outb[ob][:]),
                                wait=[("v", v_ob)], inc=[("st", 16)])
                        yev_qg_prev = h_yev_d[qg * 4 + 3]

                st_tot = P.val("st")
                P.op("sync", lambda e: None, wait=[("st", st_tot)])

        P.emit(block, sems)
    return nc


def _prep_inputs(cfg, inputs):
    T, D, H, G, E, R = cfg["T"], cfg["D"], cfg["H"], cfg["G"], cfg["E"], cfg["R"]
    DQ = 128
    NP = E // 2
    x = np.ascontiguousarray(np.asarray(inputs["L0_attn_hidden_states"],
                                        dtype=np.float32).reshape(T, D))
    perm = rope_perm(DQ)
    wq = np.asarray(inputs["L0_attn_Wq"], dtype=np.float32)
    wq = np.ascontiguousarray(wq.reshape(D, H, DQ)[:, :, perm].reshape(D, H * DQ))
    wk = np.asarray(inputs["L0_attn_Wk"], dtype=np.float32)
    wk = np.ascontiguousarray(wk.reshape(D, G, DQ)[:, :, perm].reshape(D, G * DQ))
    wv = np.ascontiguousarray(np.asarray(inputs["L0_attn_Wv"], dtype=np.float32))
    wkv = np.ascontiguousarray(np.concatenate([wk, wv], axis=1))
    wo = np.ascontiguousarray(np.asarray(inputs["L0_attn_Wo"], dtype=np.float32))
    bo = np.ascontiguousarray(np.asarray(inputs["L0_attn_bo"],
                                         dtype=np.float32).reshape(1, D))
    A = np.asarray(inputs["L0_ffn_A"], dtype=np.float32)
    at = np.ascontiguousarray(A.transpose(2, 0, 1).reshape(D, E * R))
    B = np.asarray(inputs["L0_ffn_B"], dtype=np.float32)
    bt = np.ascontiguousarray(B.transpose(0, 2, 1).reshape(NP, 2 * R, D))
    wrt = np.ascontiguousarray(np.asarray(inputs["L0_ffn_Wr"], dtype=np.float32).T)
    br = np.ascontiguousarray(np.asarray(inputs["L0_ffn_br"],
                                         dtype=np.float32).reshape(1, E))
    cs = make_cs(T, DQ)
    identb = np.eye(128, dtype=np.float32).astype(ml_dtypes.bfloat16)
    oh = make_oh(E)

    TB = T // NC if cfg.get("ag") else T
    blocks = np.arange(T // 16)
    in_maps, own_rows_all = [], []
    for c in range(NC):
        own_blocks = blocks[blocks % NC == c]
        rows = (own_blocks[:, None] * 16 + np.arange(16)[None, :]).reshape(-1)
        own_rows_all.append(rows)
        b0 = c * TB if cfg.get("ag") else 0
        in_maps.append(dict(
            xb=np.ascontiguousarray(x[b0:b0 + TB]),
            xq=np.ascontiguousarray(x[rows]), wq=wq, wkv=wkv, wo=wo,
            bo1=bo, at=at, bt=bt, wrt=wrt, br1=br, cb=make_cb(c),
            csk=np.ascontiguousarray(cs[b0:b0 + TB]),
            csq=np.ascontiguousarray(cs[rows]), oh=oh, identb=identb))
    return in_maps, own_rows_all


def kernel(**inputs):
    cfg = dict(FULL)
    nc_prog = build_nc(cfg)
    in_maps, own_rows = _prep_inputs(cfg, inputs)
    res = run_bass_kernel_spmd(nc_prog, in_maps, core_ids=list(range(NC)))
    T, D = cfg["T"], cfg["D"]
    out = np.empty((T, D), dtype=np.float32)
    for c in range(NC):
        out[own_rows[c]] = np.asarray(res.results[c]["out"], dtype=np.float32)
    return out.reshape(1, T, D)


# version 20
# speedup vs baseline: 1.2263x; 1.2263x over previous
import math
import contextlib
import numpy as np
import ml_dtypes
import concourse.bass as bass
import concourse.mybir as mybir
from concourse.bass_utils import run_bass_kernel_spmd

F32 = mybir.dt.float32
BF16 = mybir.dt.bfloat16
AX = mybir.AxisListType.X
ALU = mybir.AluOpType
ACT = mybir.ActivationFunctionType

NC = 8
BIGNEG = -1.0e9

FULL = dict(T=8192, D=2048, H=16, G=4, E=8, R=64, ag=True)


class Prog:

    def __init__(self):
        self.ops = []
        self.counts = {}

    def op(self, eng, fn, wait=(), inc=()):
        wait = [(s, int(v)) for (s, v) in wait if v > 0]
        inc = [(s, int(a)) for (s, a) in inc]
        self.ops.append((eng, fn, wait, inc))
        for s, a in inc:
            self.counts[s] = self.counts.get(s, 0) + a
        if len(inc) == 1:
            return self.counts[inc[0][0]]
        return None

    def val(self, sem):
        return self.counts.get(sem, 0)

    def emit(self, block, sems):
        for eng_name in ("sync", "tensor", "vector", "scalar", "gpsimd"):
            eng_ops = [o for o in self.ops if o[0] == eng_name]
            if not eng_ops:
                continue

            def make(ops_):
                def body(e):
                    for (_, fn, wait, inc) in ops_:
                        for s, v in wait:
                            e.wait_ge(sems[s], v)
                        instr = fn(e)
                        for s, a in inc:
                            instr = instr.then_inc(sems[s], a)
                return body
            getattr(block, eng_name)(make(eng_ops))


def apx(base_ap, offset_elems, dims):
    return bass.AP(base_ap.tensor, base_ap.offset + offset_elems,
                   [list(base_ap.ap[0])] + [list(d) for d in dims])


def dap(dram_ap, offset_elems, dims):
    return bass.AP(dram_ap.tensor, dram_ap.offset + offset_elems,
                   [list(d) for d in dims])


def rope_perm(dq=128):
    return np.concatenate([np.arange(0, dq, 2), np.arange(1, dq, 2)])


def make_cs(T, dq=128):
    inv = 1.0 / (10000.0 ** (np.arange(0, dq, 2, dtype=np.float64) / dq))
    f = np.arange(T, dtype=np.float64)[:, None] * inv[None, :]
    return np.concatenate([np.cos(f), np.sin(f)], axis=1).astype(np.float32)


def make_cb(core):
    W = 1008 + 512
    p = np.arange(128)[:, None]
    x = np.arange(W)[None, :]
    rel = x - 496
    ramp = -(128.0 * np.floor(rel / 16.0) + np.mod(rel, 16))
    cb = (ramp + p - 16.0 * core).astype(np.float32)
    cb[:, :496] = BIGNEG
    bnd = np.broadcast_to((x >= 496) & (x < 512), cb.shape)
    cb[np.asarray(bnd & (p > 16 * core + rel))] = BIGNEG
    return np.ascontiguousarray(cb)


def make_oh(E):
    oh = np.zeros((E, 4 * 128), dtype=np.float32)
    for pair in range(4):
        for m in range(128):
            e = 2 * pair + (1 if m >= 64 else 0)
            if e < E:
                oh[e, pair * 128 + m] = 1.0
    return oh


def build_nc(cfg):
    T, D, H, G, E, R = cfg["T"], cfg["D"], cfg["H"], cfg["G"], cfg["E"], cfg["R"]
    DQ = 128
    ND = D // 128
    NT = T // 128
    LQ = T // NC
    NOT = LQ // 128
    NQC = max(1, LQ // 512)
    NP = E // 2
    HD = H * DQ
    NH4 = HD // 512
    KVW = 2 * G * DQ
    GD = G * DQ
    NO4 = D // 512
    nbn = D // 512
    PPG = max(1, (H // G) // 2)
    NHP = H // 2
    slopes = [float(np.float32(2.0 ** (-(h + 1) * 8.0 / H))) for h in range(H)]
    c_tanh = float(np.float32(1.0 / (30.0 * math.sqrt(DQ))))
    inv_rank = float(np.float32(1.0 / R))

    nc = bass.Bass("TRN2", target_bir_lowering=False, debug=False)

    AG = bool(cfg.get("ag"))
    TB = T // NC if AG else T
    NTB = TB // 128
    SEC = 2 * G * DQ * TB
    VOFF = G * DQ * TB
    dxb = nc.dram_tensor("xb", [TB, D], F32, kind="ExternalInput").ap()
    dxq = nc.dram_tensor("xq", [LQ, D], F32, kind="ExternalInput").ap()
    dwq = nc.dram_tensor("wq", [D, HD], F32, kind="ExternalInput").ap()
    dwkv = nc.dram_tensor("wkv", [D, KVW], F32, kind="ExternalInput").ap()
    dwo = nc.dram_tensor("wo", [HD, D], F32, kind="ExternalInput").ap()
    dbo = nc.dram_tensor("bo1", [1, D], F32, kind="ExternalInput").ap()
    dat = nc.dram_tensor("at", [D, E * R], F32, kind="ExternalInput").ap()
    dbt = nc.dram_tensor("bt", [NP, 2 * R, D], F32, kind="ExternalInput").ap()
    dwrt = nc.dram_tensor("wrt", [D, E], F32, kind="ExternalInput").ap()
    dbr = nc.dram_tensor("br1", [1, E], F32, kind="ExternalInput").ap()
    dcb = nc.dram_tensor("cb", [128, 1520], F32, kind="ExternalInput").ap()
    dcsk = nc.dram_tensor("csk", [TB, DQ], F32, kind="ExternalInput").ap()
    dcsq = nc.dram_tensor("csq", [LQ, DQ], F32, kind="ExternalInput").ap()
    doh = nc.dram_tensor("oh", [E, 4 * 128], F32, kind="ExternalInput").ap()
    dib = nc.dram_tensor("identb", [128, 128], BF16, kind="ExternalInput").ap()
    from concourse.replica_groups import maybe_share_collective_output_space
    dkvo = nc.dram_tensor("kvown", [SEC], BF16, kind="Internal").ap()
    if AG:
        _sp = maybe_share_collective_output_space("AllGather",
                                                  [list(range(NC))])
        dkvag = nc.dram_tensor("kvag", [NC * SEC], BF16, kind="Internal",
                               addr_space=_sp).ap()
    else:
        dkvag = dkvo
    dout = nc.dram_tensor("out", [LQ, D], F32, kind="ExternalOutput").ap()
    if cfg.get("dbg"):
        dbg_qt = nc.dram_tensor("dbg_qt", [128, H * LQ], BF16, kind="ExternalOutput").ap()
        dbg_ct = nc.dram_tensor("dbg_ct", [128, H * LQ], BF16, kind="ExternalOutput").ap()
        dbg_x2 = nc.dram_tensor("dbg_x2", [128, NOT * D], BF16, kind="ExternalOutput").ap()
        dbg_ag = nc.dram_tensor("dbg_ag", [NC * SEC if AG else SEC], BF16, kind="ExternalOutput").ap()
        dbg_x1 = nc.dram_tensor("dbg_x1", [NTB * 128, D], BF16, kind="ExternalOutput").ap()
        dbg_y1 = nc.dram_tensor("dbg_y1", [128, NOT * D], BF16, kind="ExternalOutput").ap()

    est = contextlib.ExitStack()
    with est:
        sems = {}
        for s in ("ld1", "gld", "xk0", "xk1", "xk2", "cs0", "cs1", "cs2", "cs3", "cs4", "ksw", "vsw",
                  "qx0", "qx1", "wl0", "wl1", "kld0", "kld1", "vld0", "vld1",
                  "xo0", "xo1", "st", "v", "t", "s", "g", "agc"):
            sems[s] = est.enter_context(nc.semaphore(s))
        block = est.enter_context(nc.Block())

        sib = est.enter_context(nc.sbuf_tensor("sib", [128, 128], BF16))
        seps = est.enter_context(nc.sbuf_tensor("seps", [128, 1], F32))
        smv = [est.enter_context(nc.sbuf_tensor(f"smv{i}", [128, 2], F32)) for i in range(2)]
        srs = [est.enter_context(nc.sbuf_tensor(f"srs{i}", [128, 1], F32)) for i in range(2)]
        sst = [est.enter_context(nc.sbuf_tensor(f"sst{i}", [128, 24], F32)) for i in range(2)]
        x2 = est.enter_context(nc.sbuf_tensor("x2", [128, NOT * D], BF16))

        est_qb = contextlib.ExitStack()
        sQT = est_qb.enter_context(nc.sbuf_tensor("sQT", [128, H * LQ], BF16))
        sCB = est_qb.enter_context(nc.sbuf_tensor("sCB", [128, 1520], F32))

        P = Prog()
        P.op("sync", lambda e: e.dma_start(out=sCB[:], in_=dcb), inc=[("ld1", 16)])
        P.op("gpsimd", lambda e: e.dma_start(out=sib[:], in_=dib), inc=[("gld", 16)])
        v_eps = P.op("vector", lambda e: e.memset(seps[:], 1e-5), inc=[("v", 1)])

        ln_guard = [{"s": 0, "v": 0, "other": []} for _ in range(2)]
        ln_slot = [0]

        def emit_phase_barrier(extra=()):
            w = [("v", P.val("v")), ("s", P.val("s")), ("t", P.val("t")),
                 ("g", P.val("g")), ("st", P.val("st"))] + list(extra)
            for eng_ in ("sync", "tensor", "vector", "scalar", "gpsimd"):
                P.op(eng_, lambda e: None, wait=w)

        def emit_ln_stats(src_fn, first_waits=()):
            k = ln_slot[0]
            ln_slot[0] ^= 1
            g = ln_guard[k]
            for i in range(nbn):
                w0 = list(first_waits) + [("v", g["v"])] if i == 0 else []
                P.op("vector",
                     lambda e, i=i, k=k: e.bn_stats(sst[k][:, i * 6:(i + 1) * 6], src_fn(i)),
                     wait=w0, inc=[("v", 1)])
            w_ = [("v", P.val("v")), ("s", g["s"])] + list(g["other"])
            v_bn = P.op("vector", lambda e, k=k: e.bn_aggr(smv[k][:], sst[k][:, :nbn * 6]),
                        wait=w_, inc=[("v", 1)])
            g["v"] = v_bn
            w_ = [("v", v_bn), ("v", v_eps)] + list(g["other"])
            s_l = P.op("scalar", lambda e, k=k: e.activation(
                srs[k][:], smv[k][:, 1:2], ACT.Ln, bias=seps[:]), wait=w_, inc=[("s", 1)])
            s_r = P.op("scalar", lambda e, k=k: e.activation(
                srs[k][:], srs[k][:], ACT.Exp, scale=-0.5), wait=[("s", s_l)], inc=[("s", 1)])
            g["s"] = s_r
            return s_r, k

        def ln_done(k, waits):
            ln_guard[k]["other"] = list(waits)

        estA2 = contextlib.ExitStack()
        sWqb0 = estA2.enter_context(nc.sbuf_tensor("sWqb0", [128, ND * 512], BF16))
        with contextlib.ExitStack() as pa:
            sWkv = pa.enter_context(nc.sbuf_tensor("sWkv", [128, ND * KVW], BF16))
            xbuf = [pa.enter_context(nc.sbuf_tensor(f"xb{i}", [128, D], F32)) for i in range(3)]
            x1b = [pa.enter_context(nc.sbuf_tensor(f"x1b{i}", [128, D], BF16)) for i in range(2)]
            x1T = [pa.enter_context(nc.sbuf_tensor(f"x1T{i}", [128, ND * 128], BF16)) for i in range(2)]
            kvb = [pa.enter_context(nc.sbuf_tensor(f"kvb{i}", [128, KVW], BF16)) for i in range(2)]
            krp = [pa.enter_context(nc.sbuf_tensor(f"krp{i}", [128, GD], BF16)) for i in range(2)]
            kts = [pa.enter_context(nc.sbuf_tensor(f"kts{i}", [128, GD], BF16)) for i in range(2)]
            csb = [pa.enter_context(nc.sbuf_tensor(f"csb{i}", [128, DQ], F32)) for i in range(5)]
            tmp12 = pa.enter_context(nc.sbuf_tensor("tmp12", [128, GD], F32))
            pT = [pa.enter_context(nc.psum_tensor(f"pT{i}", [128, ND * 128], BF16)) for i in range(2)]
            pK = [pa.enter_context(nc.psum_tensor(f"pK{i}", [128, GD], F32)) for i in range(1)]
            pV = [pa.enter_context(nc.psum_tensor(f"pV{i}", [128, GD], F32)) for i in range(1)]
            pKT = [pa.enter_context(nc.psum_tensor(f"pKT{i}", [128, GD], BF16)) for i in range(2)]

            for d in range(ND):
                P.op("gpsimd", lambda e, d=d: e.dma_start(
                    out=sWkv[:, d * KVW:(d + 1) * KVW],
                    in_=dwkv[d * 128:(d + 1) * 128, :]), inc=[("gld", 16)])
            gld_w = P.val("gld")
            P.op("gpsimd", lambda e: e.dma_start(
                out=sWqb0[:],
                in_=dap(dwq, 0, [[HD, 128], [128 * HD, ND], [1, 512]])),
                inc=[("wl0", 16)])

            h_apply, h_rope, h_evT, h_kv, h_kvev, h_ktr, h_evKT, h_tr = \
                {}, {}, {}, {}, {}, {}, {}, {}
            xk_tot_d, cs_tot_d = {}, {}

            def a1_load(t):
                xs = f"xk{t % 3}"
                w_ = [("v", h_apply[t - 3])] if t >= 3 else []
                P.op("sync", lambda e, t=t: e.dma_start(
                    out=xbuf[t % 3][:], in_=dxb[t * 128:(t + 1) * 128, :]),
                    wait=w_, inc=[(xs, 16)])
                xk_tot_d[t] = P.val(xs)
                cs_s = f"cs{t % 5}"
                w_ = [("g", h_rope[t - 5])] if t >= 5 else []
                P.op("sync", lambda e, t=t: e.dma_start(
                    out=csb[t % 5][:], in_=dcsk[t * 128:(t + 1) * 128, :]),
                    wait=w_, inc=[(cs_s, 16)])
                cs_tot_d[t] = P.val(cs_s)

            def a1_ln(t):
                s_r, lk = emit_ln_stats(
                    lambda i, t=t: xbuf[t % 3][:, i * 512:(i + 1) * 512],
                    first_waits=[(f"xk{t % 3}", xk_tot_d[t])])
                w_ = [("s", s_r)]
                if t >= 2:
                    w_.append(("t", h_tr[t - 2]))
                if cfg.get("dbg"):
                    w_.append(("st", P.val("st")))
                h_apply[t] = P.op("vector", lambda e, t=t, lk=lk: e.tensor_scalar(
                    x1b[t % 2][:], xbuf[t % 3][:], smv[lk][:, 0:1], srs[lk][:],
                    ALU.subtract, ALU.mult), wait=w_, inc=[("v", 1)])
                ln_done(lk, [("v", h_apply[t])])

            def a1_dbg(t):
                P.op("sync", lambda e, t=t: e.dma_start(
                    out=dbg_x1[t * 128:(t + 1) * 128, :], in_=x1b[t % 2][:]),
                    wait=[("v", h_apply[t])], inc=[("st", 16)])

            def a1_trans(t):
                for d in range(ND):
                    w_ = [("v", h_apply[t])] if d == 0 else []
                    if d == 0 and t >= 2:
                        w_.append(("v", h_evT[t - 2]))
                    if d == 0 and t == 0:
                        w_.append(("gld", gld_w))
                    P.op("tensor", lambda e, d=d, t=t: e.transpose(
                        pT[t % 2][:, d * 128:(d + 1) * 128],
                        x1b[t % 2][:, d * 128:(d + 1) * 128], sib[:]),
                        wait=w_, inc=[("t", 1)])
                h_tr[t] = P.val("t")

            def a1_evT(t):
                w_ = [("t", h_tr[t])]
                if t >= 2:
                    w_.append(("t", h_kv[t - 2]))
                h_evT[t] = P.op("vector", lambda e, t=t: e.tensor_copy(
                    x1T[t % 2][:], pT[t % 2][:]), wait=w_, inc=[("v", 1)])

            def a1_kv(t):
                for d in range(ND):
                    for half in range(2):
                        w_ = []
                        if d == 0 and half == 0:
                            w_.append(("v", h_evT[t]))
                            if t >= 1:
                                w_.append(("s", h_kvev[t - 1]))
                            if t == 0:
                                w_.append(("gld", gld_w))
                        P.op("tensor", lambda e, d=d, half=half, t=t: e.matmul(
                            (pK if half == 0 else pV)[0][:],
                            x1T[t % 2][:, d * 128:(d + 1) * 128],
                            sWkv[:, d * KVW + half * GD:d * KVW + (half + 1) * GD],
                            start=(d == 0), stop=(d == ND - 1)),
                            wait=w_, inc=[("t", 1)])
                h_kv[t] = P.val("t")

            def a1_copies(t):
                w_ = [("t", h_kv[t])]
                if t >= 2:
                    w_.append(("g", h_rope[t - 2]))
                    w_.append(("vsw", 16 * t))
                P.op("scalar", lambda e, t=t: e.copy(
                    kvb[t % 2][:, 0:GD], pK[0][:]), wait=w_, inc=[("s", 1)])
                h_kvev[t] = P.op("scalar", lambda e, t=t: e.copy(
                    kvb[t % 2][:, GD:2 * GD], pV[0][:]), inc=[("s", 1)])

            def a1_rope(t):
                kp = t % 2
                ke = apx(kvb[kp][:], 0, [[DQ, G], [1, 64]])
                ko = apx(kvb[kp][:], 64, [[DQ, G], [1, 64]])
                re_ = apx(krp[kp][:], 0, [[DQ, G], [1, 64]])
                ro_ = apx(krp[kp][:], 64, [[DQ, G], [1, 64]])
                cse = apx(csb[t % 5][:], 0, [[0, G], [1, 64]])
                cso = apx(csb[t % 5][:], 64, [[0, G], [1, 64]])
                t1 = apx(tmp12[:], 0, [[64, G], [1, 64]])
                t2 = apx(tmp12[:], GD // 2, [[64, G], [1, 64]])
                w_ = [("s", h_kvev[t]), (f"cs{t % 5}", cs_tot_d[t])]
                if t >= 2:
                    w_.append(("t", h_ktr[t - 2]))
                if t >= 1:
                    w_.append(("g", h_rope[t - 1]))
                P.op("gpsimd", lambda e, a=t1, b=ke, c=cso: e.tensor_mul(a, b, c),
                     wait=w_, inc=[("g", 1)])
                P.op("gpsimd", lambda e, a=t2, b=ko, c=cso: e.tensor_mul(a, b, c),
                     inc=[("g", 1)])
                P.op("gpsimd", lambda e, a=re_, b=ke, c=cse: e.tensor_mul(a, b, c),
                     inc=[("g", 1)])
                g0 = P.op("gpsimd", lambda e, a=ro_, b=ko, c=cse: e.tensor_mul(a, b, c),
                          inc=[("g", 1)])
                g1 = P.op("gpsimd", lambda e, a=re_, b=t2: e.tensor_sub(a, a, b),
                          wait=[("g", g0)], inc=[("g", 1)])
                h_rope[t] = P.op("gpsimd", lambda e, a=ro_, b=t1: e.tensor_add(a, a, b),
                                 wait=[("g", g1)], inc=[("g", 1)])

            def a1_vstore(t):
                P.op("sync", lambda e, t=t: e.dma_start(
                    out=dap(dkvo, VOFF + t * 128 * GD, [[GD, 128], [1, GD]]),
                    in_=kvb[t % 2][:, GD:2 * GD]),
                    wait=[("s", h_kvev[t])], inc=[("vsw", 16)])

            def a1_ktrans(t):
                for g_ in range(G):
                    w_ = [("g", h_rope[t])] if g_ == 0 else []
                    if g_ == 0 and t >= 2:
                        w_.append(("v", h_evKT[t - 2]))
                    P.op("tensor", lambda e, g_=g_, t=t: e.transpose(
                        pKT[t % 2][:, g_ * 128:(g_ + 1) * 128],
                        krp[t % 2][:, g_ * 128:(g_ + 1) * 128], sib[:]),
                        wait=w_, inc=[("t", 1)])
                h_ktr[t] = P.val("t")

            def a1_evKT(t):
                w_ = [("t", h_ktr[t])]
                if t >= 2:
                    w_.append(("ksw", 16 * t))
                h_evKT[t] = P.op("vector", lambda e, t=t: e.tensor_copy(
                    kts[t % 2][:], pKT[t % 2][:]), wait=w_, inc=[("v", 1)])

            def a1_kstore(t):
                P.op("sync", lambda e, t=t: e.dma_start(
                    out=dap(dkvo, t * 128, [[TB, 128], [DQ * TB, G], [1, 128]]),
                    in_=apx(kts[t % 2][:], 0, [[128, G], [1, 128]])),
                    wait=[("v", h_evKT[t])], inc=[("ksw", 16)])

            a1_load(0)
            a1_load(1)
            for i in range(NTB + 3):
                if i + 2 < NTB:
                    a1_load(i + 2)
                if i < NTB:
                    a1_ln(i)
                    if cfg.get("dbg"):
                        a1_dbg(i)
                if 0 <= i - 1 < NTB:
                    a1_trans(i - 1)
                    a1_evT(i - 1)
                if 0 <= i - 2 < NTB:
                    a1_kv(i - 2)
                    a1_copies(i - 2)
                    a1_rope(i - 2)
                    a1_vstore(i - 2)
                if 0 <= i - 3 < NTB:
                    a1_ktrans(i - 3)
                    a1_evKT(i - 3)
                    a1_kstore(i - 3)

            ksw_tot = P.val("ksw")
            vsw_tot = P.val("vsw")
            v_a1_done = P.val("v")
            s_a1_done = P.val("s")
            if AG:
                g_ag = P.op("gpsimd", lambda e: e.collective_compute(
                    "AllGather", ALU.bypass,
                    replica_groups=[list(range(NC))],
                    ins=[dkvo.opt()], outs=[dkvag.opt()]),
                    wait=[("ksw", ksw_tot), ("vsw", vsw_tot)], inc=[("agc", 1)])
                if cfg.get("dbg"):
                    P.op("sync", lambda e: e.dma_start(
                        out=dap(dbg_ag, 0, [[1024, SEC // 1024], [1, 1024]]),
                        in_=dap(dkvo, 0, [[1024, SEC // 1024], [1, 1024]])),
                        wait=[("ksw", ksw_tot), ("vsw", vsw_tot)], inc=[("st", 16)])
            else:
                g_ag = 0

        emit_phase_barrier(extra=[("ksw", ksw_tot), ("vsw", vsw_tot)])
        with contextlib.ExitStack() as pa2:
            xqb = [pa2.enter_context(nc.sbuf_tensor(f"xqb{i}", [128, D], F32)) for i in range(2)]
            xq1b = [pa2.enter_context(nc.sbuf_tensor(f"xq1b{i}", [128, D], BF16)) for i in range(2)]
            oxT = pa2.enter_context(nc.sbuf_tensor("oxT", [128, ND * LQ], BF16))
            sWqb = [sWqb0,
                    pa2.enter_context(nc.sbuf_tensor("sWqb1", [128, ND * 512], BF16))]
            scq = pa2.enter_context(nc.sbuf_tensor("scq", [128, NOT * DQ], F32))
            qpb = [pa2.enter_context(nc.sbuf_tensor(f"qpb{i}", [128, 512], BF16)) for i in range(2)]
            qrp = [pa2.enter_context(nc.sbuf_tensor(f"qrp{i}", [128, 512], BF16)) for i in range(2)]
            qt1 = pa2.enter_context(nc.sbuf_tensor("qt1", [128, 256], F32))
            qt2 = pa2.enter_context(nc.sbuf_tensor("qt2", [128, 256], F32))
            pT2 = pa2.enter_context(nc.psum_tensor("pT2", [128, ND * 128], BF16))
            pQ = [pa2.enter_context(nc.psum_tensor(f"pQ{i}", [128, 512], F32)) for i in range(2)]
            pQT = pa2.enter_context(nc.psum_tensor("pQT", [128, 1024], BF16))

            P.op("sync", lambda e: e.dma_start(
                out=apx(scq[:], 0, [[128, NOT], [1, DQ]]),
                in_=dap(dcsq, 0, [[DQ, 128], [128 * DQ, NOT], [1, DQ]])),
                 inc=[("ld1", 16)])
            ld1_tot = P.val("ld1")

            ha_ap, ha_evT = {}, {}

            def emit_prep(ot):
                sx = ot % 2
                qs_ = f"qx{sx}"
                w_ = [("v", ha_ap[ot - 2])] if ot >= 2 else []
                P.op("sync", lambda e, ot=ot, sx=sx: e.dma_start(
                    out=xqb[sx][:], in_=dxq[ot * 128:(ot + 1) * 128, :]),
                    wait=w_, inc=[(qs_, 16)])
                qx_tot = P.val(qs_)
                s_r, lk = emit_ln_stats(
                    lambda i, sx=sx: xqb[sx][:, i * 512:(i + 1) * 512],
                    first_waits=[(qs_, qx_tot)])
                ha_ap[ot] = P.op("vector", lambda e, sx=sx, lk=lk: e.tensor_scalar(
                    xq1b[sx][:], xqb[sx][:], smv[lk][:, 0:1], srs[lk][:],
                    ALU.subtract, ALU.mult), wait=[("s", s_r)], inc=[("v", 1)])
                ln_done(lk, [("v", ha_ap[ot])])
                for d in range(ND):
                    w_ = [("v", ha_ap[ot])]
                    if d == 0 and ot > 0:
                        w_.append(("v", ha_evT[ot - 1]))
                    if d == 0 and ot == 0:
                        w_ += [("v", v_a1_done), ("s", s_a1_done)]
                    P.op("tensor", lambda e, d=d, sx=sx: e.transpose(
                        pT2[:, d * 128:(d + 1) * 128],
                        xq1b[sx][:, d * 128:(d + 1) * 128], sib[:]),
                        wait=w_, inc=[("t", 1)])
                ha_evT[ot] = P.op("vector", lambda e, ot=ot: e.tensor_copy(
                    apx(oxT[:], ot * 128, [[LQ, ND], [1, 128]]), pT2[:]),
                    wait=[("t", P.val("t"))], inc=[("v", 1)])

            emit_prep(0)
            if NOT > 1:
                emit_prep(1)

            h_wl, h_qmm, h_qrope, h_qtr, h_qev = {}, {}, {}, {}, {}

            def mk_qtail(idx, nc4, ot, qx):
                def emit():
                    for j in range(4):
                        w_ = [("v", h_qrope[idx])]
                        if j == 0 and idx >= 2:
                            w_.append(("v", h_qev[idx - 2]))
                        P.op("tensor", lambda e, j=j, qx=qx: e.transpose(
                            pQT[:, qx * 512 + j * 128:qx * 512 + (j + 1) * 128],
                            qrp[qx][:, j * 128:(j + 1) * 128], sib[:]),
                            wait=w_, inc=[("t", 1)])
                    h_qtr[idx] = P.val("t")
                    h_qev[idx] = P.op("vector", lambda e, nc4=nc4, ot=ot, qx=qx:
                                      e.tensor_copy(
                                          apx(sQT[:], (nc4 * 4) * LQ + ot * 128,
                                              [[LQ, 4], [1, 128]]),
                                          pQT[:, qx * 512:(qx + 1) * 512]),
                                      wait=[("t", h_qtr[idx])], inc=[("v", 1)])
                return emit

            pending_qtail = None
            idx = 0
            for nc4 in range(NH4):
                wp = nc4 % 2
                ws = f"wl{wp}"
                if nc4 > 0:
                    w_ = [("t", h_qmm[(nc4 - 2) * NOT + NOT - 1])] if nc4 >= 2 else []
                    P.op("gpsimd", lambda e, nc4=nc4, wp=wp: e.dma_start(
                        out=sWqb[wp][:],
                        in_=dap(dwq, nc4 * 512, [[HD, 128], [128 * HD, ND], [1, 512]])),
                        wait=w_, inc=[(ws, 16)])
                wl_tot = P.val(ws)
                for ot in range(NOT):
                    qp = idx % 2
                    for d in range(ND):
                        w_ = []
                        if d == 0:
                            w_ = [("v", ha_evT[ot]), (ws, wl_tot)]
                            if idx == 0:
                                w_.append(("s", s_a1_done))
                            if idx >= 2:
                                w_.append(("v", h_qev[idx - 2]))
                        P.op("tensor", lambda e, d=d, ot=ot, wp=wp, qp=qp: e.matmul(
                            pQ[qp][:],
                            oxT[:, d * LQ + ot * 128:d * LQ + (ot + 1) * 128],
                            sWqb[wp][:, d * 512:(d + 1) * 512],
                            start=(d == 0), stop=(d == ND - 1)),
                            wait=w_, inc=[("t", 1)])
                    h_qmm[idx] = P.val("t")
                    if nc4 == 0 and ot + 2 < NOT:
                        emit_prep(ot + 2)
                    if pending_qtail is not None:
                        pending_qtail()
                    qx = idx % 2
                    w_ = [("t", h_qmm[idx])]
                    if idx >= 2:
                        w_.append(("v", h_qrope[idx - 2]))
                    v_ev = P.op("vector", lambda e, qp=qp, qx=qx: e.tensor_copy(
                        qpb[qx][:], pQ[qp][:]), wait=w_, inc=[("v", 1)])
                    qe = apx(qpb[qx][:], 0, [[DQ, 4], [1, 64]])
                    qo = apx(qpb[qx][:], 64, [[DQ, 4], [1, 64]])
                    rqe = apx(qrp[qx][:], 0, [[DQ, 4], [1, 64]])
                    rqo = apx(qrp[qx][:], 64, [[DQ, 4], [1, 64]])
                    qce = apx(scq[:], ot * DQ, [[0, 4], [1, 64]])
                    qco = apx(scq[:], ot * DQ + 64, [[0, 4], [1, 64]])
                    qtt1 = apx(qt1[:], 0, [[64, 4], [1, 64]])
                    qtt2 = apx(qt2[:], 0, [[64, 4], [1, 64]])
                    w_ = [("v", v_ev), ("ld1", ld1_tot)]
                    if idx >= 2:
                        w_.append(("t", h_qtr[idx - 2]))
                    if idx >= 1:
                        w_.append(("v", h_qrope[idx - 1]))
                    P.op("vector", lambda e, a=qtt1, b=qe, c=qco: e.tensor_mul(a, b, c),
                         wait=w_, inc=[("v", 1)])
                    P.op("vector", lambda e, a=qtt2, b=qo, c=qco: e.tensor_mul(a, b, c),
                         inc=[("v", 1)])
                    P.op("vector", lambda e, a=rqe, b=qe, c=qce: e.tensor_mul(a, b, c),
                         inc=[("v", 1)])
                    gg = P.op("vector", lambda e, a=rqo, b=qo, c=qce: e.tensor_mul(a, b, c),
                              inc=[("v", 1)])
                    gg = P.op("vector", lambda e, a=rqe, b=qtt2: e.tensor_sub(a, a, b),
                              wait=[("v", gg)], inc=[("v", 1)])
                    h_qrope[idx] = P.op("vector", lambda e, a=rqo, b=qtt1: e.tensor_add(a, a, b),
                                        wait=[("v", gg)], inc=[("v", 1)])
                    pending_qtail = mk_qtail(idx, nc4, ot, qx)
                    idx += 1
            pending_qtail()
            v_qt_done = P.val("v")
            s_a2_done = P.val("s")
            if cfg.get("dbg"):
                P.op("sync", lambda e: e.dma_start(out=dbg_qt, in_=sQT[:]),
                     wait=[("v", v_qt_done)], inc=[("st", 16)])

        ld1_cb = P.val("ld1")
        estA2.close()
        estL2 = contextlib.ExitStack()
        est_qbL = contextlib.ExitStack()
        with contextlib.ExitStack() as pb:
            sWoB = estL2.enter_context(nc.sbuf_tensor("sWoB", [128, H * D], BF16))
            sCT = estL2.enter_context(nc.sbuf_tensor("sCT", [128, H * LQ], BF16))
            kcb = [est_qbL.enter_context(nc.sbuf_tensor(f"kcb{i}", [128, 8 * 128], BF16)) for i in range(2)]
            vcb = [est_qbL.enter_context(nc.sbuf_tensor(f"vcb{i}", [128, 8 * 129], BF16)) for i in range(2)]
            pbf = [est_qbL.enter_context(nc.sbuf_tensor(f"pbf{i}", [128, 1024], BF16)) for i in range(3)]
            tb = [est_qbL.enter_context(nc.sbuf_tensor(f"tb{i}", [128, 1024], BF16)) for i in range(2)]
            srec3 = est_qbL.enter_context(nc.sbuf_tensor("srec3", [128, 9], F32))
            sctn3 = est_qbL.enter_context(nc.sbuf_tensor("sctn3", [128, 1152], BF16))
            psS = [pb.enter_context(nc.psum_tensor(f"psS{i}", [128, 1024], F32)) for i in range(2)]
            psC = pb.enter_context(nc.psum_tensor("psC", [128, 1536], F32))
            pTC = pb.enter_context(nc.psum_tensor("pTC", [128, 1024], BF16))

            for nc4 in range(NO4):
                P.op("gpsimd", lambda e, nc4=nc4: e.dma_start(
                    out=apx(sWoB[:], nc4 * 512, [[D, H], [1, 512]]),
                    in_=dap(dwo, nc4 * 512, [[D, 128], [128 * D, H], [1, 512]])),
                    wait=([("v", v_qt_done), ("s", s_a2_done)] if nc4 == 0 else []),
                    inc=[("gld", 16)])

            s_tb = P.op("scalar", lambda e: e.nop(),
                        wait=[("s", s_a2_done)], inc=[("s", 1)])
            P.op("vector", lambda e: e.memset(
                apx(vcb[0][:], 128, [[129, 8], [1, 1]]), 1.0), inc=[("v", 1)])
            v_ones = P.op("vector", lambda e: e.memset(
                apx(vcb[1][:], 128, [[129, 8], [1, 1]]), 1.0), inc=[("v", 1)])

            fin_cell = {"v": v_ones, "psc_free": v_ones}
            rounds = [(qc, hp) for qc in range(NQC) for hp in range(NHP)]
            rpc = 1024 // TB if TB < 1024 else 1
            tpr = min(8, TB // 128)

            def round_setup(qc, hp):
                nkt = 4 * NC * (qc + 1)
                acc_last = [min(nkt - 1, 32 * qc + 8 * (a % 4) + 7)
                            for a in range(8)]
                return {"qc": qc, "hp": hp, "g_": hp // PPG, "nkt": nkt,
                        "nch": nkt // 8, "acc_last": acc_last,
                        "bank_stop_acc": [max(range(3 * b, min(3 * b + 3, 8)),
                                              key=lambda a: acc_last[a])
                                          for b in range(3)],
                        "h_kl": {}, "h_vl": {}, "h_sc": {}, "h_ctx": {},
                        "h_tanh": {}, "h_exp": {}, "fa_d": {},
                        "pend_ctx": {}, "pending_exp": None, "pbf_war": {}}

            def plan_loads(RS, ch, wk, wv):
                cp = ch % 2
                g_ = RS["g_"]
                if AG:
                    w0 = [("agc", g_ag)]
                    koff = (ch * 1024 // TB) * SEC + ((ch * 1024) % TB)
                    kdims = [[TB, 128], [SEC, rpc], [1, TB]] if rpc > 1 \
                        else [[TB, 128], [1, 1024]]
                    voff = (ch * 1024 // TB) * SEC + VOFF + g_ * DQ
                    vdims = [[GD, 128], [SEC, rpc], [128 * GD, tpr], [1, 128]] if rpc > 1 \
                        else [[GD, 128], [128 * GD, 8], [1, 128]]
                else:
                    w0 = [("ksw", ksw_tot)]
                    koff = ch * 1024
                    kdims = [[TB, 128], [1, 1024]]
                    voff = (ch * 1024) * GD + g_ * DQ
                    vdims = [[GD, 128], [128 * GD, 8], [1, 128]]
                P.op("sync", lambda e, cp=cp, g_=g_, koff=koff, kdims=kdims:
                     e.dma_start(
                         out=kcb[cp][:],
                         in_=dap(dkvag, (g_ * DQ) * TB + koff, kdims)),
                     wait=w0 + wk, inc=[(f"kld{cp}", 16)])
                RS["h_kl"][ch] = P.val(f"kld{cp}")
                if AG and rpc > 1:
                    for r2 in range(rpc):
                        P.op("sync", lambda e, cp=cp, voff=voff, r2=r2:
                             e.dma_start(
                                 out=apx(vcb[cp][:], r2 * tpr * 129,
                                         [[129, tpr], [1, 128]]),
                                 in_=dap(dkvag, voff + r2 * SEC,
                                         [[GD, 128], [128 * GD, tpr], [1, 128]])),
                             wait=(w0 + [("v", v_ones)] + wv) if r2 == 0 else [],
                             inc=[(f"vld{cp}", 16)])
                else:
                    P.op("sync", lambda e, cp=cp, voff=voff, vdims=vdims:
                         e.dma_start(
                             out=apx(vcb[cp][:], 0, [[129, 8], [1, 128]]),
                             in_=dap(dkvag, voff, vdims)),
                         wait=w0 + [("v", v_ones)] + wv,
                         inc=[(f"vld{cp}", 16)])
                RS["h_vl"][ch] = P.val(f"vld{cp}")

            def make_ctx(RS, kt, h_exp):
                sp, cp, ch = kt % 3, (kt // 8) % 2, kt // 8
                fa = RS["fa_d"][kt]
                acc_last, bank_stop_acc = RS["acc_last"], RS["bank_stop_acc"]

                def emit():
                    first_mm = [kt == 0] * 3
                    nmm = 0
                    for j in range(2):
                        qoff = fa
                        while qoff < 512:
                            sw = min(128 - (qoff % 128), 512 - qoff)
                            a = j * 4 + (qoff // 128)
                            bank = a // 3
                            st_flag = bool(first_mm[bank])
                            first_mm[bank] = False
                            sp_flag = ((kt == acc_last[a])
                                       and (a == bank_stop_acc[bank]))
                            w_ = []
                            if nmm == 0:
                                w_ = [("s", h_exp), (f"vld{cp}", RS["h_vl"][ch])]
                                if kt == 0:
                                    w_.append(("v", fin_cell["psc_free"]))
                            nmm += 1
                            P.op("tensor", lambda e, j=j, qoff=qoff, sw=sw, a=a,
                                 sp=sp, cp=cp, kt=kt, st_flag=st_flag,
                                 sp_flag=sp_flag: e.matmul(
                                     psC[(qoff % 128):(qoff % 128) + sw,
                                         (a // 3) * 512 + (a % 3) * 129:
                                         (a // 3) * 512 + (a % 3) * 129 + 129],
                                     pbf[sp][:, j * 512 + qoff:j * 512 + qoff + sw],
                                     vcb[cp][:, (kt % 8) * 129:(kt % 8) * 129 + 129],
                                     start=st_flag, stop=sp_flag),
                                 wait=w_, inc=[("t", 1)])
                            qoff += sw
                    RS["h_ctx"][kt] = P.val("t")
                return emit

            def mk_exp(RS, kt):
                fa = RS["fa_d"][kt]
                w = 512 - fa

                def emit():
                    w_ = [("s", RS["h_tanh"][kt])]
                    if kt >= 3:
                        w_.append(("t", RS["h_ctx"][kt - 3]))
                    elif RS["pbf_war"].get(kt % 3):
                        w_.append(("t", RS["pbf_war"][kt % 3]))
                    return P.op("scalar", lambda e: e.activation(
                        apx(pbf[kt % 3][:], fa, [[512, 2], [1, w]]),
                        apx(tb[kt % 2][:], fa, [[512, 2], [1, w]]),
                        ACT.Exp, scale=30.0),
                        wait=w_, inc=[("s", 1)])
                return emit

            def emit_scores(RS, kt, w_extra0):
                qc, hp = RS["qc"], RS["hp"]
                sp, cp, ch = kt % 2, (kt // 8) % 2, kt // 8
                gk = kt - 32 * qc
                fa = 0 if gk < 0 else min(384, (gk // 8) * 128)
                RS["fa_d"][kt] = fa
                w = 512 - fa
                h_scj = {}
                for j in range(2):
                    h = 2 * hp + j
                    w_ = []
                    if j == 0:
                        w_ = [(f"kld{cp}", RS["h_kl"][ch])] + w_extra0
                    P.op("tensor", lambda e, j=j, h=h, kt=kt, sp=sp, cp=cp,
                         fa=fa, w=w, qc=qc: e.matmul(
                             psS[sp][:, j * 512 + fa:j * 512 + fa + w],
                             kcb[cp][:, (kt % 8) * 128:(kt % 8 + 1) * 128],
                             sQT[:, h * LQ + qc * 512 + fa:h * LQ + qc * 512 + fa + w],
                             start=True, stop=True),
                         wait=w_, inc=[("t", 1)])
                    h_scj[j] = P.val("t")
                RS["h_sc"][kt] = P.val("t")
                return h_scj

            def emit_stt(RS, kt, h_scj, extra0):
                qc, hp = RS["qc"], RS["hp"]
                sp = kt % 2
                fa = RS["fa_d"][kt]
                w = 512 - fa
                x0 = 496 - 16 * (kt - 32 * qc)
                for j in range(2):
                    h = 2 * hp + j
                    w_ = [("t", h_scj[j])] + (extra0 if j == 0 else [])
                    P.op("vector", lambda e, j=j, h=h, sp=sp, fa=fa, w=w, x0=x0:
                         e.scalar_tensor_tensor(
                             psS[sp][:, j * 512 + fa:j * 512 + fa + w],
                             sCB[:, x0 + fa:x0 + fa + w], slopes[h],
                             psS[sp][:, j * 512 + fa:j * 512 + fa + w],
                             ALU.mult, ALU.add),
                         wait=w_, inc=[("v", 1)])
                return P.val("v")

            def emit_tanh(RS, kt, h_stt, extra):
                sp = kt % 2
                fa = RS["fa_d"][kt]
                w = 512 - fa
                RS["h_tanh"][kt] = P.op("scalar", lambda e, sp=sp, fa=fa, w=w:
                                        e.activation(
                                            apx(tb[sp][:], fa, [[512, 2], [1, w]]),
                                            apx(psS[sp][:], fa, [[512, 2], [1, w]]),
                                            ACT.Tanh, scale=c_tanh),
                                        wait=[("v", h_stt)] + extra, inc=[("s", 1)])

            def make_fin_pieces(h_last, hp, qc):
                st = {"t_prev": 0}

                def rcn_piece(b):
                    def emit():
                        nacc = 3 if b < 2 else 2
                        base = b * 512
                        v_rc = P.op("vector", lambda e, base=base, nacc=nacc:
                                    e.reciprocal(
                                        srec3[:, 3 * b:3 * b + nacc],
                                        apx(psC[:], base + 128,
                                            [[129, nacc], [1, 1]])),
                                    wait=[("t", h_last), ("v", P.val("v"))],
                                    inc=[("v", 1)])
                        fin_cell["psc_free"] = P.op(
                            "vector", lambda e, base=base, nacc=nacc, b=b:
                            e.tensor_tensor(
                                apx(sctn3[:], b * 384, [[128, nacc], [1, 128]]),
                                apx(psC[:], base, [[129, nacc], [1, 128]]),
                                apx(srec3[:], 3 * b, [[1, nacc], [0, 128]]),
                                ALU.mult),
                            wait=[("v", v_rc)], inc=[("v", 1)])
                    return emit

                def tr_piece():
                    psc_free = fin_cell["psc_free"]
                    for a in range(8):
                        w_ = []
                        if a == 0:
                            w_ = [("v", psc_free), ("v", fin_cell["v"])]
                        P.op("tensor", lambda e, a=a: e.transpose(
                            pTC[:, a * 128:(a + 1) * 128],
                            sctn3[:, (a // 3) * 384 + (a % 3) * 128:
                                  (a // 3) * 384 + (a % 3) * 128 + 128],
                            sib[:]),
                            wait=w_, inc=[("t", 1)])
                    st["t_prev"] = P.val("t")

                def copy_piece():
                    fin_cell["v"] = P.op(
                        "vector", lambda e, hp=hp, qc=qc: e.tensor_copy(
                            apx(sCT[:], (2 * hp) * LQ + qc * 512,
                                [[LQ, 2], [128, 4], [1, 128]]),
                            pTC[:, :1024]),
                        wait=[("t", st["t_prev"])], inc=[("v", 1)])
                return [rcn_piece(b) for b in range(3)] + [tr_piece, copy_piece]

            carry = None
            for ri, (qc, hp) in enumerate(rounds):
                if carry is None:
                    RS = round_setup(qc, hp)
                    t_round0 = P.val("t")
                    s_round0 = P.val("s")
                    for ch in range(min(2, RS["nch"])):
                        plan_loads(RS, ch, [("t", t_round0)], [("t", t_round0)])
                    start_kt = 0
                    fin_prev = None
                else:
                    RS, fin_prev = carry
                nkt, nch = RS["nkt"], RS["nch"]
                for kt in range(0 if carry is None else 2, nkt):
                    if carry is None and kt == 0:
                        sc0 = [("v", v_qt_done), ("s", s_round0)]
                    elif kt >= 2:
                        sc0 = [("s", RS["h_tanh"][kt - 2])]
                    else:
                        sc0 = []
                    h_scj = emit_scores(RS, kt, sc0)
                    if kt - 3 in RS["pend_ctx"]:
                        RS["pend_ctx"].pop(kt - 3)()
                    if kt % 8 == 2 and kt > 2 and (kt - 2) // 8 + 1 < nch:
                        plan_loads(RS, (kt - 2) // 8 + 1,
                                   [("t", RS["h_sc"][kt - 1])],
                                   [("t", RS["h_ctx"][kt - 3])])
                    h_stt = emit_stt(RS, kt, h_scj,
                                     [("ld1", ld1_cb)] if (ri == 0 and kt == 0)
                                     else [])
                    extra = []
                    if ri == 0 and kt == 0:
                        extra.append(("s", s_tb))
                    if kt >= 2:
                        extra.append(("s", RS["h_exp"][kt - 2]))
                    emit_tanh(RS, kt, h_stt, extra)
                    if RS["pending_exp"] is not None:
                        RS["h_exp"][kt - 1] = RS["pending_exp"]()
                        RS["pend_ctx"][kt - 1] = make_ctx(RS, kt - 1,
                                                          RS["h_exp"][kt - 1])
                    RS["pending_exp"] = mk_exp(RS, kt)
                    if fin_prev is not None and 2 <= kt <= 3:
                        for p_ in fin_prev[2 * kt - 3:2 * kt - 1]:
                            p_()
                RS["h_exp"][nkt - 1] = RS["pending_exp"]()

                if ri + 1 < len(rounds):
                    RSn = round_setup(*rounds[ri + 1])
                    ch_e = nch - 1 if (nch - 1) % 2 == 0 else nch - 2
                    kt_e = 8 * ch_e + 7
                    plan_loads(RSn, 0, [("t", RS["h_sc"][kt_e])],
                               [("t", RS["h_ctx"][kt_e])])
                    for kt in (0, 1):
                        h_scj = emit_scores(
                            RSn, kt, [("s", RS["h_tanh"][nkt - 2 + kt])])
                        h_stt = emit_stt(RSn, kt, h_scj, [])
                        emit_tanh(RSn, kt, h_stt,
                                  [("s", RS["h_exp"][nkt - 2 + kt])])
                    RS["pend_ctx"][nkt - 1] = make_ctx(RS, nkt - 1,
                                                       RS["h_exp"][nkt - 1])
                    for k_ in sorted(RS["pend_ctx"]):
                        RS["pend_ctx"].pop(k_)()
                    ch_o = nch - 1 if (nch - 1) % 2 == 1 else nch - 2
                    kt_o = 8 * ch_o + 7
                    plan_loads(RSn, 1, [("t", RS["h_sc"][kt_o])],
                               [("t", RS["h_ctx"][kt_o])])
                    RSn["pbf_war"] = {
                        m: RS["h_ctx"][nkt - 1 - ((nkt - 1 - m) % 3)]
                        for m in range(3)}
                    RSn["h_exp"][0] = mk_exp(RSn, 0)()
                    RSn["pend_ctx"][0] = make_ctx(RSn, 0, RSn["h_exp"][0])
                    pieces = make_fin_pieces(RS["h_ctx"][nkt - 1], hp, qc)
                    pieces[0]()
                    RSn["pending_exp"] = mk_exp(RSn, 1)
                    carry = (RSn, pieces)
                else:
                    RS["pend_ctx"][nkt - 1] = make_ctx(RS, nkt - 1,
                                                       RS["h_exp"][nkt - 1])
                    for k_ in sorted(RS["pend_ctx"]):
                        RS["pend_ctx"].pop(k_)()
                    for p_ in make_fin_pieces(RS["h_ctx"][nkt - 1], hp, qc):
                        p_()
                    carry = None
            v_b_done = P.val("v")
            t_b_done = P.val("t")
            s_b_done = P.val("s")
            if cfg.get("dbg"):
                P.op("sync", lambda e: e.dma_start(out=dbg_ct, in_=sCT[:]),
                     wait=[("v", v_b_done)], inc=[("st", 16)])

        est_qbL.close()

        emit_phase_barrier()
        with contextlib.ExitStack() as pc1:
            pc1a = pc1.enter_context(contextlib.ExitStack())
            xqb2 = [pc1a.enter_context(nc.sbuf_tensor(f"xqb2{i}", [128, D], F32)) for i in range(1)]
            sbo = pc1a.enter_context(nc.sbuf_tensor("sbo", [128, D], BF16))
            t3t = [pc1a.enter_context(nc.sbuf_tensor(f"t3t{i}", [128, D], F32)) for i in range(2)]
            psOf = [pc1a.enter_context(nc.psum_tensor(f"psOf{i}", [128, D], F32)) for i in range(2)]

            P.op("gpsimd", lambda e: e.dma_start(
                out=sbo[:], in_=dbo.partition_broadcast(128)), inc=[("gld", 16)])
            gld_c = P.val("gld")

            h_omm, h_t3, h_x2 = {}, {}, {}
            for ot in range(NOT):
                op_ = ot % 2
                sx = 0
                xs = "xo0"
                w_ = [("g", h_x2[ot - 1])] if ot >= 1 else []
                P.op("sync", lambda e, ot=ot, sx=sx: e.dma_start(
                    out=xqb2[sx][:], in_=dxq[ot * 128:(ot + 1) * 128, :]),
                    wait=w_, inc=[(xs, 16)])
                xo_tot = P.val(xs)
                for nc4 in range(NO4):
                    for hh in range(H):
                        w_ = []
                        if nc4 == 0 and hh == 0:
                            w_ = [("v", v_b_done), ("gld", gld_c)]
                            if ot >= 2:
                                w_.append(("v", h_t3[ot - 2]))
                        P.op("tensor", lambda e, hh=hh, ot=ot, nc4=nc4, op_=op_: e.matmul(
                            psOf[op_][:, nc4 * 512:(nc4 + 1) * 512],
                            sCT[:, hh * LQ + ot * 128:hh * LQ + (ot + 1) * 128],
                            sWoB[:, hh * D + nc4 * 512:hh * D + (nc4 + 1) * 512],
                            start=(hh == 0), stop=(hh == H - 1)),
                            wait=w_, inc=[("t", 1)])
                h_omm[ot] = P.val("t")
                v_bo = P.op("vector", lambda e, op_=op_: e.scalar_tensor_tensor(
                    psOf[op_][:], sbo[:], 1.0, psOf[op_][:], ALU.mult, ALU.add),
                    wait=[("t", h_omm[ot]), ("gld", gld_c)], inc=[("v", 1)])
                s_r, lk = emit_ln_stats(
                    lambda i, op_=op_: psOf[op_][:, i * 512:(i + 1) * 512],
                    first_waits=[("v", v_bo)])
                h_t3[ot] = P.op("vector", lambda e, op_=op_, lk=lk: e.tensor_scalar(
                    t3t[op_][:], psOf[op_][:], smv[lk][:, 0:1], srs[lk][:],
                    ALU.subtract, ALU.mult),
                    wait=[("s", s_r)] + ([("g", h_x2[ot - 2])] if ot >= 2 else []),
                    inc=[("v", 1)])
                ln_done(lk, [("v", h_t3[ot])])
                h_x2[ot] = P.op("gpsimd", lambda e, ot=ot, sx=sx, op_=op_: e.tensor_add(
                    x2[:, ot * D:(ot + 1) * D], t3t[op_][:], xqb2[sx][:]),
                    wait=[("v", h_t3[ot]), (xs, xo_tot)] +
                         ([("g", h_x2[ot - 1])] if ot >= 1 else []), inc=[("g", 1)])
            g_x2_done = P.val("g")
            if cfg.get("dbg"):
                P.op("sync", lambda e: e.dma_start(out=dbg_x2, in_=x2[:]),
                     wait=[("g", g_x2_done)], inc=[("st", 16)])

            pc1a.close()
            estL2.close()
            est_qb.close()

            emit_phase_barrier()
            with contextlib.ExitStack() as pc2:
                sAT = pc2.enter_context(nc.sbuf_tensor("sAT", [128, ND * 512], BF16))
                sBT = pc2.enter_context(nc.sbuf_tensor("sBT", [128, NP * D], BF16))
                swrt = pc2.enter_context(nc.sbuf_tensor("swrt", [128, ND * E], BF16))
                for dst, srcap in (
                        (sAT, dap(dat, 0, [[E * R, 128], [128 * E * R, ND], [1, 512]])),
                        (swrt, dap(dwrt, 0, [[E, 128], [128 * E, ND], [1, E]])),
                        (sBT, dap(dbt, 0, [[D, 128], [128 * D, NP], [1, D]]))):
                    P.op("gpsimd", lambda e, d=dst, s=srcap: e.dma_start(out=d[:], in_=s),
                         inc=[("gld", 16)])
                gld_c2 = P.val("gld")
                sbr = pc2.enter_context(nc.sbuf_tensor("sbr", [128, E], F32))
                sOH = pc2.enter_context(nc.sbuf_tensor("sOH", [8, 512], F32))
                x2ln = [pc2.enter_context(nc.sbuf_tensor(f"x2ln{i}", [128, D], BF16)) for i in range(2)]
                x2T = pc2.enter_context(nc.sbuf_tensor("x2T", [128, ND * 512], BF16))
                slg = [pc2.enter_context(nc.sbuf_tensor(f"slg{i}", [128, E], F32)) for i in range(2)]
                smx = [pc2.enter_context(nc.sbuf_tensor(f"smx{i}", [128, 1], F32)) for i in range(2)]
                snm = [pc2.enter_context(nc.sbuf_tensor(f"snm{i}", [128, 1], F32)) for i in range(2)]
                sex = [pc2.enter_context(nc.sbuf_tensor(f"sex{i}", [128, E], F32)) for i in range(2)]
                ssm = [pc2.enter_context(nc.sbuf_tensor(f"ssm{i}", [128, 1], F32)) for i in range(2)]
                src2 = [pc2.enter_context(nc.sbuf_tensor(f"src2{i}", [128, 1], F32)) for i in range(2)]
                smk = [pc2.enter_context(nc.sbuf_tensor(f"smk{i}", [128, E], F32)) for i in range(2)]
                sw_ = [pc2.enter_context(nc.sbuf_tensor(f"sw_{i}", [128, E], F32)) for i in range(4)]
                sif2 = pc2.enter_context(nc.sbuf_tensor("sif2", [128, 128], F32))
                swT = pc2.enter_context(nc.sbuf_tensor("swT", [8, 512], F32))
                wbb = pc2.enter_context(nc.sbuf_tensor("wbb", [128, 4 * 512], BF16))
                y2b = [pc2.enter_context(nc.sbuf_tensor(f"y2b{i}", [128, D], F32)) for i in range(2)]
                hpb = pc2.enter_context(nc.sbuf_tensor("hpb", [128, NP * 512], BF16))
                t4t = pc2.enter_context(nc.sbuf_tensor("t4t", [128, D], F32))
                outb = [pc2.enter_context(nc.sbuf_tensor(f"outb{i}", [128, D], F32)) for i in range(2)]

                P.op("sync", lambda e: e.dma_start(
                    out=sbr[:], in_=dbr.partition_broadcast(128)), inc=[("ld1", 16)])
                P.op("sync", lambda e: e.dma_start(out=sOH[:, :], in_=doh),
                     inc=[("ld1", 16)])
                ld1_tot = P.val("ld1")
                gi1 = P.op("gpsimd", lambda e: e.iota(
                    sif2[:], pattern=[[-1, 128]], base=0, channel_multiplier=1,
                    allow_small_or_imprecise_dtypes=True),
                    wait=[("g", g_x2_done)], inc=[("g", 1)])
                v_if = P.op("vector", lambda e: e.tensor_scalar(
                    sif2[:], sif2[:], 0.0, None, ALU.is_equal),
                    wait=[("g", gi1)], inc=[("v", 1)])

                h_oev2, st_prev = {}, {}
                h_t4, h_yev_d = {}, {}
                v_hp_qg_prev = 0
                yev_qg_prev = 0
                t_tr_prev, v_e3_prev = {}, {}
                for qg in range(max(1, LQ // 512)):
                    with contextlib.ExitStack() as pcq:
                        pca2 = pcq.enter_context(contextlib.ExitStack())
                        psR = [pca2.enter_context(nc.psum_tensor(
                            "psRq%d_%d" % (qg, i), [128, E], F32)) for i in range(2)]
                        pWT = pca2.enter_context(nc.psum_tensor("pWTq%d" % qg, [8, 128], F32))
                        pwb = pca2.enter_context(nc.psum_tensor("pwbq%d" % qg, [128, 512], F32))
                        pcaT = pcq.enter_context(contextlib.ExitStack())
                        pT3 = [pcaT.enter_context(nc.psum_tensor(
                            "pT3q%d_%d" % (qg, i), [128, ND * 128], BF16)) for i in range(2)]

                        v_e3_d, g_ap2_d = {}, {}
                        t_tr_d = {}
                        for tt in range(4):
                            ot = qg * 4 + tt
                            s_r, lk = emit_ln_stats(
                                lambda i, ot=ot: x2[:, ot * D + i * 512:ot * D + (i + 1) * 512],
                                first_waits=[("g", g_x2_done)])
                            w_ = [("s", s_r)]
                            if tt >= 2:
                                w_.append(("t", t_tr_d[tt - 2]))
                            elif t_tr_prev.get(tt + 2):
                                w_.append(("t", t_tr_prev[tt + 2]))
                            g_ap2 = P.op("vector", lambda e, ot=ot, lk=lk, tt=tt: e.tensor_scalar(
                                x2ln[tt % 2][:], x2[:, ot * D:(ot + 1) * D], smv[lk][:, 0:1],
                                srs[lk][:], ALU.subtract, ALU.mult),
                                wait=w_, inc=[("v", 1)])
                            ln_done(lk, [("v", g_ap2)])
                            g_ap2_d[tt] = g_ap2
                            for d in range(ND):
                                w_ = [("v", g_ap2)] if d == 0 else []
                                if d == 0 and tt >= 2:
                                    w_.append(("v", v_e3_d[tt - 2]))
                                if d == 0 and tt < 2:
                                    if v_e3_prev.get(tt + 2):
                                        w_.append(("v", v_e3_prev[tt + 2]))
                                    if v_hp_qg_prev:
                                        w_.append(("v", v_hp_qg_prev))
                                P.op("tensor", lambda e, d=d, tt=tt: e.transpose(
                                    pT3[tt % 2][:, d * 128:(d + 1) * 128],
                                    x2ln[tt % 2][:, d * 128:(d + 1) * 128], sib[:]),
                                    wait=w_, inc=[("t", 1)])
                            t_tr_d[tt] = P.val("t")
                            v_e3_d[tt] = P.op("vector", lambda e, tt=tt: e.tensor_copy(
                                apx(x2T[:], tt * 128, [[512, ND], [1, 128]]), pT3[tt % 2][:]),
                                wait=[("t", t_tr_d[tt])], inc=[("v", 1)])
                        t_tr_prev = t_tr_d
                        v_e3_prev = v_e3_d
                        pcaT.close()
                        pcaH = pcq.enter_context(contextlib.ExitStack())
                        psH = [pcaH.enter_context(nc.psum_tensor(
                            "psH%d_%d" % (qg, i), [128, 512], F32)) for i in range(4)]

                        v_lg_d, v_w1_d, t_h_d = {}, {}, {}
                        for tt in range(4):
                            for d in range(ND):
                                w_ = []
                                if d == 0:
                                    w_ = [("v", v_e3_d[tt])]
                                    if tt == 0:
                                        w_.append(("gld", gld_c2))
                                        if yev_qg_prev:
                                            w_.append(("v", yev_qg_prev))
                                    if tt >= 2:
                                        w_.append(("v", v_lg_d[tt - 2]))
                                P.op("tensor", lambda e, d=d, tt=tt: e.matmul(
                                    psR[tt % 2][:],
                                    x2T[:, d * 512 + tt * 128:d * 512 + (tt + 1) * 128],
                                    swrt[:, d * E:(d + 1) * E],
                                    start=(d == 0), stop=(d == ND - 1)),
                                    wait=w_, inc=[("t", 1)])
                            t_r = P.val("t")
                            v_lg = P.op("vector", lambda e, tt=tt: e.tensor_add(
                                slg[tt % 2][:], psR[tt % 2][:], sbr[:, :E]),
                                wait=[("t", t_r), ("ld1", ld1_tot)], inc=[("v", 1)])
                            v_lg_d[tt] = v_lg
                            v_mx = P.op("vector", lambda e, tt=tt: e.reduce_max(
                                smx[tt % 2][:], slg[tt % 2][:], AX), wait=[("v", v_lg)], inc=[("v", 1)])
                            v_nm = P.op("vector", lambda e, tt=tt: e.tensor_scalar_mul(
                                snm[tt % 2][:], smx[tt % 2][:], -1.0), wait=[("v", v_mx)], inc=[("v", 1)])
                            s_ex = P.op("scalar", lambda e, tt=tt: e.activation(
                                sex[tt % 2][:], slg[tt % 2][:], ACT.Exp, bias=snm[tt % 2][:], accum_out=ssm[tt % 2][:]),
                                wait=[("v", v_nm), ("s", P.val("s"))], inc=[("s", 1)])
                            v_rc = P.op("vector", lambda e, tt=tt: e.reciprocal(
                                src2[tt % 2][:], ssm[tt % 2][:]), wait=[("s", s_ex)], inc=[("v", 1)])
                            v_mk = P.op("vector", lambda e, tt=tt: e.tensor_scalar(
                                smk[tt % 2][:], slg[tt % 2][:], smx[tt % 2][:], None, ALU.is_equal),
                                wait=[("v", v_rc)], inc=[("v", 1)])
                            v_w0 = P.op("vector", lambda e, tt=tt: e.tensor_mul(
                                sw_[tt][:], smk[tt % 2][:], sex[tt % 2][:]), wait=[("v", v_mk)], inc=[("v", 1)])
                            v_w1_d[tt] = P.op("vector", lambda e, tt=tt: e.tensor_scalar(
                                sw_[tt][:], sw_[tt][:], src2[tt % 2][:], inv_rank, ALU.mult, ALU.mult),
                                wait=[("v", v_w0)], inc=[("v", 1)])
                            if tt == 1 or tt == 3:
                                for pair in (tt - 1, tt):
                                    for d in range(ND):
                                        w_ = []
                                        if d == 0 and pair == 0:
                                            w_ = [("v", v_e3_d[3])]
                                        P.op("tensor", lambda e, d=d, pair=pair: e.matmul(
                                            psH[pair][:],
                                            sAT[:, d * 512 + pair * 128:d * 512 + (pair + 1) * 128],
                                            x2T[:, d * 512:(d + 1) * 512],
                                            start=(d == 0), stop=(d == ND - 1)),
                                            wait=w_, inc=[("t", 1)])
                                    t_h_d[pair] = P.val("t")

                        v_wT_prev = 0
                        for tt in range(4):
                            w_ = [("v", v_w1_d[tt]), ("v", v_if)]
                            if v_wT_prev:
                                w_.append(("v", v_wT_prev))
                            t_wt = P.op("tensor", lambda e, tt=tt: e.transpose(
                                pWT[:], sw_[tt][:, :E], sif2[:]),
                                wait=w_, inc=[("t", 1)])
                            v_wT_prev = P.op("vector", lambda e, tt=tt: e.tensor_copy(
                                swT[:, tt * 128:(tt + 1) * 128], pWT[:]),
                                wait=[("t", t_wt)], inc=[("v", 1)])
                        v_swt = P.val("v")
                        v_wb_prev = 0
                        v_wb_d = {}
                        for pair in range(NP):
                            w_ = [("v", v_swt)]
                            if v_wb_prev:
                                w_.append(("v", v_wb_prev))
                            t_wb = P.op("tensor", lambda e, pair=pair: e.matmul(
                                pwb[:], sOH[:, pair * 128:(pair + 1) * 128],
                                swT[:, :512], start=True, stop=True),
                                wait=w_, inc=[("t", 1)])
                            v_wb_prev = P.op("vector", lambda e, pair=pair: e.tensor_copy(
                                wbb[:, pair * 512:(pair + 1) * 512], pwb[:]),
                                wait=[("t", t_wb)], inc=[("v", 1)])
                            v_wb_d[pair] = v_wb_prev
                        v_wbb = P.val("v")

                        for pair in range(NP):
                            P.op("vector", lambda e, pair=pair: e.tensor_mul(
                                hpb[:, pair * 512:(pair + 1) * 512], psH[pair][:],
                                wbb[:, pair * 512:(pair + 1) * 512]),
                                wait=[("t", t_h_d[pair]), ("v", v_wb_d[pair])], inc=[("v", 1)])
                        v_hp = P.val("v")
                        v_hp_qg_prev = v_hp
                        pcaH.close()
                        pca2.close()
                        psY = pcq.enter_context(nc.psum_tensor("psY%d" % qg, [128, D], F32))

                        for ts in range(4):
                            gi = qg * 4 + ts
                            ot = qg * 4 + ts
                            ob = gi % 2
                            for oc in range(NO4):
                                for pair in range(NP):
                                    w_ = []
                                    if oc == 0 and pair == 0:
                                        w_ = [("v", v_hp)]
                                        if ts > 0:
                                            w_.append(("v", h_yev_d[gi - 1]))
                                    P.op("tensor", lambda e, oc=oc, pair=pair, ts=ts: e.matmul(
                                        psY[:, oc * 512:(oc + 1) * 512],
                                        hpb[:, pair * 512 + ts * 128:pair * 512 + ts * 128 + 128],
                                        sBT[:, pair * D + oc * 512:pair * D + (oc + 1) * 512],
                                        start=(pair == 0), stop=(pair == NP - 1)),
                                        wait=w_, inc=[("t", 1)])
                            t_y = P.val("t")
                            yb = y2b[gi % 2]
                            h_yev = P.op("vector", lambda e, yb=yb: e.tensor_copy(
                                yb[:], psY[:]), wait=[("t", t_y)], inc=[("v", 1)])
                            h_yev_d[gi] = h_yev
                            s_r, lk = emit_ln_stats(
                                lambda i, yb=yb: yb[:, i * 512:(i + 1) * 512],
                                first_waits=[("v", h_yev)])
                            h_t4[gi] = P.op("vector", lambda e, lk=lk, yb=yb: e.tensor_scalar(
                                t4t[:], yb[:], smv[lk][:, 0:1], srs[lk][:],
                                ALU.subtract, ALU.mult), wait=[("s", s_r)], inc=[("v", 1)])
                            ln_done(lk, [("v", h_t4[gi])])
                            w_ = [("v", h_t4[gi]), ("g", g_x2_done)]
                            if gi >= 2:
                                w_.append(("st", P.val("st")))
                            v_ob = P.op("vector", lambda e, ot=ot, ob=ob: e.tensor_add(
                                outb[ob][:], t4t[:], x2[:, ot * D:(ot + 1) * D]),
                                wait=w_, inc=[("v", 1)])
                            st_prev[gi] = P.op("sync", lambda e, ot=ot, ob=ob: e.dma_start(
                                out=dout[ot * 128:(ot + 1) * 128, :], in_=outb[ob][:]),
                                wait=[("v", v_ob)], inc=[("st", 16)])
                        yev_qg_prev = h_yev_d[qg * 4 + 3]

                st_tot = P.val("st")
                P.op("sync", lambda e: None, wait=[("st", st_tot)])

        P.emit(block, sems)
    return nc


def _prep_inputs(cfg, inputs):
    T, D, H, G, E, R = cfg["T"], cfg["D"], cfg["H"], cfg["G"], cfg["E"], cfg["R"]
    DQ = 128
    NP = E // 2
    x = np.ascontiguousarray(np.asarray(inputs["L0_attn_hidden_states"],
                                        dtype=np.float32).reshape(T, D))
    perm = rope_perm(DQ)
    wq = np.asarray(inputs["L0_attn_Wq"], dtype=np.float32)
    wq = np.ascontiguousarray(wq.reshape(D, H, DQ)[:, :, perm].reshape(D, H * DQ))
    wk = np.asarray(inputs["L0_attn_Wk"], dtype=np.float32)
    wk = np.ascontiguousarray(wk.reshape(D, G, DQ)[:, :, perm].reshape(D, G * DQ))
    wv = np.ascontiguousarray(np.asarray(inputs["L0_attn_Wv"], dtype=np.float32))
    wkv = np.ascontiguousarray(np.concatenate([wk, wv], axis=1))
    wo = np.ascontiguousarray(np.asarray(inputs["L0_attn_Wo"], dtype=np.float32))
    bo = np.ascontiguousarray(np.asarray(inputs["L0_attn_bo"],
                                         dtype=np.float32).reshape(1, D))
    A = np.asarray(inputs["L0_ffn_A"], dtype=np.float32)
    at = np.ascontiguousarray(A.transpose(2, 0, 1).reshape(D, E * R))
    B = np.asarray(inputs["L0_ffn_B"], dtype=np.float32)
    bt = np.ascontiguousarray(B.transpose(0, 2, 1).reshape(NP, 2 * R, D))
    wrt = np.ascontiguousarray(np.asarray(inputs["L0_ffn_Wr"], dtype=np.float32).T)
    br = np.ascontiguousarray(np.asarray(inputs["L0_ffn_br"],
                                         dtype=np.float32).reshape(1, E))
    cs = make_cs(T, DQ)
    identb = np.eye(128, dtype=np.float32).astype(ml_dtypes.bfloat16)
    oh = make_oh(E)

    TB = T // NC if cfg.get("ag") else T
    blocks = np.arange(T // 16)
    in_maps, own_rows_all = [], []
    for c in range(NC):
        own_blocks = blocks[blocks % NC == c]
        rows = (own_blocks[:, None] * 16 + np.arange(16)[None, :]).reshape(-1)
        own_rows_all.append(rows)
        b0 = c * TB if cfg.get("ag") else 0
        in_maps.append(dict(
            xb=np.ascontiguousarray(x[b0:b0 + TB]),
            xq=np.ascontiguousarray(x[rows]), wq=wq, wkv=wkv, wo=wo,
            bo1=bo, at=at, bt=bt, wrt=wrt, br1=br, cb=make_cb(c),
            csk=np.ascontiguousarray(cs[b0:b0 + TB]),
            csq=np.ascontiguousarray(cs[rows]), oh=oh, identb=identb))
    return in_maps, own_rows_all


def kernel(**inputs):
    cfg = dict(FULL)
    nc_prog = build_nc(cfg)
    in_maps, own_rows = _prep_inputs(cfg, inputs)
    res = run_bass_kernel_spmd(nc_prog, in_maps, core_ids=list(range(NC)))
    T, D = cfg["T"], cfg["D"]
    out = np.empty((T, D), dtype=np.float32)
    for c in range(NC):
        out[own_rows[c]] = np.asarray(res.results[c]["out"], dtype=np.float32)
    return out.reshape(1, T, D)


# version 21
# speedup vs baseline: 1.2271x; 1.0007x over previous
import math
import contextlib
import numpy as np
import ml_dtypes
import concourse.bass as bass
import concourse.mybir as mybir
from concourse.bass_utils import run_bass_kernel_spmd

F32 = mybir.dt.float32
BF16 = mybir.dt.bfloat16
AX = mybir.AxisListType.X
ALU = mybir.AluOpType
ACT = mybir.ActivationFunctionType

NC = 8
BIGNEG = -1.0e9

FULL = dict(T=8192, D=2048, H=16, G=4, E=8, R=64, ag=True)


class Prog:

    def __init__(self):
        self.ops = []
        self.counts = {}

    def op(self, eng, fn, wait=(), inc=()):
        wait = [(s, int(v)) for (s, v) in wait if v > 0]
        inc = [(s, int(a)) for (s, a) in inc]
        self.ops.append((eng, fn, wait, inc))
        for s, a in inc:
            self.counts[s] = self.counts.get(s, 0) + a
        if len(inc) == 1:
            return self.counts[inc[0][0]]
        return None

    def val(self, sem):
        return self.counts.get(sem, 0)

    def emit(self, block, sems):
        for eng_name in ("sync", "tensor", "vector", "scalar", "gpsimd"):
            eng_ops = [o for o in self.ops if o[0] == eng_name]
            if not eng_ops:
                continue

            def make(ops_):
                def body(e):
                    for (_, fn, wait, inc) in ops_:
                        for s, v in wait:
                            e.wait_ge(sems[s], v)
                        instr = fn(e)
                        for s, a in inc:
                            instr = instr.then_inc(sems[s], a)
                return body
            getattr(block, eng_name)(make(eng_ops))


def apx(base_ap, offset_elems, dims):
    return bass.AP(base_ap.tensor, base_ap.offset + offset_elems,
                   [list(base_ap.ap[0])] + [list(d) for d in dims])


def dap(dram_ap, offset_elems, dims):
    return bass.AP(dram_ap.tensor, dram_ap.offset + offset_elems,
                   [list(d) for d in dims])


def rope_perm(dq=128):
    return np.concatenate([np.arange(0, dq, 2), np.arange(1, dq, 2)])


def make_cs(T, dq=128):
    inv = 1.0 / (10000.0 ** (np.arange(0, dq, 2, dtype=np.float64) / dq))
    f = np.arange(T, dtype=np.float64)[:, None] * inv[None, :]
    return np.concatenate([np.cos(f), np.sin(f)], axis=1).astype(np.float32)


def make_cb(core):
    W = 1008 + 512
    p = np.arange(128)[:, None]
    x = np.arange(W)[None, :]
    rel = x - 496
    ramp = -(128.0 * np.floor(rel / 16.0) + np.mod(rel, 16))
    cb = (ramp + p - 16.0 * core).astype(np.float32)
    cb[:, :496] = BIGNEG
    bnd = np.broadcast_to((x >= 496) & (x < 512), cb.shape)
    cb[np.asarray(bnd & (p > 16 * core + rel))] = BIGNEG
    return np.ascontiguousarray(cb)


def make_oh(E):
    oh = np.zeros((E, 4 * 128), dtype=np.float32)
    for pair in range(4):
        for m in range(128):
            e = 2 * pair + (1 if m >= 64 else 0)
            if e < E:
                oh[e, pair * 128 + m] = 1.0
    return oh


def build_nc(cfg):
    T, D, H, G, E, R = cfg["T"], cfg["D"], cfg["H"], cfg["G"], cfg["E"], cfg["R"]
    DQ = 128
    ND = D // 128
    NT = T // 128
    LQ = T // NC
    NOT = LQ // 128
    NQC = max(1, LQ // 512)
    NP = E // 2
    HD = H * DQ
    NH4 = HD // 512
    KVW = 2 * G * DQ
    GD = G * DQ
    NO4 = D // 512
    nbn = D // 512
    PPG = max(1, (H // G) // 2)
    NHP = H // 2
    slopes = [float(np.float32(2.0 ** (-(h + 1) * 8.0 / H))) for h in range(H)]
    c_tanh = float(np.float32(1.0 / (30.0 * math.sqrt(DQ))))
    inv_rank = float(np.float32(1.0 / R))

    nc = bass.Bass("TRN2", target_bir_lowering=False, debug=False)

    AG = bool(cfg.get("ag"))
    TB = T // NC if AG else T
    NTB = TB // 128
    SEC = 2 * G * DQ * TB
    VOFF = G * DQ * TB
    dxb = nc.dram_tensor("xb", [TB, D], F32, kind="ExternalInput").ap()
    dxq = nc.dram_tensor("xq", [LQ, D], F32, kind="ExternalInput").ap()
    dwq = nc.dram_tensor("wq", [D, HD], F32, kind="ExternalInput").ap()
    dwkv = nc.dram_tensor("wkv", [D, KVW], F32, kind="ExternalInput").ap()
    dwo = nc.dram_tensor("wo", [HD, D], F32, kind="ExternalInput").ap()
    dbo = nc.dram_tensor("bo1", [1, D], F32, kind="ExternalInput").ap()
    dat = nc.dram_tensor("at", [D, E * R], F32, kind="ExternalInput").ap()
    dbt = nc.dram_tensor("bt", [NP, 2 * R, D], F32, kind="ExternalInput").ap()
    dwrt = nc.dram_tensor("wrt", [D, E], F32, kind="ExternalInput").ap()
    dbr = nc.dram_tensor("br1", [1, E], F32, kind="ExternalInput").ap()
    dcb = nc.dram_tensor("cb", [128, 1520], F32, kind="ExternalInput").ap()
    dcsk = nc.dram_tensor("csk", [TB, DQ], F32, kind="ExternalInput").ap()
    dcsq = nc.dram_tensor("csq", [LQ, DQ], F32, kind="ExternalInput").ap()
    doh = nc.dram_tensor("oh", [E, 4 * 128], F32, kind="ExternalInput").ap()
    dib = nc.dram_tensor("identb", [128, 128], BF16, kind="ExternalInput").ap()
    from concourse.replica_groups import maybe_share_collective_output_space
    dkvo = nc.dram_tensor("kvown", [SEC], BF16, kind="Internal").ap()
    if AG:
        _sp = maybe_share_collective_output_space("AllGather",
                                                  [list(range(NC))])
        dkvag = nc.dram_tensor("kvag", [NC * SEC], BF16, kind="Internal",
                               addr_space=_sp).ap()
    else:
        dkvag = dkvo
    dout = nc.dram_tensor("out", [LQ, D], F32, kind="ExternalOutput").ap()
    if cfg.get("dbg"):
        dbg_qt = nc.dram_tensor("dbg_qt", [128, H * LQ], BF16, kind="ExternalOutput").ap()
        dbg_ct = nc.dram_tensor("dbg_ct", [128, H * LQ], BF16, kind="ExternalOutput").ap()
        dbg_x2 = nc.dram_tensor("dbg_x2", [128, NOT * D], BF16, kind="ExternalOutput").ap()
        dbg_ag = nc.dram_tensor("dbg_ag", [NC * SEC if AG else SEC], BF16, kind="ExternalOutput").ap()
        dbg_x1 = nc.dram_tensor("dbg_x1", [NTB * 128, D], BF16, kind="ExternalOutput").ap()
        dbg_y1 = nc.dram_tensor("dbg_y1", [128, NOT * D], BF16, kind="ExternalOutput").ap()

    est = contextlib.ExitStack()
    with est:
        sems = {}
        for s in ("ld1", "gld", "xk0", "xk1", "xk2", "cs0", "cs1", "cs2", "cs3", "cs4", "ksw", "vsw",
                  "qx0", "qx1", "wl0", "wl1", "kld0", "kld1", "vld0", "vld1",
                  "xo0", "xo1", "st", "v", "t", "s", "g", "agc"):
            sems[s] = est.enter_context(nc.semaphore(s))
        block = est.enter_context(nc.Block())

        sib = est.enter_context(nc.sbuf_tensor("sib", [128, 128], BF16))
        seps = est.enter_context(nc.sbuf_tensor("seps", [128, 1], F32))
        smv = [est.enter_context(nc.sbuf_tensor(f"smv{i}", [128, 2], F32)) for i in range(2)]
        srs = [est.enter_context(nc.sbuf_tensor(f"srs{i}", [128, 1], F32)) for i in range(2)]
        sst = [est.enter_context(nc.sbuf_tensor(f"sst{i}", [128, 24], F32)) for i in range(2)]
        x2 = est.enter_context(nc.sbuf_tensor("x2", [128, NOT * D], BF16))

        est_qb = contextlib.ExitStack()
        sQT = est_qb.enter_context(nc.sbuf_tensor("sQT", [128, H * LQ], BF16))
        sCB = est_qb.enter_context(nc.sbuf_tensor("sCB", [128, 1520], F32))

        P = Prog()
        P.op("sync", lambda e: e.dma_start(out=sCB[:], in_=dcb), inc=[("ld1", 16)])
        P.op("gpsimd", lambda e: e.dma_start(out=sib[:], in_=dib), inc=[("gld", 16)])
        v_eps = P.op("vector", lambda e: e.memset(seps[:], 1e-5), inc=[("v", 1)])

        ln_guard = [{"s": 0, "v": 0, "other": []} for _ in range(2)]
        ln_slot = [0]

        def emit_phase_barrier(extra=()):
            w = [("v", P.val("v")), ("s", P.val("s")), ("t", P.val("t")),
                 ("g", P.val("g")), ("st", P.val("st"))] + list(extra)
            for eng_ in ("sync", "tensor", "vector", "scalar", "gpsimd"):
                P.op(eng_, lambda e: None, wait=w)

        def emit_ln_stats(src_fn, first_waits=()):
            k = ln_slot[0]
            ln_slot[0] ^= 1
            g = ln_guard[k]
            for i in range(nbn):
                w0 = list(first_waits) + [("v", g["v"])] if i == 0 else []
                P.op("vector",
                     lambda e, i=i, k=k: e.bn_stats(sst[k][:, i * 6:(i + 1) * 6], src_fn(i)),
                     wait=w0, inc=[("v", 1)])
            w_ = [("v", P.val("v")), ("s", g["s"])] + list(g["other"])
            v_bn = P.op("vector", lambda e, k=k: e.bn_aggr(smv[k][:], sst[k][:, :nbn * 6]),
                        wait=w_, inc=[("v", 1)])
            g["v"] = v_bn
            w_ = [("v", v_bn), ("v", v_eps)] + list(g["other"])
            s_l = P.op("scalar", lambda e, k=k: e.activation(
                srs[k][:], smv[k][:, 1:2], ACT.Ln, bias=seps[:]), wait=w_, inc=[("s", 1)])
            s_r = P.op("scalar", lambda e, k=k: e.activation(
                srs[k][:], srs[k][:], ACT.Exp, scale=-0.5), wait=[("s", s_l)], inc=[("s", 1)])
            g["s"] = s_r
            return s_r, k

        def ln_done(k, waits):
            ln_guard[k]["other"] = list(waits)

        estA2 = contextlib.ExitStack()
        sWqb0 = estA2.enter_context(nc.sbuf_tensor("sWqb0", [128, ND * 512], BF16))
        with contextlib.ExitStack() as pa:
            sWkv = pa.enter_context(nc.sbuf_tensor("sWkv", [128, ND * KVW], BF16))
            xbuf = [pa.enter_context(nc.sbuf_tensor(f"xb{i}", [128, D], F32)) for i in range(3)]
            x1b = [pa.enter_context(nc.sbuf_tensor(f"x1b{i}", [128, D], BF16)) for i in range(2)]
            x1T = [pa.enter_context(nc.sbuf_tensor(f"x1T{i}", [128, ND * 128], BF16)) for i in range(2)]
            kvb = [pa.enter_context(nc.sbuf_tensor(f"kvb{i}", [128, KVW], BF16)) for i in range(2)]
            krp = [pa.enter_context(nc.sbuf_tensor(f"krp{i}", [128, GD], BF16)) for i in range(2)]
            kts = [pa.enter_context(nc.sbuf_tensor(f"kts{i}", [128, GD], BF16)) for i in range(2)]
            csb = [pa.enter_context(nc.sbuf_tensor(f"csb{i}", [128, DQ], F32)) for i in range(5)]
            tmp12 = pa.enter_context(nc.sbuf_tensor("tmp12", [128, GD], F32))
            pT = [pa.enter_context(nc.psum_tensor(f"pT{i}", [128, ND * 128], BF16)) for i in range(2)]
            pK = [pa.enter_context(nc.psum_tensor(f"pK{i}", [128, GD], F32)) for i in range(1)]
            pV = [pa.enter_context(nc.psum_tensor(f"pV{i}", [128, GD], F32)) for i in range(1)]
            pKT = [pa.enter_context(nc.psum_tensor(f"pKT{i}", [128, GD], BF16)) for i in range(2)]

            for d in range(ND):
                P.op("gpsimd", lambda e, d=d: e.dma_start(
                    out=sWkv[:, d * KVW:(d + 1) * KVW],
                    in_=dwkv[d * 128:(d + 1) * 128, :]), inc=[("gld", 16)])
            gld_w = P.val("gld")
            P.op("gpsimd", lambda e: e.dma_start(
                out=sWqb0[:],
                in_=dap(dwq, 0, [[HD, 128], [128 * HD, ND], [1, 512]])),
                inc=[("wl0", 16)])

            h_apply, h_rope, h_evT, h_kv, h_kvev, h_ktr, h_evKT, h_tr = \
                {}, {}, {}, {}, {}, {}, {}, {}
            xk_tot_d, cs_tot_d = {}, {}

            def a1_load(t):
                xs = f"xk{t % 3}"
                w_ = [("v", h_apply[t - 3])] if t >= 3 else []
                P.op("sync", lambda e, t=t: e.dma_start(
                    out=xbuf[t % 3][:], in_=dxb[t * 128:(t + 1) * 128, :]),
                    wait=w_, inc=[(xs, 16)])
                xk_tot_d[t] = P.val(xs)
                cs_s = f"cs{t % 5}"
                w_ = [("g", h_rope[t - 5])] if t >= 5 else []
                P.op("sync", lambda e, t=t: e.dma_start(
                    out=csb[t % 5][:], in_=dcsk[t * 128:(t + 1) * 128, :]),
                    wait=w_, inc=[(cs_s, 16)])
                cs_tot_d[t] = P.val(cs_s)

            def a1_ln(t):
                s_r, lk = emit_ln_stats(
                    lambda i, t=t: xbuf[t % 3][:, i * 512:(i + 1) * 512],
                    first_waits=[(f"xk{t % 3}", xk_tot_d[t])])
                w_ = [("s", s_r)]
                if t >= 2:
                    w_.append(("t", h_tr[t - 2]))
                if cfg.get("dbg"):
                    w_.append(("st", P.val("st")))
                h_apply[t] = P.op("vector", lambda e, t=t, lk=lk: e.tensor_scalar(
                    x1b[t % 2][:], xbuf[t % 3][:], smv[lk][:, 0:1], srs[lk][:],
                    ALU.subtract, ALU.mult), wait=w_, inc=[("v", 1)])
                ln_done(lk, [("v", h_apply[t])])

            def a1_dbg(t):
                P.op("sync", lambda e, t=t: e.dma_start(
                    out=dbg_x1[t * 128:(t + 1) * 128, :], in_=x1b[t % 2][:]),
                    wait=[("v", h_apply[t])], inc=[("st", 16)])

            def a1_trans(t):
                for d in range(ND):
                    w_ = [("v", h_apply[t])] if d == 0 else []
                    if d == 0 and t >= 2:
                        w_.append(("v", h_evT[t - 2]))
                    if d == 0 and t == 0:
                        w_.append(("gld", gld_w))
                    P.op("tensor", lambda e, d=d, t=t: e.transpose(
                        pT[t % 2][:, d * 128:(d + 1) * 128],
                        x1b[t % 2][:, d * 128:(d + 1) * 128], sib[:]),
                        wait=w_, inc=[("t", 1)])
                h_tr[t] = P.val("t")

            def a1_evT(t):
                w_ = [("t", h_tr[t])]
                if t >= 2:
                    w_.append(("t", h_kv[t - 2]))
                h_evT[t] = P.op("vector", lambda e, t=t: e.tensor_copy(
                    x1T[t % 2][:], pT[t % 2][:]), wait=w_, inc=[("v", 1)])

            def a1_kv(t):
                for d in range(ND):
                    for half in range(2):
                        w_ = []
                        if d == 0 and half == 0:
                            w_.append(("v", h_evT[t]))
                            if t >= 1:
                                w_.append(("s", h_kvev[t - 1]))
                            if t == 0:
                                w_.append(("gld", gld_w))
                        P.op("tensor", lambda e, d=d, half=half, t=t: e.matmul(
                            (pK if half == 0 else pV)[0][:],
                            x1T[t % 2][:, d * 128:(d + 1) * 128],
                            sWkv[:, d * KVW + half * GD:d * KVW + (half + 1) * GD],
                            start=(d == 0), stop=(d == ND - 1)),
                            wait=w_, inc=[("t", 1)])
                h_kv[t] = P.val("t")

            def a1_copies(t):
                w_ = [("t", h_kv[t])]
                if t >= 2:
                    w_.append(("g", h_rope[t - 2]))
                    w_.append(("vsw", 16 * t))
                P.op("scalar", lambda e, t=t: e.copy(
                    kvb[t % 2][:, 0:GD], pK[0][:]), wait=w_, inc=[("s", 1)])
                h_kvev[t] = P.op("scalar", lambda e, t=t: e.copy(
                    kvb[t % 2][:, GD:2 * GD], pV[0][:]), inc=[("s", 1)])

            def a1_rope(t):
                kp = t % 2
                ke = apx(kvb[kp][:], 0, [[DQ, G], [1, 64]])
                ko = apx(kvb[kp][:], 64, [[DQ, G], [1, 64]])
                re_ = apx(krp[kp][:], 0, [[DQ, G], [1, 64]])
                ro_ = apx(krp[kp][:], 64, [[DQ, G], [1, 64]])
                cse = apx(csb[t % 5][:], 0, [[0, G], [1, 64]])
                cso = apx(csb[t % 5][:], 64, [[0, G], [1, 64]])
                t1 = apx(tmp12[:], 0, [[64, G], [1, 64]])
                t2 = apx(tmp12[:], GD // 2, [[64, G], [1, 64]])
                w_ = [("s", h_kvev[t]), (f"cs{t % 5}", cs_tot_d[t])]
                if t >= 2:
                    w_.append(("t", h_ktr[t - 2]))
                if t >= 1:
                    w_.append(("g", h_rope[t - 1]))
                P.op("gpsimd", lambda e, a=t1, b=ke, c=cso: e.tensor_mul(a, b, c),
                     wait=w_, inc=[("g", 1)])
                P.op("gpsimd", lambda e, a=t2, b=ko, c=cso: e.tensor_mul(a, b, c),
                     inc=[("g", 1)])
                P.op("gpsimd", lambda e, a=re_, b=ke, c=cse: e.tensor_mul(a, b, c),
                     inc=[("g", 1)])
                g0 = P.op("gpsimd", lambda e, a=ro_, b=ko, c=cse: e.tensor_mul(a, b, c),
                          inc=[("g", 1)])
                g1 = P.op("gpsimd", lambda e, a=re_, b=t2: e.tensor_sub(a, a, b),
                          wait=[("g", g0)], inc=[("g", 1)])
                h_rope[t] = P.op("gpsimd", lambda e, a=ro_, b=t1: e.tensor_add(a, a, b),
                                 wait=[("g", g1)], inc=[("g", 1)])

            def a1_vstore(t):
                P.op("sync", lambda e, t=t: e.dma_start(
                    out=dap(dkvo, VOFF + t * 128 * GD, [[GD, 128], [1, GD]]),
                    in_=kvb[t % 2][:, GD:2 * GD]),
                    wait=[("s", h_kvev[t])], inc=[("vsw", 16)])

            def a1_ktrans(t):
                for g_ in range(G):
                    w_ = [("g", h_rope[t])] if g_ == 0 else []
                    if g_ == 0 and t >= 2:
                        w_.append(("v", h_evKT[t - 2]))
                    P.op("tensor", lambda e, g_=g_, t=t: e.transpose(
                        pKT[t % 2][:, g_ * 128:(g_ + 1) * 128],
                        krp[t % 2][:, g_ * 128:(g_ + 1) * 128], sib[:]),
                        wait=w_, inc=[("t", 1)])
                h_ktr[t] = P.val("t")

            def a1_evKT(t):
                w_ = [("t", h_ktr[t])]
                if t >= 2:
                    w_.append(("ksw", 16 * t))
                h_evKT[t] = P.op("vector", lambda e, t=t: e.tensor_copy(
                    kts[t % 2][:], pKT[t % 2][:]), wait=w_, inc=[("v", 1)])

            def a1_kstore(t):
                P.op("sync", lambda e, t=t: e.dma_start(
                    out=dap(dkvo, t * 128, [[TB, 128], [DQ * TB, G], [1, 128]]),
                    in_=apx(kts[t % 2][:], 0, [[128, G], [1, 128]])),
                    wait=[("v", h_evKT[t])], inc=[("ksw", 16)])

            a1_load(0)
            a1_load(1)
            for i in range(NTB + 3):
                if i + 2 < NTB:
                    a1_load(i + 2)
                if i < NTB:
                    a1_ln(i)
                    if cfg.get("dbg"):
                        a1_dbg(i)
                if 0 <= i - 1 < NTB:
                    a1_trans(i - 1)
                    a1_evT(i - 1)
                if 0 <= i - 2 < NTB:
                    a1_kv(i - 2)
                    a1_copies(i - 2)
                    a1_rope(i - 2)
                    a1_vstore(i - 2)
                if 0 <= i - 3 < NTB:
                    a1_ktrans(i - 3)
                    a1_evKT(i - 3)
                    a1_kstore(i - 3)

            ksw_tot = P.val("ksw")
            vsw_tot = P.val("vsw")
            v_a1_done = P.val("v")
            s_a1_done = P.val("s")
            if AG:
                g_ag = P.op("gpsimd", lambda e: e.collective_compute(
                    "AllGather", ALU.bypass,
                    replica_groups=[list(range(NC))],
                    ins=[dkvo.opt()], outs=[dkvag.opt()]),
                    wait=[("ksw", ksw_tot), ("vsw", vsw_tot)], inc=[("agc", 1)])
                if cfg.get("dbg"):
                    P.op("sync", lambda e: e.dma_start(
                        out=dap(dbg_ag, 0, [[1024, SEC // 1024], [1, 1024]]),
                        in_=dap(dkvo, 0, [[1024, SEC // 1024], [1, 1024]])),
                        wait=[("ksw", ksw_tot), ("vsw", vsw_tot)], inc=[("st", 16)])
            else:
                g_ag = 0

        emit_phase_barrier(extra=[("ksw", ksw_tot), ("vsw", vsw_tot)])
        with contextlib.ExitStack() as pa2:
            xqb = [pa2.enter_context(nc.sbuf_tensor(f"xqb{i}", [128, D], F32)) for i in range(2)]
            xq1b = [pa2.enter_context(nc.sbuf_tensor(f"xq1b{i}", [128, D], BF16)) for i in range(2)]
            oxT = pa2.enter_context(nc.sbuf_tensor("oxT", [128, ND * LQ], BF16))
            sWqb = [sWqb0,
                    pa2.enter_context(nc.sbuf_tensor("sWqb1", [128, ND * 512], BF16))]
            scq = pa2.enter_context(nc.sbuf_tensor("scq", [128, NOT * DQ], F32))
            qpb = [pa2.enter_context(nc.sbuf_tensor(f"qpb{i}", [128, 512], BF16)) for i in range(2)]
            qrp = [pa2.enter_context(nc.sbuf_tensor(f"qrp{i}", [128, 512], BF16)) for i in range(2)]
            qt1 = pa2.enter_context(nc.sbuf_tensor("qt1", [128, 256], F32))
            qt2 = pa2.enter_context(nc.sbuf_tensor("qt2", [128, 256], F32))
            pT2 = pa2.enter_context(nc.psum_tensor("pT2", [128, ND * 128], BF16))
            pQ = [pa2.enter_context(nc.psum_tensor(f"pQ{i}", [128, 512], F32)) for i in range(2)]
            pQT = pa2.enter_context(nc.psum_tensor("pQT", [128, 1024], BF16))

            P.op("sync", lambda e: e.dma_start(
                out=apx(scq[:], 0, [[128, NOT], [1, DQ]]),
                in_=dap(dcsq, 0, [[DQ, 128], [128 * DQ, NOT], [1, DQ]])),
                 inc=[("ld1", 16)])
            ld1_tot = P.val("ld1")

            ha_ap, ha_evT = {}, {}

            def emit_prep(ot):
                sx = ot % 2
                qs_ = f"qx{sx}"
                w_ = [("v", ha_ap[ot - 2])] if ot >= 2 else []
                P.op("sync", lambda e, ot=ot, sx=sx: e.dma_start(
                    out=xqb[sx][:], in_=dxq[ot * 128:(ot + 1) * 128, :]),
                    wait=w_, inc=[(qs_, 16)])
                qx_tot = P.val(qs_)
                s_r, lk = emit_ln_stats(
                    lambda i, sx=sx: xqb[sx][:, i * 512:(i + 1) * 512],
                    first_waits=[(qs_, qx_tot)])
                ha_ap[ot] = P.op("vector", lambda e, sx=sx, lk=lk: e.tensor_scalar(
                    xq1b[sx][:], xqb[sx][:], smv[lk][:, 0:1], srs[lk][:],
                    ALU.subtract, ALU.mult), wait=[("s", s_r)], inc=[("v", 1)])
                ln_done(lk, [("v", ha_ap[ot])])
                for d in range(ND):
                    w_ = [("v", ha_ap[ot])]
                    if d == 0 and ot > 0:
                        w_.append(("v", ha_evT[ot - 1]))
                    if d == 0 and ot == 0:
                        w_ += [("v", v_a1_done), ("s", s_a1_done)]
                    P.op("tensor", lambda e, d=d, sx=sx: e.transpose(
                        pT2[:, d * 128:(d + 1) * 128],
                        xq1b[sx][:, d * 128:(d + 1) * 128], sib[:]),
                        wait=w_, inc=[("t", 1)])
                ha_evT[ot] = P.op("vector", lambda e, ot=ot: e.tensor_copy(
                    apx(oxT[:], ot * 128, [[LQ, ND], [1, 128]]), pT2[:]),
                    wait=[("t", P.val("t"))], inc=[("v", 1)])

            emit_prep(0)
            if NOT > 1:
                emit_prep(1)

            h_wl, h_qmm, h_qrope, h_qtr, h_qev = {}, {}, {}, {}, {}

            def mk_qtail(idx, nc4, ot, qx):
                def emit():
                    for j in range(4):
                        w_ = [("v", h_qrope[idx])]
                        if j == 0 and idx >= 2:
                            w_.append(("v", h_qev[idx - 2]))
                        P.op("tensor", lambda e, j=j, qx=qx: e.transpose(
                            pQT[:, qx * 512 + j * 128:qx * 512 + (j + 1) * 128],
                            qrp[qx][:, j * 128:(j + 1) * 128], sib[:]),
                            wait=w_, inc=[("t", 1)])
                    h_qtr[idx] = P.val("t")
                    h_qev[idx] = P.op("vector", lambda e, nc4=nc4, ot=ot, qx=qx:
                                      e.tensor_copy(
                                          apx(sQT[:], (nc4 * 4) * LQ + ot * 128,
                                              [[LQ, 4], [1, 128]]),
                                          pQT[:, qx * 512:(qx + 1) * 512]),
                                      wait=[("t", h_qtr[idx])], inc=[("v", 1)])
                return emit

            pending_qtail = None
            idx = 0
            for nc4 in range(NH4):
                wp = nc4 % 2
                ws = f"wl{wp}"
                if nc4 > 0:
                    w_ = [("t", h_qmm[(nc4 - 2) * NOT + NOT - 1])] if nc4 >= 2 else []
                    P.op("gpsimd", lambda e, nc4=nc4, wp=wp: e.dma_start(
                        out=sWqb[wp][:],
                        in_=dap(dwq, nc4 * 512, [[HD, 128], [128 * HD, ND], [1, 512]])),
                        wait=w_, inc=[(ws, 16)])
                wl_tot = P.val(ws)
                for ot in range(NOT):
                    qp = idx % 2
                    for d in range(ND):
                        w_ = []
                        if d == 0:
                            w_ = [("v", ha_evT[ot]), (ws, wl_tot)]
                            if idx == 0:
                                w_.append(("s", s_a1_done))
                            if idx >= 2:
                                w_.append(("v", h_qev[idx - 2]))
                        P.op("tensor", lambda e, d=d, ot=ot, wp=wp, qp=qp: e.matmul(
                            pQ[qp][:],
                            oxT[:, d * LQ + ot * 128:d * LQ + (ot + 1) * 128],
                            sWqb[wp][:, d * 512:(d + 1) * 512],
                            start=(d == 0), stop=(d == ND - 1)),
                            wait=w_, inc=[("t", 1)])
                    h_qmm[idx] = P.val("t")
                    if nc4 == 0 and ot + 2 < NOT:
                        emit_prep(ot + 2)
                    if pending_qtail is not None:
                        pending_qtail()
                    qx = idx % 2
                    w_ = [("t", h_qmm[idx])]
                    if idx >= 2:
                        w_.append(("v", h_qrope[idx - 2]))
                    v_ev = P.op("vector", lambda e, qp=qp, qx=qx: e.tensor_copy(
                        qpb[qx][:], pQ[qp][:]), wait=w_, inc=[("v", 1)])
                    qe = apx(qpb[qx][:], 0, [[DQ, 4], [1, 64]])
                    qo = apx(qpb[qx][:], 64, [[DQ, 4], [1, 64]])
                    rqe = apx(qrp[qx][:], 0, [[DQ, 4], [1, 64]])
                    rqo = apx(qrp[qx][:], 64, [[DQ, 4], [1, 64]])
                    qce = apx(scq[:], ot * DQ, [[0, 4], [1, 64]])
                    qco = apx(scq[:], ot * DQ + 64, [[0, 4], [1, 64]])
                    qtt1 = apx(qt1[:], 0, [[64, 4], [1, 64]])
                    qtt2 = apx(qt2[:], 0, [[64, 4], [1, 64]])
                    w_ = [("v", v_ev), ("ld1", ld1_tot)]
                    if idx >= 2:
                        w_.append(("t", h_qtr[idx - 2]))
                    if idx >= 1:
                        w_.append(("v", h_qrope[idx - 1]))
                    P.op("vector", lambda e, a=qtt1, b=qe, c=qco: e.tensor_mul(a, b, c),
                         wait=w_, inc=[("v", 1)])
                    P.op("vector", lambda e, a=qtt2, b=qo, c=qco: e.tensor_mul(a, b, c),
                         inc=[("v", 1)])
                    P.op("vector", lambda e, a=rqe, b=qe, c=qce: e.tensor_mul(a, b, c),
                         inc=[("v", 1)])
                    gg = P.op("vector", lambda e, a=rqo, b=qo, c=qce: e.tensor_mul(a, b, c),
                              inc=[("v", 1)])
                    gg = P.op("vector", lambda e, a=rqe, b=qtt2: e.tensor_sub(a, a, b),
                              wait=[("v", gg)], inc=[("v", 1)])
                    h_qrope[idx] = P.op("vector", lambda e, a=rqo, b=qtt1: e.tensor_add(a, a, b),
                                        wait=[("v", gg)], inc=[("v", 1)])
                    pending_qtail = mk_qtail(idx, nc4, ot, qx)
                    idx += 1
            pending_qtail()
            v_qt_done = P.val("v")
            s_a2_done = P.val("s")
            if cfg.get("dbg"):
                P.op("sync", lambda e: e.dma_start(out=dbg_qt, in_=sQT[:]),
                     wait=[("v", v_qt_done)], inc=[("st", 16)])

        ld1_cb = P.val("ld1")
        estA2.close()
        estL2 = contextlib.ExitStack()
        est_qbL = contextlib.ExitStack()
        with contextlib.ExitStack() as pb:
            sWoB = estL2.enter_context(nc.sbuf_tensor("sWoB", [128, H * D], BF16))
            sCT = estL2.enter_context(nc.sbuf_tensor("sCT", [128, H * LQ], BF16))
            kcb = [est_qbL.enter_context(nc.sbuf_tensor(f"kcb{i}", [128, 8 * 128], BF16)) for i in range(2)]
            vcb = [est_qbL.enter_context(nc.sbuf_tensor(f"vcb{i}", [128, 8 * 129], BF16)) for i in range(2)]
            pbf = [est_qbL.enter_context(nc.sbuf_tensor(f"pbf{i}", [128, 1024], BF16)) for i in range(3)]
            tb = [est_qbL.enter_context(nc.sbuf_tensor(f"tb{i}", [128, 1024], BF16)) for i in range(2)]
            srec3 = est_qbL.enter_context(nc.sbuf_tensor("srec3", [128, 9], F32))
            sctn3 = est_qbL.enter_context(nc.sbuf_tensor("sctn3", [128, 1152], BF16))
            psS = [pb.enter_context(nc.psum_tensor(f"psS{i}", [128, 1024], F32)) for i in range(2)]
            psC = pb.enter_context(nc.psum_tensor("psC", [128, 1536], F32))
            pTC = pb.enter_context(nc.psum_tensor("pTC", [128, 1024], BF16))

            for nc4 in range(NO4):
                P.op("gpsimd", lambda e, nc4=nc4: e.dma_start(
                    out=apx(sWoB[:], nc4 * 512, [[D, H], [1, 512]]),
                    in_=dap(dwo, nc4 * 512, [[D, 128], [128 * D, H], [1, 512]])),
                    wait=([("v", v_qt_done), ("s", s_a2_done)] if nc4 == 0 else []),
                    inc=[("gld", 16)])

            s_tb = P.op("scalar", lambda e: e.nop(),
                        wait=[("s", s_a2_done)], inc=[("s", 1)])
            P.op("vector", lambda e: e.memset(
                apx(vcb[0][:], 128, [[129, 8], [1, 1]]), 1.0), inc=[("v", 1)])
            v_ones = P.op("vector", lambda e: e.memset(
                apx(vcb[1][:], 128, [[129, 8], [1, 1]]), 1.0), inc=[("v", 1)])

            fin_cell = {"v": v_ones, "psc_free": v_ones}
            rounds = [(qc, hp) for qc in range(NQC) for hp in range(NHP)]
            rpc = 1024 // TB if TB < 1024 else 1
            tpr = min(8, TB // 128)

            def round_setup(qc, hp):
                nkt = 4 * NC * (qc + 1)
                acc_last = [min(nkt - 1, 32 * qc + 8 * (a % 4) + 7)
                            for a in range(8)]
                return {"qc": qc, "hp": hp, "g_": hp // PPG, "nkt": nkt,
                        "nch": nkt // 8, "acc_last": acc_last,
                        "bank_stop_acc": [max(range(3 * b, min(3 * b + 3, 8)),
                                              key=lambda a: acc_last[a])
                                          for b in range(3)],
                        "h_kl": {}, "h_vl": {}, "h_sc": {}, "h_ctx": {},
                        "h_tanh": {}, "h_exp": {}, "fa_d": {},
                        "pend_ctx": {}, "pending_exp": None, "pbf_war": {}}

            def plan_loads(RS, ch, wk, wv):
                cp = ch % 2
                g_ = RS["g_"]
                if AG:
                    w0 = [("agc", g_ag)]
                    koff = (ch * 1024 // TB) * SEC + ((ch * 1024) % TB)
                    kdims = [[TB, 128], [SEC, rpc], [1, TB]] if rpc > 1 \
                        else [[TB, 128], [1, 1024]]
                    voff = (ch * 1024 // TB) * SEC + VOFF + g_ * DQ
                    vdims = [[GD, 128], [SEC, rpc], [128 * GD, tpr], [1, 128]] if rpc > 1 \
                        else [[GD, 128], [128 * GD, 8], [1, 128]]
                else:
                    w0 = [("ksw", ksw_tot)]
                    koff = ch * 1024
                    kdims = [[TB, 128], [1, 1024]]
                    voff = (ch * 1024) * GD + g_ * DQ
                    vdims = [[GD, 128], [128 * GD, 8], [1, 128]]
                P.op("sync", lambda e, cp=cp, g_=g_, koff=koff, kdims=kdims:
                     e.dma_start(
                         out=kcb[cp][:],
                         in_=dap(dkvag, (g_ * DQ) * TB + koff, kdims)),
                     wait=w0 + wk, inc=[(f"kld{cp}", 16)])
                RS["h_kl"][ch] = P.val(f"kld{cp}")
                if AG and rpc > 1:
                    for r2 in range(rpc):
                        P.op("sync", lambda e, cp=cp, voff=voff, r2=r2:
                             e.dma_start(
                                 out=apx(vcb[cp][:], r2 * tpr * 129,
                                         [[129, tpr], [1, 128]]),
                                 in_=dap(dkvag, voff + r2 * SEC,
                                         [[GD, 128], [128 * GD, tpr], [1, 128]])),
                             wait=(w0 + [("v", v_ones)] + wv) if r2 == 0 else [],
                             inc=[(f"vld{cp}", 16)])
                else:
                    P.op("sync", lambda e, cp=cp, voff=voff, vdims=vdims:
                         e.dma_start(
                             out=apx(vcb[cp][:], 0, [[129, 8], [1, 128]]),
                             in_=dap(dkvag, voff, vdims)),
                         wait=w0 + [("v", v_ones)] + wv,
                         inc=[(f"vld{cp}", 16)])
                RS["h_vl"][ch] = P.val(f"vld{cp}")

            def make_ctx(RS, kt, h_exp):
                sp, cp, ch = kt % 3, (kt // 8) % 2, kt // 8
                fa = RS["fa_d"][kt]
                acc_last, bank_stop_acc = RS["acc_last"], RS["bank_stop_acc"]

                def emit():
                    first_mm = [kt == 0] * 3
                    nmm = 0
                    for j in range(2):
                        qoff = fa
                        while qoff < 512:
                            sw = min(128 - (qoff % 128), 512 - qoff)
                            a = j * 4 + (qoff // 128)
                            bank = a // 3
                            st_flag = bool(first_mm[bank])
                            first_mm[bank] = False
                            sp_flag = ((kt == acc_last[a])
                                       and (a == bank_stop_acc[bank]))
                            w_ = []
                            if nmm == 0:
                                w_ = [("s", h_exp), (f"vld{cp}", RS["h_vl"][ch])]
                                if kt == 0:
                                    w_.append(("v", fin_cell["psc_free"]))
                            nmm += 1
                            P.op("tensor", lambda e, j=j, qoff=qoff, sw=sw, a=a,
                                 sp=sp, cp=cp, kt=kt, st_flag=st_flag,
                                 sp_flag=sp_flag: e.matmul(
                                     psC[(qoff % 128):(qoff % 128) + sw,
                                         (a // 3) * 512 + (a % 3) * 129:
                                         (a // 3) * 512 + (a % 3) * 129 + 129],
                                     pbf[sp][:, j * 512 + qoff:j * 512 + qoff + sw],
                                     vcb[cp][:, (kt % 8) * 129:(kt % 8) * 129 + 129],
                                     start=st_flag, stop=sp_flag),
                                 wait=w_, inc=[("t", 1)])
                            qoff += sw
                    RS["h_ctx"][kt] = P.val("t")
                return emit

            def mk_exp(RS, kt):
                fa = RS["fa_d"][kt]
                w = 512 - fa

                def emit():
                    w_ = [("s", RS["h_tanh"][kt])]
                    if kt >= 3:
                        w_.append(("t", RS["h_ctx"][kt - 3]))
                    elif RS["pbf_war"].get(kt % 3):
                        w_.append(("t", RS["pbf_war"][kt % 3]))
                    return P.op("scalar", lambda e: e.activation(
                        apx(pbf[kt % 3][:], fa, [[512, 2], [1, w]]),
                        apx(tb[kt % 2][:], fa, [[512, 2], [1, w]]),
                        ACT.Exp, scale=30.0),
                        wait=w_, inc=[("s", 1)])
                return emit

            def emit_scores(RS, kt, w_extra0):
                qc, hp = RS["qc"], RS["hp"]
                sp, cp, ch = kt % 2, (kt // 8) % 2, kt // 8
                gk = kt - 32 * qc
                fa = 0 if gk < 0 else min(384, (gk // 8) * 128)
                RS["fa_d"][kt] = fa
                w = 512 - fa
                h_scj = {}
                for j in range(2):
                    h = 2 * hp + j
                    w_ = []
                    if j == 0:
                        w_ = [(f"kld{cp}", RS["h_kl"][ch])] + w_extra0
                    P.op("tensor", lambda e, j=j, h=h, kt=kt, sp=sp, cp=cp,
                         fa=fa, w=w, qc=qc: e.matmul(
                             psS[sp][:, j * 512 + fa:j * 512 + fa + w],
                             kcb[cp][:, (kt % 8) * 128:(kt % 8 + 1) * 128],
                             sQT[:, h * LQ + qc * 512 + fa:h * LQ + qc * 512 + fa + w],
                             start=True, stop=True),
                         wait=w_, inc=[("t", 1)])
                    h_scj[j] = P.val("t")
                RS["h_sc"][kt] = P.val("t")
                return h_scj

            def emit_stt(RS, kt, h_scj, extra0):
                qc, hp = RS["qc"], RS["hp"]
                sp = kt % 2
                fa = RS["fa_d"][kt]
                w = 512 - fa
                x0 = 496 - 16 * (kt - 32 * qc)
                for j in range(2):
                    h = 2 * hp + j
                    w_ = [("t", h_scj[j])] + (extra0 if j == 0 else [])
                    P.op("vector", lambda e, j=j, h=h, sp=sp, fa=fa, w=w, x0=x0:
                         e.scalar_tensor_tensor(
                             psS[sp][:, j * 512 + fa:j * 512 + fa + w],
                             sCB[:, x0 + fa:x0 + fa + w], slopes[h],
                             psS[sp][:, j * 512 + fa:j * 512 + fa + w],
                             ALU.mult, ALU.add),
                         wait=w_, inc=[("v", 1)])
                return P.val("v")

            def emit_tanh(RS, kt, h_stt, extra):
                sp = kt % 2
                fa = RS["fa_d"][kt]
                w = 512 - fa
                RS["h_tanh"][kt] = P.op("scalar", lambda e, sp=sp, fa=fa, w=w:
                                        e.activation(
                                            apx(tb[sp][:], fa, [[512, 2], [1, w]]),
                                            apx(psS[sp][:], fa, [[512, 2], [1, w]]),
                                            ACT.Tanh, scale=c_tanh),
                                        wait=[("v", h_stt)] + extra, inc=[("s", 1)])

            def make_fin_pieces(h_last, hp, qc):
                st = {"t_prev": 0}

                def rcn_piece(b):
                    def emit():
                        nacc = 3 if b < 2 else 2
                        base = b * 512
                        v_rc = P.op("vector", lambda e, base=base, nacc=nacc:
                                    e.reciprocal(
                                        srec3[:, 3 * b:3 * b + nacc],
                                        apx(psC[:], base + 128,
                                            [[129, nacc], [1, 1]])),
                                    wait=[("t", h_last), ("v", P.val("v"))],
                                    inc=[("v", 1)])
                        fin_cell["psc_free"] = P.op(
                            "vector", lambda e, base=base, nacc=nacc, b=b:
                            e.tensor_tensor(
                                apx(sctn3[:], b * 384, [[128, nacc], [1, 128]]),
                                apx(psC[:], base, [[129, nacc], [1, 128]]),
                                apx(srec3[:], 3 * b, [[1, nacc], [0, 128]]),
                                ALU.mult),
                            wait=[("v", v_rc)], inc=[("v", 1)])
                    return emit

                def tr_piece():
                    psc_free = fin_cell["psc_free"]
                    for a in range(8):
                        w_ = []
                        if a == 0:
                            w_ = [("v", psc_free), ("v", fin_cell["v"])]
                        P.op("tensor", lambda e, a=a: e.transpose(
                            pTC[:, a * 128:(a + 1) * 128],
                            sctn3[:, (a // 3) * 384 + (a % 3) * 128:
                                  (a // 3) * 384 + (a % 3) * 128 + 128],
                            sib[:]),
                            wait=w_, inc=[("t", 1)])
                    st["t_prev"] = P.val("t")

                def copy_piece():
                    fin_cell["v"] = P.op(
                        "vector", lambda e, hp=hp, qc=qc: e.tensor_copy(
                            apx(sCT[:], (2 * hp) * LQ + qc * 512,
                                [[LQ, 2], [128, 4], [1, 128]]),
                            pTC[:, :1024]),
                        wait=[("t", st["t_prev"])], inc=[("v", 1)])
                return [rcn_piece(b) for b in range(3)] + [tr_piece, copy_piece]

            carry = None
            for ri, (qc, hp) in enumerate(rounds):
                if carry is None:
                    RS = round_setup(qc, hp)
                    t_round0 = P.val("t")
                    s_round0 = P.val("s")
                    for ch in range(min(2, RS["nch"])):
                        plan_loads(RS, ch, [("t", t_round0)], [("t", t_round0)])
                    start_kt = 0
                    fin_prev = None
                else:
                    RS, fin_prev = carry
                nkt, nch = RS["nkt"], RS["nch"]
                for kt in range(0 if carry is None else 2, nkt):
                    if carry is None and kt == 0:
                        sc0 = [("v", v_qt_done), ("s", s_round0)]
                    elif kt >= 2:
                        sc0 = [("s", RS["h_tanh"][kt - 2])]
                    else:
                        sc0 = []
                    h_scj = emit_scores(RS, kt, sc0)
                    if kt - 3 in RS["pend_ctx"]:
                        RS["pend_ctx"].pop(kt - 3)()
                    if kt % 8 == 2 and kt > 2 and (kt - 2) // 8 + 1 < nch:
                        plan_loads(RS, (kt - 2) // 8 + 1,
                                   [("t", RS["h_sc"][kt - 1])],
                                   [("t", RS["h_ctx"][kt - 3])])
                    h_stt = emit_stt(RS, kt, h_scj,
                                     [("ld1", ld1_cb)] if (ri == 0 and kt == 0)
                                     else [])
                    extra = []
                    if ri == 0 and kt == 0:
                        extra.append(("s", s_tb))
                    if kt >= 2:
                        extra.append(("s", RS["h_exp"][kt - 2]))
                    emit_tanh(RS, kt, h_stt, extra)
                    if RS["pending_exp"] is not None:
                        RS["h_exp"][kt - 1] = RS["pending_exp"]()
                        RS["pend_ctx"][kt - 1] = make_ctx(RS, kt - 1,
                                                          RS["h_exp"][kt - 1])
                    RS["pending_exp"] = mk_exp(RS, kt)
                    if fin_prev is not None and 2 <= kt <= 3:
                        for p_ in fin_prev[2 * kt - 3:2 * kt - 1]:
                            p_()
                RS["h_exp"][nkt - 1] = RS["pending_exp"]()

                if ri + 1 < len(rounds):
                    RSn = round_setup(*rounds[ri + 1])
                    ch_e = nch - 1 if (nch - 1) % 2 == 0 else nch - 2
                    kt_e = 8 * ch_e + 7
                    plan_loads(RSn, 0, [("t", RS["h_sc"][kt_e])],
                               [("t", RS["h_ctx"][kt_e])])
                    for kt in (0, 1):
                        h_scj = emit_scores(
                            RSn, kt, [("s", RS["h_tanh"][nkt - 2 + kt])])
                        h_stt = emit_stt(RSn, kt, h_scj, [])
                        emit_tanh(RSn, kt, h_stt,
                                  [("s", RS["h_exp"][nkt - 2 + kt])])
                    RS["pend_ctx"][nkt - 1] = make_ctx(RS, nkt - 1,
                                                       RS["h_exp"][nkt - 1])
                    for k_ in sorted(RS["pend_ctx"]):
                        RS["pend_ctx"].pop(k_)()
                    ch_o = nch - 1 if (nch - 1) % 2 == 1 else nch - 2
                    kt_o = 8 * ch_o + 7
                    plan_loads(RSn, 1, [("t", RS["h_sc"][kt_o])],
                               [("t", RS["h_ctx"][kt_o])])
                    RSn["pbf_war"] = {
                        m: RS["h_ctx"][nkt - 1 - ((nkt - 1 - m) % 3)]
                        for m in range(3)}
                    RSn["h_exp"][0] = mk_exp(RSn, 0)()
                    RSn["pend_ctx"][0] = make_ctx(RSn, 0, RSn["h_exp"][0])
                    pieces = make_fin_pieces(RS["h_ctx"][nkt - 1], hp, qc)
                    pieces[0]()
                    RSn["pending_exp"] = mk_exp(RSn, 1)
                    carry = (RSn, pieces)
                else:
                    RS["pend_ctx"][nkt - 1] = make_ctx(RS, nkt - 1,
                                                       RS["h_exp"][nkt - 1])
                    for k_ in sorted(RS["pend_ctx"]):
                        RS["pend_ctx"].pop(k_)()
                    for p_ in make_fin_pieces(RS["h_ctx"][nkt - 1], hp, qc):
                        p_()
                    carry = None
            v_b_done = P.val("v")
            t_b_done = P.val("t")
            s_b_done = P.val("s")
            if cfg.get("dbg"):
                P.op("sync", lambda e: e.dma_start(out=dbg_ct, in_=sCT[:]),
                     wait=[("v", v_b_done)], inc=[("st", 16)])

        est_qbL.close()

        emit_phase_barrier()
        with contextlib.ExitStack() as pc1:
            pc1a = pc1.enter_context(contextlib.ExitStack())
            xqb2 = [pc1a.enter_context(nc.sbuf_tensor(f"xqb2{i}", [128, D], F32)) for i in range(1)]
            sbo = pc1a.enter_context(nc.sbuf_tensor("sbo", [128, D], BF16))
            t3t = [pc1a.enter_context(nc.sbuf_tensor(f"t3t{i}", [128, D], F32)) for i in range(2)]
            psOf = [pc1a.enter_context(nc.psum_tensor(f"psOf{i}", [128, D], F32)) for i in range(2)]

            P.op("gpsimd", lambda e: e.dma_start(
                out=sbo[:], in_=dbo.partition_broadcast(128)), inc=[("gld", 16)])
            gld_c = P.val("gld")

            h_omm, h_t3, h_x2 = {}, {}, {}
            for ot in range(NOT):
                op_ = ot % 2
                sx = 0
                xs = "xo0"
                w_ = [("g", h_x2[ot - 1])] if ot >= 1 else []
                P.op("sync", lambda e, ot=ot, sx=sx: e.dma_start(
                    out=xqb2[sx][:], in_=dxq[ot * 128:(ot + 1) * 128, :]),
                    wait=w_, inc=[(xs, 16)])
                xo_tot = P.val(xs)
                for nc4 in range(NO4):
                    for hh in range(H):
                        w_ = []
                        if nc4 == 0 and hh == 0:
                            w_ = [("v", v_b_done), ("gld", gld_c)]
                            if ot >= 2:
                                w_.append(("v", h_t3[ot - 2]))
                        P.op("tensor", lambda e, hh=hh, ot=ot, nc4=nc4, op_=op_: e.matmul(
                            psOf[op_][:, nc4 * 512:(nc4 + 1) * 512],
                            sCT[:, hh * LQ + ot * 128:hh * LQ + (ot + 1) * 128],
                            sWoB[:, hh * D + nc4 * 512:hh * D + (nc4 + 1) * 512],
                            start=(hh == 0), stop=(hh == H - 1)),
                            wait=w_, inc=[("t", 1)])
                h_omm[ot] = P.val("t")
                v_bo = P.op("vector", lambda e, op_=op_: e.scalar_tensor_tensor(
                    psOf[op_][:], sbo[:], 1.0, psOf[op_][:], ALU.mult, ALU.add),
                    wait=[("t", h_omm[ot]), ("gld", gld_c)], inc=[("v", 1)])
                s_r, lk = emit_ln_stats(
                    lambda i, op_=op_: psOf[op_][:, i * 512:(i + 1) * 512],
                    first_waits=[("v", v_bo)])
                h_t3[ot] = P.op("vector", lambda e, op_=op_, lk=lk: e.tensor_scalar(
                    t3t[op_][:], psOf[op_][:], smv[lk][:, 0:1], srs[lk][:],
                    ALU.subtract, ALU.mult),
                    wait=[("s", s_r)] + ([("g", h_x2[ot - 2])] if ot >= 2 else []),
                    inc=[("v", 1)])
                ln_done(lk, [("v", h_t3[ot])])
                h_x2[ot] = P.op("gpsimd", lambda e, ot=ot, sx=sx, op_=op_: e.tensor_add(
                    x2[:, ot * D:(ot + 1) * D], t3t[op_][:], xqb2[sx][:]),
                    wait=[("v", h_t3[ot]), (xs, xo_tot)] +
                         ([("g", h_x2[ot - 1])] if ot >= 1 else []), inc=[("g", 1)])
            g_x2_done = P.val("g")
            if cfg.get("dbg"):
                P.op("sync", lambda e: e.dma_start(out=dbg_x2, in_=x2[:]),
                     wait=[("g", g_x2_done)], inc=[("st", 16)])

            pc1a.close()
            estL2.close()
            est_qb.close()

            emit_phase_barrier()
            with contextlib.ExitStack() as pc2:
                sAT = pc2.enter_context(nc.sbuf_tensor("sAT", [128, ND * 512], BF16))
                sBT = pc2.enter_context(nc.sbuf_tensor("sBT", [128, NP * D], BF16))
                swrt = pc2.enter_context(nc.sbuf_tensor("swrt", [128, ND * E], BF16))
                for dst, srcap in (
                        (sAT, dap(dat, 0, [[E * R, 128], [128 * E * R, ND], [1, 512]])),
                        (swrt, dap(dwrt, 0, [[E, 128], [128 * E, ND], [1, E]])),
                        (sBT, dap(dbt, 0, [[D, 128], [128 * D, NP], [1, D]]))):
                    P.op("gpsimd", lambda e, d=dst, s=srcap: e.dma_start(out=d[:], in_=s),
                         inc=[("gld", 16)])
                gld_c2 = P.val("gld")
                sbr = pc2.enter_context(nc.sbuf_tensor("sbr", [128, E], F32))
                sOH = pc2.enter_context(nc.sbuf_tensor("sOH", [8, 512], F32))
                x2ln = [pc2.enter_context(nc.sbuf_tensor(f"x2ln{i}", [128, D], BF16)) for i in range(2)]
                x2T = pc2.enter_context(nc.sbuf_tensor("x2T", [128, ND * 512], BF16))
                slg = [pc2.enter_context(nc.sbuf_tensor(f"slg{i}", [128, E], F32)) for i in range(2)]
                smx = [pc2.enter_context(nc.sbuf_tensor(f"smx{i}", [128, 1], F32)) for i in range(2)]
                snm = [pc2.enter_context(nc.sbuf_tensor(f"snm{i}", [128, 1], F32)) for i in range(2)]
                sex = [pc2.enter_context(nc.sbuf_tensor(f"sex{i}", [128, E], F32)) for i in range(2)]
                ssm = [pc2.enter_context(nc.sbuf_tensor(f"ssm{i}", [128, 1], F32)) for i in range(2)]
                src2 = [pc2.enter_context(nc.sbuf_tensor(f"src2{i}", [128, 1], F32)) for i in range(2)]
                smk = [pc2.enter_context(nc.sbuf_tensor(f"smk{i}", [128, E], F32)) for i in range(2)]
                sw_ = [pc2.enter_context(nc.sbuf_tensor(f"sw_{i}", [128, E], F32)) for i in range(4)]
                sif2 = pc2.enter_context(nc.sbuf_tensor("sif2", [128, 128], F32))
                swT = pc2.enter_context(nc.sbuf_tensor("swT", [8, 512], F32))
                wbb = pc2.enter_context(nc.sbuf_tensor("wbb", [128, 4 * 512], BF16))
                y2b = [pc2.enter_context(nc.sbuf_tensor(f"y2b{i}", [128, D], F32)) for i in range(2)]
                hpb = pc2.enter_context(nc.sbuf_tensor("hpb", [128, NP * 512], BF16))
                t4t = pc2.enter_context(nc.sbuf_tensor("t4t", [128, D], F32))
                outb = [pc2.enter_context(nc.sbuf_tensor(f"outb{i}", [128, D], F32)) for i in range(2)]

                P.op("sync", lambda e: e.dma_start(
                    out=sbr[:], in_=dbr.partition_broadcast(128)), inc=[("ld1", 16)])
                P.op("sync", lambda e: e.dma_start(out=sOH[:, :], in_=doh),
                     inc=[("ld1", 16)])
                ld1_tot = P.val("ld1")
                gi1 = P.op("gpsimd", lambda e: e.iota(
                    sif2[:], pattern=[[-1, 128]], base=0, channel_multiplier=1,
                    allow_small_or_imprecise_dtypes=True),
                    wait=[("g", g_x2_done)], inc=[("g", 1)])
                v_if = P.op("vector", lambda e: e.tensor_scalar(
                    sif2[:], sif2[:], 0.0, None, ALU.is_equal),
                    wait=[("g", gi1)], inc=[("v", 1)])

                h_oev2, st_prev = {}, {}
                h_t4, h_yev_d = {}, {}
                v_hp_qg_prev = 0
                yev_qg_prev = 0
                t_tr_prev, v_e3_prev = {}, {}
                for qg in range(max(1, LQ // 512)):
                    with contextlib.ExitStack() as pcq:
                        pca2 = pcq.enter_context(contextlib.ExitStack())
                        psR = [pca2.enter_context(nc.psum_tensor(
                            "psRq%d_%d" % (qg, i), [128, E], F32)) for i in range(2)]
                        pWT = pca2.enter_context(nc.psum_tensor("pWTq%d" % qg, [8, 128], F32))
                        pwb = pca2.enter_context(nc.psum_tensor("pwbq%d" % qg, [128, 512], F32))
                        pcaT = pcq.enter_context(contextlib.ExitStack())
                        pT3 = [pcaT.enter_context(nc.psum_tensor(
                            "pT3q%d_%d" % (qg, i), [128, ND * 128], BF16)) for i in range(2)]

                        v_e3_d, g_ap2_d = {}, {}
                        t_tr_d = {}
                        for tt in range(4):
                            ot = qg * 4 + tt
                            s_r, lk = emit_ln_stats(
                                lambda i, ot=ot: x2[:, ot * D + i * 512:ot * D + (i + 1) * 512],
                                first_waits=[("g", g_x2_done)])
                            w_ = [("s", s_r)]
                            if tt >= 2:
                                w_.append(("t", t_tr_d[tt - 2]))
                            elif t_tr_prev.get(tt + 2):
                                w_.append(("t", t_tr_prev[tt + 2]))
                            g_ap2 = P.op("vector", lambda e, ot=ot, lk=lk, tt=tt: e.tensor_scalar(
                                x2ln[tt % 2][:], x2[:, ot * D:(ot + 1) * D], smv[lk][:, 0:1],
                                srs[lk][:], ALU.subtract, ALU.mult),
                                wait=w_, inc=[("v", 1)])
                            ln_done(lk, [("v", g_ap2)])
                            g_ap2_d[tt] = g_ap2
                            for d in range(ND):
                                w_ = [("v", g_ap2)] if d == 0 else []
                                if d == 0 and tt >= 2:
                                    w_.append(("v", v_e3_d[tt - 2]))
                                if d == 0 and tt < 2:
                                    if v_e3_prev.get(tt + 2):
                                        w_.append(("v", v_e3_prev[tt + 2]))
                                    if v_hp_qg_prev:
                                        w_.append(("v", v_hp_qg_prev))
                                    if tt == 0 and yev_qg_prev:
                                        w_.append(("v", yev_qg_prev))
                                P.op("tensor", lambda e, d=d, tt=tt: e.transpose(
                                    pT3[tt % 2][:, d * 128:(d + 1) * 128],
                                    x2ln[tt % 2][:, d * 128:(d + 1) * 128], sib[:]),
                                    wait=w_, inc=[("t", 1)])
                            t_tr_d[tt] = P.val("t")
                            v_e3_d[tt] = P.op("vector", lambda e, tt=tt: e.tensor_copy(
                                apx(x2T[:], tt * 128, [[512, ND], [1, 128]]), pT3[tt % 2][:]),
                                wait=[("t", t_tr_d[tt])], inc=[("v", 1)])
                        t_tr_prev = t_tr_d
                        v_e3_prev = v_e3_d
                        pcaT.close()
                        pcaH = pcq.enter_context(contextlib.ExitStack())
                        psH = [pcaH.enter_context(nc.psum_tensor(
                            "psH%d_%d" % (qg, i), [128, 512], F32)) for i in range(4)]

                        v_lg_d, v_w1_d, t_h_d = {}, {}, {}
                        for tt in range(4):
                            for d in range(ND):
                                w_ = []
                                if d == 0:
                                    w_ = [("v", v_e3_d[tt])]
                                    if tt == 0:
                                        w_.append(("gld", gld_c2))
                                        if yev_qg_prev:
                                            w_.append(("v", yev_qg_prev))
                                    if tt >= 2:
                                        w_.append(("v", v_lg_d[tt - 2]))
                                P.op("tensor", lambda e, d=d, tt=tt: e.matmul(
                                    psR[tt % 2][:],
                                    x2T[:, d * 512 + tt * 128:d * 512 + (tt + 1) * 128],
                                    swrt[:, d * E:(d + 1) * E],
                                    start=(d == 0), stop=(d == ND - 1)),
                                    wait=w_, inc=[("t", 1)])
                            t_r = P.val("t")
                            v_lg = P.op("vector", lambda e, tt=tt: e.tensor_add(
                                slg[tt % 2][:], psR[tt % 2][:], sbr[:, :E]),
                                wait=[("t", t_r), ("ld1", ld1_tot)], inc=[("v", 1)])
                            v_lg_d[tt] = v_lg
                            v_mx = P.op("vector", lambda e, tt=tt: e.reduce_max(
                                smx[tt % 2][:], slg[tt % 2][:], AX), wait=[("v", v_lg)], inc=[("v", 1)])
                            v_nm = P.op("vector", lambda e, tt=tt: e.tensor_scalar_mul(
                                snm[tt % 2][:], smx[tt % 2][:], -1.0), wait=[("v", v_mx)], inc=[("v", 1)])
                            s_ex = P.op("scalar", lambda e, tt=tt: e.activation(
                                sex[tt % 2][:], slg[tt % 2][:], ACT.Exp, bias=snm[tt % 2][:], accum_out=ssm[tt % 2][:]),
                                wait=[("v", v_nm), ("s", P.val("s"))], inc=[("s", 1)])
                            v_rc = P.op("vector", lambda e, tt=tt: e.reciprocal(
                                src2[tt % 2][:], ssm[tt % 2][:]), wait=[("s", s_ex)], inc=[("v", 1)])
                            v_mk = P.op("vector", lambda e, tt=tt: e.tensor_scalar(
                                smk[tt % 2][:], slg[tt % 2][:], smx[tt % 2][:], None, ALU.is_equal),
                                wait=[("v", v_rc)], inc=[("v", 1)])
                            v_w0 = P.op("vector", lambda e, tt=tt: e.tensor_mul(
                                sw_[tt][:], smk[tt % 2][:], sex[tt % 2][:]), wait=[("v", v_mk)], inc=[("v", 1)])
                            v_w1_d[tt] = P.op("vector", lambda e, tt=tt: e.tensor_scalar(
                                sw_[tt][:], sw_[tt][:], src2[tt % 2][:], inv_rank, ALU.mult, ALU.mult),
                                wait=[("v", v_w0)], inc=[("v", 1)])
                            if tt == 1 or tt == 3:
                                for pair in (tt - 1, tt):
                                    for d in range(ND):
                                        w_ = []
                                        if d == 0 and pair == 0:
                                            w_ = [("v", v_e3_d[3])]
                                        P.op("tensor", lambda e, d=d, pair=pair: e.matmul(
                                            psH[pair][:],
                                            sAT[:, d * 512 + pair * 128:d * 512 + (pair + 1) * 128],
                                            x2T[:, d * 512:(d + 1) * 512],
                                            start=(d == 0), stop=(d == ND - 1)),
                                            wait=w_, inc=[("t", 1)])
                                    t_h_d[pair] = P.val("t")

                        v_wT_prev = 0
                        for tt in range(4):
                            w_ = [("v", v_w1_d[tt]), ("v", v_if)]
                            if v_wT_prev:
                                w_.append(("v", v_wT_prev))
                            t_wt = P.op("tensor", lambda e, tt=tt: e.transpose(
                                pWT[:], sw_[tt][:, :E], sif2[:]),
                                wait=w_, inc=[("t", 1)])
                            v_wT_prev = P.op("vector", lambda e, tt=tt: e.tensor_copy(
                                swT[:, tt * 128:(tt + 1) * 128], pWT[:]),
                                wait=[("t", t_wt)], inc=[("v", 1)])
                        v_swt = P.val("v")
                        v_wb_prev = 0
                        v_wb_d = {}
                        for pair in range(NP):
                            w_ = [("v", v_swt)]
                            if v_wb_prev:
                                w_.append(("v", v_wb_prev))
                            t_wb = P.op("tensor", lambda e, pair=pair: e.matmul(
                                pwb[:], sOH[:, pair * 128:(pair + 1) * 128],
                                swT[:, :512], start=True, stop=True),
                                wait=w_, inc=[("t", 1)])
                            v_wb_prev = P.op("vector", lambda e, pair=pair: e.tensor_copy(
                                wbb[:, pair * 512:(pair + 1) * 512], pwb[:]),
                                wait=[("t", t_wb)], inc=[("v", 1)])
                            v_wb_d[pair] = v_wb_prev
                        v_wbb = P.val("v")

                        for pair in range(NP):
                            P.op("vector", lambda e, pair=pair: e.tensor_mul(
                                hpb[:, pair * 512:(pair + 1) * 512], psH[pair][:],
                                wbb[:, pair * 512:(pair + 1) * 512]),
                                wait=[("t", t_h_d[pair]), ("v", v_wb_d[pair])], inc=[("v", 1)])
                        v_hp = P.val("v")
                        v_hp_qg_prev = v_hp
                        pcaH.close()
                        pca2.close()
                        psY2 = [pcq.enter_context(nc.psum_tensor(
                            "psY%d_%d" % (qg, i), [128, D], F32)) for i in range(2)]

                        for ts in range(4):
                            gi = qg * 4 + ts
                            ot = qg * 4 + ts
                            ob = gi % 2
                            for oc in range(NO4):
                                for pair in range(NP):
                                    w_ = []
                                    if oc == 0 and pair == 0:
                                        w_ = [("v", v_hp)]
                                        if ts >= 2:
                                            w_.append(("v", h_yev_d[gi - 2]))
                                    P.op("tensor", lambda e, oc=oc, pair=pair, ts=ts: e.matmul(
                                        psY2[ts % 2][:, oc * 512:(oc + 1) * 512],
                                        hpb[:, pair * 512 + ts * 128:pair * 512 + ts * 128 + 128],
                                        sBT[:, pair * D + oc * 512:pair * D + (oc + 1) * 512],
                                        start=(pair == 0), stop=(pair == NP - 1)),
                                        wait=w_, inc=[("t", 1)])
                            t_y = P.val("t")
                            yb = y2b[gi % 2]
                            h_yev = P.op("vector", lambda e, yb=yb, ts=ts: e.tensor_copy(
                                yb[:], psY2[ts % 2][:]), wait=[("t", t_y)], inc=[("v", 1)])
                            h_yev_d[gi] = h_yev
                            s_r, lk = emit_ln_stats(
                                lambda i, yb=yb: yb[:, i * 512:(i + 1) * 512],
                                first_waits=[("v", h_yev)])
                            h_t4[gi] = P.op("vector", lambda e, lk=lk, yb=yb: e.tensor_scalar(
                                t4t[:], yb[:], smv[lk][:, 0:1], srs[lk][:],
                                ALU.subtract, ALU.mult), wait=[("s", s_r)], inc=[("v", 1)])
                            ln_done(lk, [("v", h_t4[gi])])
                            w_ = [("v", h_t4[gi]), ("g", g_x2_done)]
                            if gi >= 2:
                                w_.append(("st", P.val("st")))
                            v_ob = P.op("vector", lambda e, ot=ot, ob=ob: e.tensor_add(
                                outb[ob][:], t4t[:], x2[:, ot * D:(ot + 1) * D]),
                                wait=w_, inc=[("v", 1)])
                            st_prev[gi] = P.op("sync", lambda e, ot=ot, ob=ob: e.dma_start(
                                out=dout[ot * 128:(ot + 1) * 128, :], in_=outb[ob][:]),
                                wait=[("v", v_ob)], inc=[("st", 16)])
                        yev_qg_prev = h_yev_d[qg * 4 + 3]

                st_tot = P.val("st")
                P.op("sync", lambda e: None, wait=[("st", st_tot)])

        P.emit(block, sems)
    return nc


def _prep_inputs(cfg, inputs):
    T, D, H, G, E, R = cfg["T"], cfg["D"], cfg["H"], cfg["G"], cfg["E"], cfg["R"]
    DQ = 128
    NP = E // 2
    x = np.ascontiguousarray(np.asarray(inputs["L0_attn_hidden_states"],
                                        dtype=np.float32).reshape(T, D))
    perm = rope_perm(DQ)
    wq = np.asarray(inputs["L0_attn_Wq"], dtype=np.float32)
    wq = np.ascontiguousarray(wq.reshape(D, H, DQ)[:, :, perm].reshape(D, H * DQ))
    wk = np.asarray(inputs["L0_attn_Wk"], dtype=np.float32)
    wk = np.ascontiguousarray(wk.reshape(D, G, DQ)[:, :, perm].reshape(D, G * DQ))
    wv = np.ascontiguousarray(np.asarray(inputs["L0_attn_Wv"], dtype=np.float32))
    wkv = np.ascontiguousarray(np.concatenate([wk, wv], axis=1))
    wo = np.ascontiguousarray(np.asarray(inputs["L0_attn_Wo"], dtype=np.float32))
    bo = np.ascontiguousarray(np.asarray(inputs["L0_attn_bo"],
                                         dtype=np.float32).reshape(1, D))
    A = np.asarray(inputs["L0_ffn_A"], dtype=np.float32)
    at = np.ascontiguousarray(A.transpose(2, 0, 1).reshape(D, E * R))
    B = np.asarray(inputs["L0_ffn_B"], dtype=np.float32)
    bt = np.ascontiguousarray(B.transpose(0, 2, 1).reshape(NP, 2 * R, D))
    wrt = np.ascontiguousarray(np.asarray(inputs["L0_ffn_Wr"], dtype=np.float32).T)
    br = np.ascontiguousarray(np.asarray(inputs["L0_ffn_br"],
                                         dtype=np.float32).reshape(1, E))
    cs = make_cs(T, DQ)
    identb = np.eye(128, dtype=np.float32).astype(ml_dtypes.bfloat16)
    oh = make_oh(E)

    TB = T // NC if cfg.get("ag") else T
    blocks = np.arange(T // 16)
    in_maps, own_rows_all = [], []
    for c in range(NC):
        own_blocks = blocks[blocks % NC == c]
        rows = (own_blocks[:, None] * 16 + np.arange(16)[None, :]).reshape(-1)
        own_rows_all.append(rows)
        b0 = c * TB if cfg.get("ag") else 0
        in_maps.append(dict(
            xb=np.ascontiguousarray(x[b0:b0 + TB]),
            xq=np.ascontiguousarray(x[rows]), wq=wq, wkv=wkv, wo=wo,
            bo1=bo, at=at, bt=bt, wrt=wrt, br1=br, cb=make_cb(c),
            csk=np.ascontiguousarray(cs[b0:b0 + TB]),
            csq=np.ascontiguousarray(cs[rows]), oh=oh, identb=identb))
    return in_maps, own_rows_all


def kernel(**inputs):
    cfg = dict(FULL)
    nc_prog = build_nc(cfg)
    in_maps, own_rows = _prep_inputs(cfg, inputs)
    res = run_bass_kernel_spmd(nc_prog, in_maps, core_ids=list(range(NC)))
    T, D = cfg["T"], cfg["D"]
    out = np.empty((T, D), dtype=np.float32)
    for c in range(NC):
        out[own_rows[c]] = np.asarray(res.results[c]["out"], dtype=np.float32)
    return out.reshape(1, T, D)
